# version 03
# baseline (speedup 1.0000x reference)
import numpy as np
import ml_dtypes
import concourse.bass as bass
import concourse.mybir as mybir
from concourse.bass_utils import run_bass_kernel_spmd
from contextlib import ExitStack

F32 = mybir.dt.float32
BF16 = mybir.dt.bfloat16
AF = mybir.ActivationFunctionType
ALU = mybir.AluOpType
AX = mybir.AxisListType

T = 2048
D = 1024
DI = 2048
DS = 16
DTR = 64
E = 8
NB = 512
LSC = 256
POOL_BYTES = 206 * 1024

ENGINES = ('tensor', 'vector', 'scalar', 'gpsimd', 'sync')
DMA_QUEUES = ('sync', 'gpsimd', 'scalar')


class Rec:
    __slots__ = ('eng', 'fn', 'deps', 'pre_drain', 'inc', 'dma', 'idx', 'val')

    def __init__(self, eng, fn, dma, idx):
        self.eng = eng
        self.fn = fn
        self.deps = []
        self.pre_drain = False
        self.inc = dma
        self.dma = dma
        self.idx = idx
        self.val = None


class P:

    def __init__(self, nc):
        self.nc = nc
        self.stack = ExitStack()
        self.recs = {e: [] for e in ENGINES}
        self.sems = {}
        for name in list(ENGINES) + [q + '_q' for q in DMA_QUEUES]:
            self.sems[name] = self.stack.enter_context(nc.semaphore('sem_' + name))
        self.buf = {}
        self.last_drain = {e: -1 for e in ENGINES}
        self.force_drain = set()
        self.seen = {}

    @staticmethod
    def _keys(items):
        out = []
        for it in items or ():
            if isinstance(it, tuple):
                h, tag = it
            else:
                h, tag = it, None
            out.append((id(h), tag))
        return out

    def _state(self, tid, tag):
        return self.buf.setdefault((tid, tag), {'w': None, 'r': []})

    def _conflicting(self, tid, tag):
        states = []
        if tag is None:
            found = False
            for (t2, _g2), st in list(self.buf.items()):
                if t2 == tid:
                    states.append(st)
                    found = True
            if not found:
                states.append(self._state(tid, None))
        else:
            states.append(self._state(tid, tag))
            if (tid, None) in self.buf:
                states.append(self.buf[(tid, None)])
        return states

    def _edge(self, rec, producer):
        if producer is None or producer is rec:
            return
        if producer.eng == rec.eng and not producer.dma and not rec.dma:
            if rec.eng == 'tensor':
                return
        producer.inc = True
        rec.deps.append(producer)

    def _wire(self, rec, reads, writes):
        if rec.eng in self.force_drain and not rec.dma:
            if rec.eng not in ('tensor',):
                rec.pre_drain = True
                self.last_drain[rec.eng] = rec.idx
            self.force_drain.discard(rec.eng)
        for tid, tag in self._keys(reads):
            for st in self._conflicting(tid, tag):
                self._edge(rec, st['w'])
        for tid, tag in self._keys(writes):
            for st in self._conflicting(tid, tag):
                self._edge(rec, st['w'])
                for r in st['r']:
                    self._edge(rec, r)
        for tid, tag in self._keys(reads):
            self._state(tid, tag)['r'].append(rec)
        for tid, tag in self._keys(writes):
            st = self._state(tid, tag)
            st['w'] = rec
            st['r'] = []

    def op(self, eng, fn, reads=(), writes=()):
        rec = Rec(eng, fn, False, len(self.recs[eng]))
        self._wire(rec, reads, writes)
        self.recs[eng].append(rec)
        return rec

    def dma(self, queue, fn, reads=(), writes=()):
        rec = Rec(queue, fn, True, len(self.recs[queue]))
        self._wire(rec, reads, writes)
        self.recs[queue].append(rec)
        return rec

    def barrier(self):
        lasts = []
        for e in ENGINES:
            for r in reversed(self.recs[e]):
                if r.fn is not None:
                    r.inc = True
                    lasts.append(r)
                    break
        for e in ENGINES:
            rec = Rec(e, None, False, len(self.recs[e]))
            rec.deps = [l for l in lasts if l.eng != e or l.dma]
            self.recs[e].append(rec)
        self.force_drain = set(('vector', 'scalar'))

    def _sem_name(self, rec):
        return rec.eng + '_q' if rec.dma else rec.eng

    def finish_and_flush(self, block):
        lasts = []
        for q in DMA_QUEUES:
            dr = [r for r in self.recs[q] if r.dma]
            if dr:
                dr[-1].inc = True
                lasts.append(dr[-1])
        rec = Rec('sync', None, False, len(self.recs['sync']))
        rec.deps = lasts
        self.recs['sync'].append(rec)

        cnt = {}
        for e in ENGINES:
            for r in self.recs[e]:
                if r.inc:
                    sn = self._sem_name(r)
                    cnt[sn] = cnt.get(sn, 0) + (16 if r.dma else 1)
                    r.val = cnt[sn]

        def emit_for(eng):
            def body(eh):
                for r in self.recs[eng]:
                    for dep in r.deps:
                        sn = self._sem_name(dep)
                        key = (eng, sn)
                        if self.seen.get(key, 0) < dep.val:
                            self.seen[key] = dep.val
                            eh.wait_ge(self.sems[sn], dep.val)
                    if r.pre_drain:
                        eh.drain()
                    if r.fn is None:
                        assert not r.inc, 'barrier rec marked inc'
                        continue
                    inst = r.fn(eh)
                    if r.inc:
                        inst.then_inc(self.sems[self._sem_name(r)], 16 if r.dma else 1)
            return body

        for eng in ENGINES:
            if self.recs[eng]:
                getattr(block, eng)(emit_for(eng))


class Pool:

    def __init__(self, nc, stack, nbytes):
        self.nw = nbytes // 4
        self.base = stack.enter_context(nc.sbuf_tensor('pool', [128, self.nw], F32))
        self.cur = 0
        self.epoch = 0
        self.free_list = []
        self.live = {}

    def alloc(self, shape, dtype, name):
        assert name not in self.live, name
        esz = {F32: 1, BF16: 2}[dtype]
        n_el = int(np.prod(shape[1:]))
        nw = (n_el + esz - 1) // esz
        nw = (nw + 15) & ~15
        off = None
        for i, (o, size, ep) in enumerate(self.free_list):
            if ep < self.epoch and size >= nw:
                off = o
                if size > nw:
                    self.free_list[i] = (o + nw, size - nw, ep)
                else:
                    self.free_list.pop(i)
                break
        if off is None:
            off = self.cur
            assert off + nw <= self.nw, f'SBUF pool overflow at {name}: {(off + nw) * 4} bytes'
            self.cur = off + nw
        self.live[name] = (off, nw)
        v = self.base[:shape[0], off:off + nw]
        if dtype == BF16:
            v = v.bitcast(BF16)
        v = v[:, :n_el]
        if len(shape) > 2:
            dims = ' '.join(f'd{i}' for i in range(1, len(shape)))
            v = v.rearrange(f'p ({dims}) -> p {dims}',
                            **{f'd{i}': shape[i] for i in range(1, len(shape))})
        return v

    def alloc_list(self, n, shape, dtype, name):
        return [self.alloc(shape, dtype, f'{name}@{i}') for i in range(n)]

    def free(self, name):
        off, nw = self.live.pop(name)
        self.free_list.append((off, nw, self.epoch))
        self.free_list.sort()
        merged = []
        for o, s, ep in self.free_list:
            if merged and merged[-1][0] + merged[-1][1] == o and merged[-1][2] == ep:
                mo, ms, mep = merged[-1]
                merged[-1] = (mo, ms + s, mep)
            else:
                merged.append((o, s, ep))
        self.free_list = merged

    def free_listed(self, name, n):
        for i in range(n):
            self.free(f'{name}@{i}')

    def new_epoch(self):
        self.epoch += 1
        old_ep = self.epoch - 1
        self.free_list = [(o, s, min(ep, old_ep)) for o, s, ep in self.free_list]
        self.free_list.sort()
        merged = []
        for o, s, ep in self.free_list:
            if merged and merged[-1][0] + merged[-1][1] == o:
                merged[-1] = (merged[-1][0], merged[-1][1] + s, old_ep)
            else:
                merged.append((o, s, old_ep))
        self.free_list = merged
        if self.free_list and self.free_list[-1][0] + self.free_list[-1][1] == self.cur:
            self.cur = self.free_list[-1][0]
            self.free_list.pop()


def host_prep(w):
    g = {}
    f32 = lambda a: np.ascontiguousarray(np.asarray(a), dtype=np.float32)
    g['w_inT'] = f32(w['L0_attn_in_proj_w'].T)
    g['conv_w_pp'] = f32(np.transpose(np.reshape(np.asarray(w['L0_attn_conv_w'])[:, 0, :], (16, 128, 4)), (1, 0, 2)).reshape(128, 64))
    g['conv_b_pp'] = f32(np.reshape(w['L0_attn_conv_b'], (16, 128)).T)
    g['xpw_dtT'] = f32(np.asarray(w['L0_attn_x_proj_w'])[:DTR].T)
    g['xpw_BT'] = f32(np.asarray(w['L0_attn_x_proj_w'])[DTR:DTR + DS].T)
    g['xpw_CT'] = f32(np.asarray(w['L0_attn_x_proj_w'])[DTR + DS:].T)
    g['dtw_T'] = f32(w['L0_attn_dt_proj_w'].T)
    g['dtb_pp'] = f32(np.reshape(w['L0_attn_dt_proj_b'], (16, 128)).T)
    g['Alog_pp'] = f32(np.transpose(np.reshape(w['L0_attn_A_log'], (16, 128, DS)), (1, 0, 2)).reshape(128, 16 * DS))
    g['D_pp'] = f32(np.reshape(w['L0_attn_D'], (16, 128)).T)
    g['w_outT'] = f32(w['L0_attn_out_proj_w'].T)
    g['gate_wT'] = f32(w['L0_ffn_gate_w'].T)
    g['gate_b_bc'] = f32(np.broadcast_to(np.asarray(w['L0_ffn_gate_b'])[None, :], (128, 8)))
    w1 = f32(w['L0_ffn_w1']); w3 = f32(w['L0_ffn_w3'])
    g['w1gT'] = np.ascontiguousarray(np.transpose(w1[:, 0::2, :], (0, 2, 1)))
    g['w1lT'] = np.ascontiguousarray(np.transpose(w1[:, 1::2, :], (0, 2, 1)))
    g['w3gT'] = np.ascontiguousarray(np.transpose(w3[:, 0::2, :], (0, 2, 1)))
    g['w3lT'] = np.ascontiguousarray(np.transpose(w3[:, 1::2, :], (0, 2, 1)))
    b1 = f32(w['L0_ffn_b1']); b3 = f32(w['L0_ffn_b3'])
    g['b1g'] = np.ascontiguousarray(b1[:, 0::2]); g['b1l'] = np.ascontiguousarray(b1[:, 1::2])
    g['b3g'] = np.ascontiguousarray(b3[:, 0::2]); g['b3l'] = np.ascontiguousarray(b3[:, 1::2])
    g['w2T'] = np.ascontiguousarray(np.transpose(f32(w['L0_ffn_w2']), (0, 2, 1))) * np.float32(1.0 / 1.702)
    g['b2'] = f32(w['L0_ffn_b2'])
    sw1 = f32(w['L0_ffn_sw1']); sw3 = f32(w['L0_ffn_sw3'])
    g['sw1gT'] = np.ascontiguousarray(sw1[0::2, :].T); g['sw1lT'] = np.ascontiguousarray(sw1[1::2, :].T)
    g['sw3gT'] = np.ascontiguousarray(sw3[0::2, :].T); g['sw3lT'] = np.ascontiguousarray(sw3[1::2, :].T)
    sb1 = f32(w['L0_ffn_sb1']); sb3 = f32(w['L0_ffn_sb3'])
    g['sb1g'] = np.ascontiguousarray(sb1[0::2])[None, :]; g['sb1l'] = np.ascontiguousarray(sb1[1::2])[None, :]
    g['sb3g'] = np.ascontiguousarray(sb3[0::2])[None, :]; g['sb3l'] = np.ascontiguousarray(sb3[1::2])[None, :]
    g['sw2T'] = f32(w['L0_ffn_sw2'].T) * np.float32(1.0 / 1.702)
    g['sb2'] = f32(w['L0_ffn_sb2'])[None, :]
    g['WqT'] = f32(w['L1_attn_Wq'].T); g['WkT'] = f32(w['L1_attn_Wk'].T); g['WvT'] = f32(w['L1_attn_Wv'].T)
    g['bq'] = f32(w['L1_attn_bq'])[None, :]; g['bk'] = f32(w['L1_attn_bk'])[None, :]; g['bv'] = f32(w['L1_attn_bv'])[None, :]
    g['w_red'] = f32(w['L1_ffn_w_red'])
    g['wg'] = f32(w['L1_ffn_wg'])
    g['l1w1'] = f32(w['L1_ffn_w1'])
    g['l1b1_pp'] = np.ascontiguousarray(np.transpose(np.reshape(f32(w['L1_ffn_b1']), (8, 32, 128)), (0, 2, 1)))
    g['l1w2'] = f32(w['L1_ffn_w2'])
    g['l1b2'] = f32(w['L1_ffn_b2'])
    g['ident'] = np.eye(128, dtype=np.float32)
    g['ones_row'] = np.ones((1, NB), np.float32)
    tri = np.tril(np.ones((128, 128), np.float32))
    g['trineg'] = (tri - 1.0) * 1e30
    sel8 = np.zeros((8, 8 * 128), np.float32)
    for e in range(8):
        sel8[e, e * 128:(e + 1) * 128] = 1.0
    g['sel8'] = sel8
    return g


class Ctx:
    def __init__(self, nc, p, pool, dram):
        self.nc = nc
        self.p = p
        self.pool = pool
        self.dram = dram
        self.psum = []
        self.psum_bf = []
        self.consts = {}


def load_const_tiles(cx):
    p, pool = cx.p, cx.pool
    for name, shape, src in [('trineg', (128, 128), 'trineg'),
                             ('gate_b_bc', (128, 8), 'gate_b_bc')]:
        t = pool.alloc(shape, F32, 'c_' + name)
        cx.consts[name] = t
        p.dma('sync', lambda e, t=t, src=src: e.dma_start(out=t, in_=cx.dram[src]), writes=[t])
    for val, nm in ((1e-5, 'eps'), (1.0, 'one')):
        t = pool.alloc((128, 1), F32, 'c_' + nm)
        cx.consts[nm] = t
        p.op('vector', lambda e, t=t, val=val: e.memset(t, val), writes=[t])
    for name, shape, src in [('ident_bf', (128, 128), 'ident'),
                             ('ones_bf', (1, NB), 'ones_row'),
                             ('sel8_bf', (8, 8 * 128), 'sel8')]:
        t = pool.alloc(shape, BF16, 'c_' + name)
        cx.consts[name] = t
        p.dma('gpsimd', lambda e, t=t, src=src: e.dma_start(out=t, in_=cx.dram[src]), writes=[t])


def load_weight_fm(cx, src_ap, kchunks, mcols, name, queue='gpsimd'):
    ts = cx.pool.alloc_list(kchunks, (128, mcols), BF16, name)
    for kc in range(kchunks):
        cx.p.dma(queue, lambda e, t=ts[kc], kc=kc: e.dma_start(
            out=t, in_=src_ap[kc * 128:(kc + 1) * 128, :]), writes=[ts[kc]])
    return ts


def mm_group(cx, psum, pairs, extra=(), key=None):
    p = cx.p
    if key is None:
        key = psum
    allr = []
    for pr in pairs:
        allr += pr[2]
    for pr in extra:
        allr += pr[2]
    n = len(pairs)
    ne = len(extra)
    for i, pr in enumerate(pairs):
        lt, rt = pr[0], pr[1]
        start = (i == 0)
        stop = (i == n - 1) and ne == 0
        p.op('tensor', lambda e, lt=lt, rt=rt, s=start, st=stop: e.matmul(psum, lt, rt, start=s, stop=st),
             reads=allr if start else (), writes=[key] if (start or stop) else ())
    for j, pr in enumerate(extra):
        lt, rt = pr[0], pr[1]
        stop = (j == ne - 1)
        p.op('tensor', lambda e, lt=lt, rt=rt, st=stop: e.matmul(psum, lt, rt, start=False, stop=st),
             reads=(), writes=[key] if stop else ())


def emit_ln(cx, p, pool, x_src_fn, xn_hi, xn_lo, bufs, ti, route_fn=None):
    xt, xsq, xnf, xh, xl, st = (bufs[k] for k in ('xt', 'xsq', 'xnf', 'xh', 'xl', 'st'))
    x_src_fn(ti)
    p.op('vector', lambda e: e.tensor_reduce(out=st[:, 0:1], in_=xt, axis=AX.X, op=ALU.add),
         reads=[(xt, 'x1')], writes=[(st, 's')])
    p.op('scalar', lambda e: e.activation(out=xsq, in_=xt, func=AF.Square, accum_out=st[:, 1:2]),
         reads=[(xt, 'x1')], writes=[xsq, (st, 'q')])
    p.op('vector', lambda e: e.tensor_scalar_mul(st[:, 2:3], st[:, 0:1], 1.0 / D), reads=[(st, 's')], writes=[(st, 'm')])
    p.op('vector', lambda e: e.tensor_scalar_mul(st[:, 3:4], st[:, 0:1], -1.0 / D), reads=[(st, 's')], writes=[(st, 'nm')])
    p.op('vector', lambda e: e.tensor_tensor(st[:, 4:5], st[:, 2:3], st[:, 2:3], ALU.mult), reads=[(st, 'm')], writes=[(st, 'm2')])
    p.op('vector', lambda e: e.scalar_tensor_tensor(out=st[:, 5:6], in0=st[:, 1:2], scalar=1.0 / D, in1=st[:, 4:5],
                                                    op0=ALU.mult, op1=ALU.subtract),
         reads=[(st, 'q'), (st, 'm2')], writes=[(st, 'v')])
    p.op('scalar', lambda e: e.activation(out=st[:, 6:7], in_=st[:, 5:6], func=AF.Ln, bias=cx.consts['eps'][:, 0:1]),
         reads=[(st, 'v'), cx.consts['eps']], writes=[(st, 'ln')])
    p.op('scalar', lambda e: e.activation(out=st[:, 7:8], in_=st[:, 6:7], func=AF.Exp, scale=-0.5),
         reads=[(st, 'ln')], writes=[(st, 'r')])
    p.op('vector', lambda e: e.scalar_tensor_tensor(out=xnf, in0=xt, scalar=st[:, 3:4],
                                                    in1=st[:, 7:8].broadcast_to([128, D]),
                                                    op0=ALU.add, op1=ALU.mult),
         reads=[(xt, 'x1'), (st, 'nm'), (st, 'r')], writes=[xnf])
    p.op('scalar', lambda e: e.activation(out=xh, in_=xnf, func=AF.Copy), reads=[xnf], writes=[xh])
    p.op('vector', lambda e: e.tensor_tensor(xl, xnf, xh, ALU.subtract), reads=[xnf, xh], writes=[xl])
    for kc in range(8):
        for (src, dstl) in ((xh, xn_hi), (xl, xn_lo)):
            pb = cx.psum_bf[kc % 2]
            p.op('tensor', lambda e, src=src, kc=kc, pb=pb: e.transpose(
                pb[:, :], src[:, kc * 128:(kc + 1) * 128], cx.consts['ident_bf']),
                reads=[src, cx.consts['ident_bf']], writes=[pb])
            p.op('scalar', lambda e, dt_=dstl[kc], ti=ti, pb=pb: e.activation(
                out=dt_[:, ti * 128:(ti + 1) * 128], in_=pb[:, :], func=AF.Copy),
                reads=[pb], writes=[(dstl[kc], f't{ti}')])
    if route_fn is not None:
        route_fn(ti, xh, xl)


def emit_l0_route(cx, ti, xn_hi, xn_lo, gwhi, gwlo, cw8, rb):
    p = cx.p
    gb = cx.consts['gate_b_bc']
    cwt, scr1, scr2 = rb['cwt'], rb['scr1'], rb['scr2']
    ps = cx.psum[4]
    pairs = []
    for kc in range(8):
        for (xa, ga) in ((xn_hi, gwhi), (xn_hi, gwlo), (xn_lo, gwhi), (xn_lo, gwlo)):
            pairs.append((xa[kc][:, ti * 128:(ti + 1) * 128], ga[:, kc, :],
                          [(xa[kc], f't{ti}'), ga]))
    mm_group(cx, ps[:, 0:8], pairs, key=ps)
    p.op('vector', lambda e: e.tensor_copy(scr1, ps[:, 0:8]), reads=[ps], writes=[scr1])
    p.op('vector', lambda e: e.tensor_reduce(out=scr2[:, 8:9], in_=scr1, axis=AX.X, op=ALU.max),
         reads=[scr1], writes=[(scr2, 'c8')])
    p.op('vector', lambda e: e.tensor_scalar_mul(scr2[:, 9:10], scr2[:, 8:9], -1.0),
         reads=[(scr2, 'c8')], writes=[(scr2, 'c9')])
    p.op('scalar', lambda e: e.activation(out=cwt, in_=scr1, func=AF.Exp, bias=scr2[:, 9:10], accum_out=scr2[:, 10:11]),
         reads=[scr1, (scr2, 'c9')], writes=[cwt, (scr2, 'c10')])
    p.op('vector', lambda e: e.reciprocal(scr2[:, 11:12], scr2[:, 10:11]), reads=[(scr2, 'c10')], writes=[(scr2, 'c11')])
    p.op('vector', lambda e: e.tensor_scalar(out=cwt, in0=cwt, scalar1=scr2[:, 11:12], scalar2=None, op0=ALU.mult),
         reads=[cwt, (scr2, 'c11')], writes=[cwt])
    p.op('vector', lambda e: e.tensor_tensor(scr1, cwt, gb, ALU.add), reads=[cwt, gb], writes=[scr1])
    p.op('vector', lambda e: e.tensor_reduce(out=scr2[:, 12:13], in_=scr1, axis=AX.X, op=ALU.max),
         reads=[scr1], writes=[(scr2, 'c12')])
    p.op('vector', lambda e: e.tensor_scalar(out=scr2[:, 0:8], in0=scr1, scalar1=scr2[:, 12:13], scalar2=None, op0=ALU.is_ge),
         reads=[scr1, (scr2, 'c12')], writes=[(scr2, 'v8')])
    p.op('vector', lambda e: e.scalar_tensor_tensor(out=scr2[:, 0:8], in0=scr2[:, 0:8], scalar=-1e30, in1=scr1,
                                                    op0=ALU.mult, op1=ALU.add),
         reads=[(scr2, 'v8'), scr1], writes=[(scr2, 'v8')])
    p.op('vector', lambda e: e.tensor_reduce(out=scr2[:, 13:14], in_=scr2[:, 0:8], axis=AX.X, op=ALU.max),
         reads=[(scr2, 'v8')], writes=[(scr2, 'c13')])
    p.op('vector', lambda e: e.tensor_scalar(out=scr2[:, 0:8], in0=scr1, scalar1=scr2[:, 13:14], scalar2=None, op0=ALU.is_ge),
         reads=[scr1, (scr2, 'c13')], writes=[(scr2, 'v8')])
    p.op('vector', lambda e: e.tensor_tensor(cwt, cwt, scr2[:, 0:8], ALU.mult),
         reads=[cwt, (scr2, 'v8')], writes=[cwt])
    p.op('vector', lambda e: e.tensor_copy(rb['cwb'], cwt), reads=[cwt], writes=[rb['cwb']])
    pb = cx.psum_bf[0]
    p.op('tensor', lambda e, pb=pb: e.transpose(pb[0:8, :], rb['cwb'], cx.consts['ident_bf']),
         reads=[rb['cwb'], cx.consts['ident_bf']], writes=[pb])
    p.op('vector', lambda e, ti=ti, pb=pb: e.tensor_copy(cw8[:, ti * 128:(ti + 1) * 128], pb[0:8, :]),
         reads=[pb], writes=[(cw8, f't{ti}')])


def build(cfg):
    nc = bass.Bass('TRN2', target_bir_lowering=False)
    stop_after = cfg.get('stop_after')
    dbg_names = set(cfg.get('dbg', ()))

    dram = {}
    for k, shape in cfg['g_shapes'].items():
        dram[k] = nc.dram_tensor('g_' + k, list(shape), F32, kind='ExternalInput').ap()
    dram['x'] = nc.dram_tensor('x', [T, D], F32, kind='ExternalInput').ap()
    out = nc.dram_tensor('out', [T, D], F32, kind='ExternalOutput').ap()

    scrb = {}
    scrb['sz'] = nc.dram_tensor('scrb_sz', [DI, T], BF16, kind='Internal').ap()
    scrb['xc'] = nc.dram_tensor('scrb_xc', [DI, T], BF16, kind='Internal').ap()
    scrb['s0'] = nc.dram_tensor('scrb_s0', [D, T], BF16, kind='Internal').ap()
    x1_scr = nc.dram_tensor('scr_x1', [T, D], F32, kind='Internal').ap()
    s1_scr = nc.dram_tensor('scr_s1', [T, D], BF16, kind='Internal').ap()

    dbg = {}

    def mkdbg(name, shape, dt=F32):
        if name in dbg_names:
            dbg[name] = nc.dram_tensor('dbg_' + name, list(shape), dt, kind='ExternalOutput').ap()
            return dbg[name]
        return None

    stk = ExitStack()
    p = P(nc)
    pool = Pool(nc, stk, POOL_BYTES)
    cx = Ctx(nc, p, pool, dram)
    for i in range(6):
        cx.psum.append(stk.enter_context(nc.psum_tensor(f'ps{i}', [128, NB], F32)))
    cx.psum_bf.append(stk.enter_context(nc.psum_tensor('psbf0', [128, 128], BF16)))
    cx.psum_bf.append(stk.enter_context(nc.psum_tensor('psbf1', [128, 128], BF16)))

    load_const_tiles(cx)

    def done():
        blk = stk.enter_context(nc.Block())
        p.finish_and_flush(blk)
        stk.close()
        p.stack.close()
        return nc, dbg

    def dump_list(lst, name, width, dt=BF16):
        dd = mkdbg(name, (128, len(lst) * width), dt)
        if dd is not None:
            for i, t in enumerate(lst):
                p.dma('sync', lambda e, t=t, i=i: e.dma_start(out=dd[:, i * width:(i + 1) * width], in_=t),
                      reads=[(t, None)])

    cw8 = pool.alloc((8, T), BF16, 'cw8')
    xn_hi = pool.alloc_list(8, (128, T), BF16, 'xn_hi')
    xn_lo = pool.alloc_list(8, (128, T), BF16, 'xn_lo')
    lb = {k: pool.alloc(s, dt, 'a1_' + k) for k, s, dt in (
        ('xt', (128, D), F32), ('xsq', (128, D), F32), ('xnf', (128, D), F32),
        ('xh', (128, D), BF16), ('xl', (128, D), BF16), ('st', (128, 16), F32))}
    rb = {'cwt': pool.alloc((128, 8), F32, 'a1_cwt'),
          'scr1': pool.alloc((128, 8), F32, 'a1_scr1'),
          'scr2': pool.alloc((128, 16), F32, 'a1_scr2'),
          'cwb': pool.alloc((128, 8), BF16, 'a1_cwbf')}
    gw32 = pool.alloc((128, 8, 8), F32, 'a1_gw32')
    p.dma('sync', lambda e: e.dma_start(out=gw32, in_=dram['gate_wT'].rearrange('(kc p) m -> p kc m', p=128)),
          writes=[gw32])
    gwhi = pool.alloc((128, 8, 8), BF16, 'a1_gwhi')
    gwlo = pool.alloc((128, 8, 8), BF16, 'a1_gwlo')
    gw32f = gw32.rearrange('p a b -> p (a b)')
    gwhif = gwhi.rearrange('p a b -> p (a b)')
    gwlof = gwlo.rearrange('p a b -> p (a b)')
    p.op('vector', lambda e: e.tensor_copy(gwhif, gw32f), reads=[gw32], writes=[gwhi])
    p.op('vector', lambda e: e.tensor_tensor(gwlof, gw32f, gwhif, ALU.subtract), reads=[gw32, gwhi], writes=[gwlo])

    def x_src(ti):
        p.dma('sync', lambda e, ti=ti: e.dma_start(out=lb['xt'], in_=dram['x'][ti * 128:(ti + 1) * 128, :]),
              writes=[(lb['xt'], 'x1')])

    for ti in range(16):
        emit_ln(cx, p, pool, x_src, xn_hi, xn_lo, lb, ti,
                route_fn=lambda ti, xh, xl: emit_l0_route(cx, ti, xn_hi, xn_lo, gwhi, gwlo, cw8, rb))
    p.barrier()
    if stop_after == 'A1':
        dump_list(xn_hi, 'xn_hi', T)
        dump_list(xn_lo, 'xn_lo', T)
        d2 = mkdbg('cw8', (8, T), BF16)
        if d2 is not None:
            p.dma('sync', lambda e: e.dma_start(out=d2, in_=cw8), reads=[(cw8, None)])
        return done()
    for k in list(lb):
        pool.free('a1_' + k)
    for k in ('cwt', 'scr1', 'scr2', 'cwbf', 'gw32', 'gwhi', 'gwlo'):
        pool.free('a1_' + k)
    pool.free_listed('xn_lo', 8)
    pool.new_epoch()

    import os as _os2
    if _os2.environ.get('A2_SKIPW', '0') == '1':
        w_in = cx.pool.alloc_list(8, (128, 4096), BF16, 'w_in')
    else:
        w_in = load_weight_fm(cx, dram['w_inT'], 8, 4096, 'w_in')
    xi = pool.alloc_list(16, (128, T + 4), BF16, 'xi')
    if _os2.environ.get('A2_SKIPMS', '0') != '1':
        for gi in range(16):
            p.op('vector', lambda e, gi=gi: e.memset(xi[gi][:, 0:4], 0.0), writes=[xi[gi]])
    sz_t = pool.alloc((128, NB), BF16, 'sz_t')
    import os as _os
    _mlim = int(_os.environ.get('A2_MLIM', '32'))
    _skipsz = _os.environ.get('A2_SKIPSZ', '0') == '1'
    for m in range(_mlim):
        for n in range(T // NB):
            ps = cx.psum[m % 2]
            pairs = [(w_in[kc][:, m * 128:(m + 1) * 128],
                      xn_hi[kc][:, n * NB:(n + 1) * NB], [w_in[kc], (xn_hi[kc], None)]) for kc in range(8)]
            mm_group(cx, ps[:, :], pairs, key=ps)
            if m < 16:
                p.op('vector', lambda e, m=m, n=n, ps=ps: e.tensor_copy(
                    xi[m][:, 3 + n * NB: 3 + n * NB + NB], ps[:, :]),
                    reads=[ps], writes=[xi[m]])
            else:
                gi = m - 16
                p.op('scalar', lambda e, ps=ps: e.activation(out=sz_t, in_=ps[:, :], func=AF.Silu),
                     reads=[ps], writes=[sz_t])
                if not _skipsz:
                    p.dma('gpsimd', lambda e, gi=gi, n=n: e.dma_start(
                        out=scrb['sz'][gi * 128:(gi + 1) * 128, n * NB:(n + 1) * NB], in_=sz_t),
                        reads=[sz_t], writes=[(scrb['sz'], f'g{gi}')])
    p.barrier()
    if stop_after == 'A2':
        dump_list(xi, 'xi', T + 4)
        return done()
    pool.free_listed('w_in', 8)
    pool.free('sz_t')
    pool.new_epoch()

    conv_w = pool.alloc((128, 16, 4), F32, 'conv_w')
    conv_b = pool.alloc((128, 16), F32, 'conv_b')
    p.dma('sync', lambda e: e.dma_start(out=conv_w.rearrange('p a b -> p (a b)'), in_=dram['conv_w_pp']), writes=[conv_w])
    p.dma('sync', lambda e: e.dma_start(out=conv_b, in_=dram['conv_b_pp']), writes=[conv_b])
    xcv = pool.alloc_list(16, (128, T), BF16, 'xcv')
    cscr = pool.alloc((128, T), F32, 'cscr')
    for gi in range(16):
        p.op('vector', lambda e, gi=gi: e.tensor_scalar(
            out=cscr, in0=xi[gi][:, 3:3 + T], scalar1=conv_w[:, gi, 3:4], scalar2=None, op0=ALU.mult),
            reads=[xi[gi], conv_w], writes=[cscr])
        for j in range(3):
            p.op('vector', lambda e, gi=gi, j=j: e.scalar_tensor_tensor(
                out=cscr, in0=xi[gi][:, j:j + T], scalar=conv_w[:, gi, j:j + 1], in1=cscr,
                op0=ALU.mult, op1=ALU.add),
                reads=[xi[gi], conv_w, cscr], writes=[cscr])
        p.op('scalar', lambda e, gi=gi: e.activation(
            out=xcv[gi], in_=cscr, func=AF.Silu, bias=conv_b[:, gi:gi + 1]),
            reads=[cscr, conv_b], writes=[xcv[gi]])
        p.dma('gpsimd', lambda e, gi=gi: e.dma_start(
            out=scrb['xc'][gi * 128:(gi + 1) * 128, :], in_=xcv[gi]),
            reads=[xcv[gi]], writes=[(scrb['xc'], f'g{gi}')])
    p.barrier()
    pool.free_listed('xi', 16)
    pool.free('cscr')
    pool.new_epoch()
    xpw_dt = load_weight_fm(cx, dram['xpw_dtT'], 16, 64, 'xpw_dt')
    xpw_B = load_weight_fm(cx, dram['xpw_BT'], 16, 16, 'xpw_B')
    xpw_C = load_weight_fm(cx, dram['xpw_CT'], 16, 16, 'xpw_C')
    dtr_f = pool.alloc((64, T), BF16, 'dtr_f')
    B_f = pool.alloc((16, T), BF16, 'B_f')
    C_f = pool.alloc((16, T), BF16, 'C_f')
    for n in range(T // NB):
        for (wt, rows, dst) in ((xpw_dt, 64, dtr_f), (xpw_B, 16, B_f), (xpw_C, 16, C_f)):
            ps = cx.psum[3]
            pairs = [(wt[kc][:, :rows], xcv[kc][:, n * NB:(n + 1) * NB], [wt[kc], xcv[kc]]) for kc in range(16)]
            mm_group(cx, ps[:rows, :], pairs, key=ps)
            p.op('vector', lambda e, ps=ps, dst=dst, rows=rows, n=n: e.tensor_copy(
                dst[:rows, n * NB:(n + 1) * NB], ps[:rows, :]),
                reads=[ps], writes=[(dst, f'n{n}')])
    p.barrier()
    pool.free_listed('xcv', 16)
    pool.free_listed('xpw_dt', 16); pool.free_listed('xpw_B', 16); pool.free_listed('xpw_C', 16)
    pool.free('conv_w'); pool.free('conv_b')
    pool.new_epoch()
    if stop_after == 'A3':
        for nm, tt, rows in (('dtr_f', dtr_f, 64), ('B_f', B_f, 16), ('C_f', C_f, 16)):
            dd = mkdbg(nm, (rows, T), BF16)
            if dd is not None:
                p.dma('sync', lambda e, dd=dd, tt=tt, rows=rows: e.dma_start(out=dd, in_=tt[:rows, :]), reads=[(tt, None)])
        return done()

    emit_scan(cx, dtr_f, B_f, C_f, scrb)
    p.barrier()
    for nm in ('dtr_f', 'B_f', 'C_f'):
        pool.free(nm)
    pool.new_epoch()
    if stop_after == 'B':
        dd = mkdbg('s0', (D, T), BF16)
        if dd is not None:
            sload = pool.alloc((128, T), BF16, 'dbg_s0')
            for md in range(8):
                p.dma('sync', lambda e, md=md: e.dma_start(out=sload, in_=scrb['s0'][md * 128:(md + 1) * 128, :]),
                      reads=[(scrb['s0'], None)], writes=[sload])
                p.dma('sync', lambda e, md=md: e.dma_start(out=dd[md * 128:(md + 1) * 128, :], in_=sload),
                      reads=[sload], writes=[(dd, f'm{md}')])
        return done()

    y_acc = pool.alloc_list(8, (128, T), F32, 'y_acc')
    emit_l0ffn(cx, xn_hi, cw8, y_acc)
    p.barrier()
    pool.new_epoch()
    if stop_after == 'C':
        dump_list(y_acc, 'y_acc', T, F32)
        return done()

    cw1_8 = pool.alloc((8, T), BF16, 'cw1_8')
    emit_combine_ln(cx, y_acc, scrb['s0'], xn_hi, cw1_8, x1_scr)
    p.barrier()
    pool.new_epoch()
    if stop_after == 'D':
        dump_list(xn_hi, 'xn1_hi', T)
        d2 = mkdbg('cw1_8', (8, T), BF16)
        if d2 is not None:
            p.dma('sync', lambda e: e.dma_start(out=d2, in_=cw1_8), reads=[(cw1_8, None)])
        return done()

    s1 = pool.alloc_list(16, (128, D), BF16, 's1')
    emit_l1attn(cx, xn_hi, s1)
    for ti in range(16):
        p.dma('gpsimd', lambda e, ti=ti: e.dma_start(out=s1_scr[ti * 128:(ti + 1) * 128, :], in_=s1[ti]),
              reads=[s1[ti]], writes=[(s1_scr, f't{ti}')])
    p.barrier()
    if stop_after == 'E':
        dump_list(s1, 's1', D)
        return done()
    pool.free_listed('s1', 16)
    pool.new_epoch()

    y1 = pool.alloc_list(8, (128, T), F32, 'y1')
    emit_l1ffn(cx, xn_hi, cw1_8, y1)
    p.barrier()
    pool.new_epoch()

    emit_final(cx, y1, s1_scr, x1_scr, out)
    return done()


def emit_scan(cx, dtr_f, B_f, C_f, scrb):
    p, pool = cx.p, cx.pool
    L = LSC
    ones = cx.consts['ones_bf']
    dtw_t = pool.alloc((64, DI), BF16, 'dtw_t')
    p.dma('gpsimd', lambda e: e.dma_start(out=dtw_t, in_=cx.dram['dtw_T']), writes=[dtw_t])
    dtb = pool.alloc((128, 16), F32, 'dtb')
    p.dma('sync', lambda e: e.dma_start(out=dtb, in_=cx.dram['dtb_pp']), writes=[dtb])
    alog = pool.alloc((128, 16 * DS), F32, 'alog')
    p.dma('sync', lambda e: e.dma_start(out=alog, in_=cx.dram['Alog_pp']), writes=[alog])
    A_pp = pool.alloc((128, 16 * DS), F32, 'A_pp')
    p.op('scalar', lambda e: e.activation(out=A_pp, in_=alog, func=AF.Exp), reads=[alog], writes=[(A_pp, 'e')])
    p.op('vector', lambda e: e.tensor_scalar_mul(A_pp, A_pp, -1.0), reads=[(A_pp, 'e')], writes=[A_pp])
    D_pp = pool.alloc((128, 16), F32, 'D_pp')
    p.dma('sync', lambda e: e.dma_start(out=D_pp, in_=cx.dram['D_pp']), writes=[D_pp])
    w_out = load_weight_fm(cx, cx.dram['w_outT'], 16, D, 'w_out')

    dA = pool.alloc((128, DS * L), F32, 'dA')
    u = pool.alloc((128, DS * L), F32, 'u')
    h = pool.alloc((128, DS * L), F32, 'h')
    dtg = pool.alloc((128, L), F32, 'dtg')
    dtx = pool.alloc((128, L), F32, 'dtx')
    spx = pool.alloc((128, L), F32, 'spx')
    yg = pool.alloc((128, L), F32, 'yg')
    hc = pool.alloc((128, 16, DS), F32, 'hc')
    xcg = pool.alloc_list(16, (128, L), BF16, 'xcg')
    szg = pool.alloc_list(16, (128, L), BF16, 'szg')
    ygate = pool.alloc_list(16, (128, L), BF16, 'ygate')
    bflat = pool.alloc((1, DS * L), BF16, 'bflat')
    cflat = pool.alloc((1, DS * L), BF16, 'cflat')
    B_rep = pool.alloc((128, DS * L), BF16, 'B_rep')
    C_rep = pool.alloc((128, DS * L), BF16, 'C_rep')
    sout = pool.alloc((128, L), BF16, 'sout')
    for gi in range(16):
        p.op('vector', lambda e, gi=gi: e.memset(hc[:, gi, :], 0.0), writes=[(hc, f'g{gi}')])

    NCH = T // L
    for tc in range(NCH):
        t0 = tc * L
        p.dma('sync', lambda e, t0=t0: e.dma_start(out=bflat, in_=B_f[:, t0:t0 + L]),
              reads=[(B_f, None)], writes=[bflat])
        p.dma('sync', lambda e, t0=t0: e.dma_start(out=cflat, in_=C_f[:, t0:t0 + L]),
              reads=[(C_f, None)], writes=[cflat])
        for (flat, rep) in ((bflat, B_rep), (cflat, C_rep)):
            for nb in range(DS * L // NB):
                ps = cx.psum[5]
                mm_group(cx, ps[:, :], [(ones[:, 0:128], flat[:, nb * NB:(nb + 1) * NB], [flat, ones])], key=ps)
                p.op('scalar', lambda e, ps=ps, rep=rep, nb=nb: e.activation(
                    out=rep[:, nb * NB:(nb + 1) * NB], in_=ps[:, :], func=AF.Copy),
                    reads=[ps], writes=[(rep, f'n{nb}')])
        rep_reads_B = [(B_rep, f'n{nb}') for nb in range(DS * L // NB)]
        rep_reads_C = [(C_rep, f'n{nb}') for nb in range(DS * L // NB)]
        for gi in range(16):
            p.dma('sync', lambda e, gi=gi, t0=t0: e.dma_start(
                out=xcg[gi], in_=scrb['xc'][gi * 128:(gi + 1) * 128, t0:t0 + L]),
                reads=[(scrb['xc'], f'g{gi}')], writes=[xcg[gi]])
            p.dma('sync', lambda e, gi=gi, t0=t0: e.dma_start(
                out=szg[gi], in_=scrb['sz'][gi * 128:(gi + 1) * 128, t0:t0 + L]),
                reads=[(scrb['sz'], f'g{gi}')], writes=[szg[gi]])
        for gi in range(16):
            ps = cx.psum[4]
            mm_group(cx, ps[:, :L], [(dtw_t[:, gi * 128:(gi + 1) * 128], dtr_f[:, t0:t0 + L],
                                      [dtw_t, (dtr_f, None)])], key=ps)
            p.op('scalar', lambda e, gi=gi, ps=ps: e.activation(out=spx, in_=ps[:, :L], func=AF.Abs, bias=dtb[:, gi:gi + 1]),
                 reads=[ps, dtb], writes=[spx])
            p.op('scalar', lambda e: e.activation(out=spx, in_=spx, func=AF.Exp, scale=-1.0),
                 reads=[spx], writes=[spx])
            p.op('scalar', lambda e: e.activation(out=spx, in_=spx, func=AF.Ln, bias=cx.consts['one'][:, 0:1]),
                 reads=[spx, cx.consts['one']], writes=[spx])
            p.op('scalar', lambda e, gi=gi, ps=ps: e.activation(out=dtg, in_=ps[:, :L], func=AF.Relu, bias=dtb[:, gi:gi + 1]),
                 reads=[ps, dtb], writes=[dtg])
            p.op('vector', lambda e: e.tensor_tensor(dtg, dtg, spx, ALU.add),
                 reads=[dtg, spx], writes=[dtg])
            p.op('vector', lambda e, gi=gi: e.tensor_tensor(dtx, dtg, xcg[gi], ALU.mult),
                 reads=[dtg, xcg[gi]], writes=[dtx])
            for s in range(DS):
                p.op('scalar', lambda e, gi=gi, s=s: e.activation(
                    out=dA[:, s * L:(s + 1) * L], in_=dtg, func=AF.Exp, scale=A_pp[:, gi * DS + s: gi * DS + s + 1]),
                    reads=[dtg, A_pp], writes=[(dA, f's{s}')])
            dA_reads = [(dA, f's{s}') for s in range(DS)]
            H = DS // 2
            for hs in range(2):
                sl = slice(hs * H * L, (hs + 1) * H * L)
                p.op('vector', lambda e, sl=sl: e.tensor_tensor(
                    u[:, sl].rearrange('p (s l) -> p s l', s=H),
                    dtx[:, None, :].broadcast_to([128, H, L]),
                    B_rep[:, sl].rearrange('p (s l) -> p s l', s=H),
                    ALU.mult),
                    reads=[dtx] + rep_reads_B, writes=[(u, f'h{hs}')])
            if tc > 0:
                p.op('vector', lambda e, gi=gi: e.tensor_tensor(dtx[:, 0:DS], dA[:, 0::L], hc[:, gi, :], ALU.mult),
                     reads=dA_reads + [(hc, f'g{gi}')], writes=[dtx])
                p.op('vector', lambda e: e.tensor_tensor(u[:, 0::L], dtx[:, 0:DS], u[:, 0::L], ALU.add),
                     reads=[dtx, (u, None)], writes=[(u, None)])
            p.op('vector', lambda e: e.memset(dA[:, 0::L], 0.0), writes=[(dA, None)])
            for hs in range(2):
                sl = slice(hs * H * L, (hs + 1) * H * L)
                dtags = [(dA, f's{s}') for s in range(hs * H, (hs + 1) * H)]
                p.op('vector', lambda e, sl=sl: e.tensor_tensor_scan(
                    h[:, sl], dA[:, sl], u[:, sl], 0.0, ALU.mult, ALU.add),
                    reads=dtags + [(u, f'h{hs}')], writes=[(h, f'h{hs}')])
            p.op('vector', lambda e, gi=gi: e.tensor_copy(hc[:, gi, :], h[:, L - 1::L]),
                 reads=[(h, 'h0'), (h, 'h1')], writes=[(hc, f'g{gi}')])
            for hs in range(2):
                sl = slice(hs * H * L, (hs + 1) * H * L)
                wtags = [(dA, f's{s}') for s in range(hs * H, (hs + 1) * H)]
                p.op('vector', lambda e, sl=sl: e.tensor_tensor(
                    dA[:, sl].rearrange('p (s l) -> p s l', s=H),
                    h[:, sl].rearrange('p (s l) -> p s l', s=H),
                    C_rep[:, sl].rearrange('p (s l) -> p s l', s=H),
                    ALU.mult),
                    reads=[(h, f'h{hs}')] + rep_reads_C, writes=wtags)
            for w in (8, 4, 2, 1):
                rtags = [(dA, f's{s}') for s in range(2 * w)]
                wt = [(dA, f's{s}') for s in range(w)]
                p.op('vector', lambda e, w=w: e.tensor_tensor(
                    dA[:, :w * L], dA[:, :w * L], dA[:, w * L:2 * w * L], ALU.add),
                    reads=rtags, writes=wt)
            p.op('vector', lambda e, gi=gi: e.scalar_tensor_tensor(
                out=yg, in0=xcg[gi], scalar=D_pp[:, gi:gi + 1], in1=dA[:, :L], op0=ALU.mult, op1=ALU.add),
                reads=[xcg[gi], D_pp, (dA, 's0')], writes=[yg])
            p.op('vector', lambda e, gi=gi: e.tensor_tensor(ygate[gi], yg, szg[gi], ALU.mult),
                 reads=[yg, szg[gi]], writes=[ygate[gi]])
        for md in range(8):
            ps = cx.psum[5]
            pairs = [(w_out[kc][:, md * 128:(md + 1) * 128], ygate[kc], [w_out[kc], ygate[kc]])
                     for kc in range(16)]
            mm_group(cx, ps[:, :L], pairs, key=ps)
            p.op('scalar', lambda e, ps=ps: e.activation(out=sout, in_=ps[:, :L], func=AF.Copy),
                 reads=[ps], writes=[sout])
            p.dma('gpsimd', lambda e, md=md, t0=t0: e.dma_start(
                out=scrb['s0'][md * 128:(md + 1) * 128, t0:t0 + L], in_=sout),
                reads=[sout], writes=[(scrb['s0'], f'm{md}c{t0}')])
    for nm in ('dtw_t', 'dtb', 'alog', 'A_pp', 'D_pp', 'dA', 'u', 'h', 'dtg', 'dtx',
               'spx', 'yg', 'hc', 'bflat', 'cflat', 'B_rep', 'C_rep', 'sout'):
        pool.free(nm)
    pool.free_listed('w_out', 16)
    for nm in ('xcg', 'szg', 'ygate'):
        pool.free_listed(nm, 16)


def emit_l0ffn(cx, xn_hi, cw8, y_acc):
    p, pool = cx.p, cx.pool
    ones = cx.consts['ones_bf']
    d = cx.dram
    for md in range(8):
        p.op('vector', lambda e, md=md: e.memset(y_acc[md], 0.0), writes=[y_acc[md]])
    b2t = pool.alloc((8, D), BF16, 'b2t')
    p.dma('gpsimd', lambda e: e.dma_start(out=b2t, in_=d['b2']), writes=[b2t])
    for md in range(8):
        for n in range(T // NB):
            ps = cx.psum[5]
            mm_group(cx, ps[:, :], [(b2t[:, md * 128:(md + 1) * 128], cw8[:, n * NB:(n + 1) * NB],
                                     [b2t, (cw8, None)])], key=ps)
            p.op('vector', lambda e, md=md, n=n, ps=ps: e.tensor_tensor(
                y_acc[md][:, n * NB:(n + 1) * NB], y_acc[md][:, n * NB:(n + 1) * NB], ps[:, :], ALU.add),
                reads=[ps, y_acc[md]], writes=[y_acc[md]])
    cwb = pool.alloc((128, T), BF16, 'cwb')
    ev = {k: pool.alloc((128, NB), F32, 'l0_' + k) for k in ('hg', 'h3g', 'hl', 'h3l', 'sg')}
    ab4 = pool.alloc_list(4, (128, NB), BF16, 'ab4')
    wpar = []
    for par in range(2):
        wpar.append({m: pool.alloc_list(8, (128, 512), BF16, f'l0w_{par}_{m}')
                     for m in ('w1g', 'w1l', 'w3g', 'w3l')})
    w2b = pool.alloc_list(4, (128, 1024), BF16, 'l0w2')
    bgs = {m: pool.alloc((1, 1024), BF16, 'l0b_' + m) for m in ('bg1', 'bl1', 'bg3', 'bl3')}

    def load_half(par, wsrcs, hh):
        for mi, m in enumerate(('w1g', 'w1l', 'w3g', 'w3l')):
            for kc in range(8):
                t = wpar[par][m][kc]
                p.dma('gpsimd', lambda e, t=t, src=wsrcs[mi], hh=hh, kc=kc: e.dma_start(
                    out=t, in_=src[kc * 128:(kc + 1) * 128, hh * 512:(hh + 1) * 512]), writes=[t])

    def load_w2(w2src, hh):
        for kc in range(4):
            t = w2b[kc]
            p.dma('gpsimd', lambda e, t=t, src=w2src, hh=hh, kc=kc: e.dma_start(
                out=t, in_=src[hh * 512 + kc * 128: hh * 512 + (kc + 1) * 128, :]), writes=[t])

    passes = []
    for ei in range(E):
        passes.append((ei, (d['w1gT'][ei], d['w1lT'][ei], d['w3gT'][ei], d['w3lT'][ei]),
                       (d['b1g'][ei:ei + 1], d['b1l'][ei:ei + 1], d['b3g'][ei:ei + 1], d['b3l'][ei:ei + 1]),
                       d['w2T'][ei], True))
    passes.append((0, (d['sw1gT'], d['sw1lT'], d['sw3gT'], d['sw3lT']),
                   (d['sb1g'], d['sb1l'], d['sb3g'], d['sb3l']), d['sw2T'], False))

    halves = [(pi, hh) for pi in range(len(passes)) for hh in range(2)]
    load_half(0, passes[0][1], 0)
    for idx, (pi, hh) in enumerate(halves):
        ei, wsrcs, bsrcs, w2src, scaled = passes[pi]
        par = idx % 2
        if idx + 1 < len(halves):
            pi2, hh2 = halves[idx + 1]
            load_half(1 - par, passes[pi2][1], hh2)
        load_w2(w2src, hh)
        if hh == 0:
            if scaled:
                for n in range(T // NB):
                    ps = cx.psum[5]
                    mm_group(cx, ps[:, :], [(cx.consts['sel8_bf'][:, ei * 128:(ei + 1) * 128],
                                             cw8[:, n * NB:(n + 1) * NB], [cx.consts['sel8_bf'], (cw8, None)])], key=ps)
                    p.op('scalar', lambda e, n=n, ps=ps: e.activation(out=cwb[:, n * NB:(n + 1) * NB],
                                                                      in_=ps[:, :], func=AF.Copy),
                         reads=[ps], writes=[(cwb, f'n{n}')])
            for m, srcb in (('bg1', bsrcs[0]), ('bl1', bsrcs[1]), ('bg3', bsrcs[2]), ('bl3', bsrcs[3])):
                p.dma('gpsimd', lambda e, t=bgs[m], src=srcb: e.dma_start(out=t, in_=src), writes=[bgs[m]])
        W = wpar[par]
        for n in range(T // NB):
            for mf in range(4):
                fa = hh * 512 + mf * 128
                for ci, (m, bm) in enumerate((('w1g', 'bg1'), ('w3g', 'bg3'), ('w1l', 'bl1'), ('w3l', 'bl3'))):
                    psb = cx.psum[ci]
                    pairs = [(W[m][kc][:, mf * 128:(mf + 1) * 128], xn_hi[kc][:, n * NB:(n + 1) * NB],
                              [W[m][kc], (xn_hi[kc], None)]) for kc in range(8)]
                    extra = [(bgs[bm][:, fa:fa + 128], ones[:, :NB], [bgs[bm], ones])]
                    mm_group(cx, psb[:, :], pairs, extra=extra, key=psb)
                p.op('scalar', lambda e: e.activation(out=ev['hg'], in_=cx.psum[0][:, :], func=AF.Copy),
                     reads=[cx.psum[0]], writes=[ev['hg']])
                p.op('scalar', lambda e: e.activation(out=ev['h3g'], in_=cx.psum[1][:, :], func=AF.Copy),
                     reads=[cx.psum[1]], writes=[ev['h3g']])
                p.op('scalar', lambda e: e.activation(out=ev['hl'], in_=cx.psum[2][:, :], func=AF.Copy),
                     reads=[cx.psum[2]], writes=[ev['hl']])
                p.op('vector', lambda e: e.tensor_copy(ev['h3l'], cx.psum[3][:, :]),
                     reads=[cx.psum[3]], writes=[ev['h3l']])
                p.op('vector', lambda e: e.tensor_tensor(ev['hg'], ev['hg'], ev['h3g'], ALU.mult),
                     reads=[ev['hg'], ev['h3g']], writes=[ev['hg']])
                p.op('vector', lambda e: e.tensor_tensor(ev['hl'], ev['hl'], ev['h3l'], ALU.mult),
                     reads=[ev['hl'], ev['h3l']], writes=[ev['hl']])
                p.op('vector', lambda e: e.tensor_scalar(out=ev['hg'], in0=ev['hg'], scalar1=7.0, scalar2=None, op0=ALU.min),
                     reads=[ev['hg']], writes=[ev['hg']])
                p.op('scalar', lambda e: e.activation(out=ev['sg'], in_=ev['hg'], func=AF.Silu, scale=1.702),
                     reads=[ev['hg']], writes=[ev['sg']])
                p.op('vector', lambda e: e.tensor_scalar(out=ev['hl'], in0=ev['hl'], scalar1=7.0, scalar2=-7.0,
                                                         op0=ALU.min, op1=ALU.max),
                     reads=[ev['hl']], writes=[ev['hl']])
                p.op('vector', lambda e: e.scalar_tensor_tensor(out=ev['hl'], in0=ev['hl'], scalar=1.0, in1=ev['sg'],
                                                                op0=ALU.add, op1=ALU.mult),
                     reads=[ev['hl'], ev['sg']], writes=[ev['hl']])
                if scaled:
                    p.op('vector', lambda e, mf=mf, n=n: e.tensor_tensor(
                        ab4[mf], ev['hl'], cwb[:, n * NB:(n + 1) * NB], ALU.mult),
                        reads=[ev['hl'], (cwb, f'n{n}')], writes=[ab4[mf]])
                else:
                    p.op('vector', lambda e, mf=mf: e.tensor_copy(ab4[mf], ev['hl']),
                         reads=[ev['hl']], writes=[ab4[mf]])
            for md in range(8):
                ps = cx.psum[4 + md % 2]
                pairs = [(w2b[kc][:, md * 128:(md + 1) * 128], ab4[kc], [w2b[kc], ab4[kc]]) for kc in range(4)]
                mm_group(cx, ps[:, :], pairs, key=ps)
                p.op('vector', lambda e, md=md, n=n, ps=ps: e.tensor_tensor(
                    y_acc[md][:, n * NB:(n + 1) * NB], y_acc[md][:, n * NB:(n + 1) * NB], ps[:, :], ALU.add),
                    reads=[ps, y_acc[md]], writes=[y_acc[md]])
    sb2t = pool.alloc((1, D), BF16, 'sb2t')
    p.dma('gpsimd', lambda e: e.dma_start(out=sb2t, in_=d['sb2']), writes=[sb2t])
    for md in range(8):
        for n in range(T // NB):
            ps = cx.psum[5]
            mm_group(cx, ps[:, :], [(sb2t[:, md * 128:(md + 1) * 128], ones[:, :NB], [sb2t, ones])], key=ps)
            p.op('vector', lambda e, md=md, n=n, ps=ps: e.tensor_tensor(
                y_acc[md][:, n * NB:(n + 1) * NB], y_acc[md][:, n * NB:(n + 1) * NB], ps[:, :], ALU.add),
                reads=[ps, y_acc[md]], writes=[y_acc[md]])
    for nm in ('b2t', 'cwb', 'sb2t'):
        pool.free(nm)
    for k in ev:
        pool.free('l0_' + k)
    pool.free_listed('ab4', 4)
    for par in range(2):
        for m in ('w1g', 'w1l', 'w3g', 'w3l'):
            pool.free_listed(f'l0w_{par}_{m}', 8)
    pool.free_listed('l0w2', 4)
    for m in bgs:
        pool.free('l0b_' + m)


def emit_combine_ln(cx, y_acc, s0_scr, xn_hi, cw1_8, x1_scr):
    p, pool = cx.p, cx.pool
    gbf = pool.alloc_list(8, (128, T), BF16, 'gbf')
    for md in range(8):
        p.op('scalar', lambda e, md=md: e.activation(out=gbf[md], in_=y_acc[md], func=AF.Silu),
             reads=[y_acc[md]], writes=[gbf[md]])
    p.barrier()
    pool.free_listed('y_acc', 8)
    pool.new_epoch()
    s0f = pool.alloc_list(8, (128, T), BF16, 's0f')
    for md in range(8):
        p.dma('sync', lambda e, md=md: e.dma_start(out=s0f[md], in_=s0_scr[md * 128:(md + 1) * 128, :]),
              reads=[(s0_scr, None)], writes=[s0f[md]])
    xn_lo = pool.alloc_list(8, (128, T), BF16, 'xn_lo1')
    wred32 = pool.alloc((128, 8, 16), F32, 'wred32')
    p.dma('sync', lambda e: e.dma_start(out=wred32, in_=cx.dram['w_red'].rearrange('(kc p) m -> p kc m', p=128)),
          writes=[wred32])
    wredh = pool.alloc((128, 8, 16), BF16, 'wredh')
    wredl = pool.alloc((128, 8, 16), BF16, 'wredl')
    wr32f = wred32.rearrange('p a b -> p (a b)')
    wrhf = wredh.rearrange('p a b -> p (a b)')
    wrlf = wredl.rearrange('p a b -> p (a b)')
    p.op('vector', lambda e: e.tensor_copy(wrhf, wr32f), reads=[wred32], writes=[wredh])
    p.op('vector', lambda e: e.tensor_tensor(wrlf, wr32f, wrhf, ALU.subtract), reads=[wred32, wredh], writes=[wredl])
    wgt = pool.alloc((8, 16), F32, 'wgt')
    p.dma('sync', lambda e: e.dma_start(out=wgt, in_=cx.dram['wg']), writes=[wgt])
    wsc = pool.alloc((8, 4), F32, 'wsc')
    wsq = pool.alloc((8, 16), F32, 'wsq')
    p.op('scalar', lambda e: e.activation(out=wsq, in_=wgt, func=AF.Square, accum_out=wsc[:, 0:1]),
         reads=[wgt], writes=[wsq, (wsc, 'c0')])
    p.op('scalar', lambda e: e.activation(out=wsc[:, 1:2], in_=wsc[:, 0:1], func=AF.Sqrt),
         reads=[(wsc, 'c0')], writes=[(wsc, 'c1')])
    p.op('vector', lambda e: e.tensor_scalar(out=wsc[:, 1:2], in0=wsc[:, 1:2], scalar1=1e-4, scalar2=None, op0=ALU.max),
         reads=[(wsc, 'c1')], writes=[(wsc, 'c1')])
    p.op('vector', lambda e: e.reciprocal(wsc[:, 2:3], wsc[:, 1:2]), reads=[(wsc, 'c1')], writes=[(wsc, 'c2')])
    p.op('vector', lambda e: e.tensor_scalar(out=wgt, in0=wgt, scalar1=wsc[:, 2:3], scalar2=None, op0=ALU.mult),
         reads=[wgt, (wsc, 'c2')], writes=[wgt])
    wgh = pool.alloc((8, 16), BF16, 'wgh')
    wgl = pool.alloc((8, 16), BF16, 'wgl')
    p.op('vector', lambda e: e.tensor_copy(wgh, wgt), reads=[wgt], writes=[wgh])
    p.op('vector', lambda e: e.tensor_tensor(wgl, wgt, wgh, ALU.subtract), reads=[wgt, wgh], writes=[wgl])
    wghT = pool.alloc((16, 8), BF16, 'wghT')
    wglT = pool.alloc((16, 8), BF16, 'wglT')
    for (src, dst) in ((wgh, wghT), (wgl, wglT)):
        pb = cx.psum_bf[1]
        p.op('tensor', lambda e, src=src, pb=pb: e.transpose(pb[0:16, 0:8], src, cx.consts['ident_bf'][0:8, 0:8]),
             reads=[src, cx.consts['ident_bf']], writes=[pb])
        p.op('vector', lambda e, dst=dst, pb=pb: e.tensor_copy(dst, pb[0:16, 0:8]), reads=[pb], writes=[dst])

    lb = {k: pool.alloc(s, dt, 'd_' + k) for k, s, dt in (
        ('xt', (128, D), F32), ('xsq', (128, D), F32), ('xnf', (128, D), F32),
        ('xh', (128, D), BF16), ('xl', (128, D), BF16), ('st', (128, 16), F32))}
    gt = pool.alloc((128, D), BF16, 'd_gt')
    smt = pool.alloc((128, D), BF16, 'd_smt')
    redt = pool.alloc((128, 16), F32, 'd_redt')
    redh = pool.alloc((128, 16), BF16, 'd_redh')
    redl = pool.alloc((128, 16), BF16, 'd_redl')
    rht = pool.alloc((16, 128), BF16, 'd_rht')
    rlt = pool.alloc((16, 128), BF16, 'd_rlt')
    lg = pool.alloc((128, 8), F32, 'd_lg')
    r1 = pool.alloc((128, 16), F32, 'd_r1')
    r2 = pool.alloc((128, 8), F32, 'd_r2')
    cwbf = pool.alloc((128, 8), BF16, 'd_cwbf')

    def x_src(ti):
        for md in range(8):
            pb = cx.psum_bf[md % 2]
            p.op('tensor', lambda e, md=md, ti=ti, pb=pb: e.transpose(
                pb[:, :], gbf[md][:, ti * 128:(ti + 1) * 128], cx.consts['ident_bf']),
                reads=[gbf[md]], writes=[pb])
            p.op('scalar', lambda e, md=md, pb=pb: e.activation(out=gt[:, md * 128:(md + 1) * 128], in_=pb[:, :], func=AF.Copy),
                 reads=[pb], writes=[(gt, f'm{md}')])
            pb2 = cx.psum_bf[(md + 1) % 2]
            p.op('tensor', lambda e, md=md, ti=ti, pb2=pb2: e.transpose(
                pb2[:, :], s0f[md][:, ti * 128:(ti + 1) * 128], cx.consts['ident_bf']),
                reads=[s0f[md]], writes=[pb2])
            p.op('scalar', lambda e, md=md, pb2=pb2: e.activation(out=smt[:, md * 128:(md + 1) * 128], in_=pb2[:, :], func=AF.Copy),
                 reads=[pb2], writes=[(smt, f'm{md}')])
        p.dma('sync', lambda e, ti=ti: e.dma_start(out=lb['xnf'], in_=cx.dram['x'][ti * 128:(ti + 1) * 128, :]),
              writes=[(lb['xnf'], 'x0')])
        greads = [(gt, f'm{m}') for m in range(8)] + [(smt, f'm{m}') for m in range(8)]
        p.op('vector', lambda e: e.tensor_tensor(lb['xt'], gt, smt, ALU.mult), reads=greads, writes=[(lb['xt'], 'x1')])
        p.op('vector', lambda e: e.tensor_tensor(lb['xt'], lb['xt'], lb['xnf'], ALU.add),
             reads=[(lb['xt'], 'x1'), (lb['xnf'], 'x0')], writes=[(lb['xt'], 'x1')])
        p.dma('sync', lambda e, ti=ti: e.dma_start(out=x1_scr[ti * 128:(ti + 1) * 128, :], in_=lb['xt']),
              reads=[(lb['xt'], 'x1')], writes=[(x1_scr, f't{ti}')])

    def route1(ti, xh, xl):
        ps = cx.psum[4]
        pairs = []
        for kc in range(8):
            for (xa, ga) in ((xn_hi, wredh), (xn_hi, wredl), (xn_lo, wredh), (xn_lo, wredl)):
                pairs.append((xa[kc][:, ti * 128:(ti + 1) * 128], ga[:, kc, :],
                              [(xa[kc], f't{ti}'), ga]))
        mm_group(cx, ps[:, 0:16], pairs, key=ps)
        p.op('vector', lambda e, ps=ps: e.tensor_copy(redt, ps[:, 0:16]), reads=[ps], writes=[redt])
        p.op('vector', lambda e: e.tensor_copy(redh, redt), reads=[redt], writes=[redh])
        p.op('vector', lambda e: e.tensor_tensor(redl, redt, redh, ALU.subtract), reads=[redt, redh], writes=[redl])
        for (src, dst) in ((redh, rht), (redl, rlt)):
            pb = cx.psum_bf[1]
            p.op('tensor', lambda e, src=src, pb=pb: e.transpose(pb[0:16, :], src, cx.consts['ident_bf']),
                 reads=[src], writes=[pb])
            p.op('vector', lambda e, dst=dst, pb=pb: e.tensor_copy(dst, pb[0:16, :]), reads=[pb], writes=[dst])
        ps2 = cx.psum[5]
        pairs2 = [(rht, wghT, [rht, wghT]), (rht, wglT, [rht, wglT]),
                  (rlt, wghT, [rlt, wghT]), (rlt, wglT, [rlt, wglT])]
        mm_group(cx, ps2[:, 0:8], pairs2, key=ps2)
        p.op('vector', lambda e, ps2=ps2: e.tensor_copy(lg, ps2[:, 0:8]), reads=[ps2], writes=[lg])
        p.op('vector', lambda e: e.tensor_reduce(out=r1[:, 0:1], in_=lg, axis=AX.X, op=ALU.max),
             reads=[lg], writes=[(r1, 'c0')])
        p.op('vector', lambda e: e.tensor_scalar_mul(r1[:, 1:2], r1[:, 0:1], -1.0), reads=[(r1, 'c0')], writes=[(r1, 'c1')])
        p.op('scalar', lambda e: e.activation(out=r2, in_=lg, func=AF.Exp, bias=r1[:, 1:2], accum_out=r1[:, 2:3]),
             reads=[lg, (r1, 'c1')], writes=[r2, (r1, 'c2')])
        p.op('vector', lambda e: e.reciprocal(r1[:, 3:4], r1[:, 2:3]), reads=[(r1, 'c2')], writes=[(r1, 'c3')])
        p.op('vector', lambda e: e.tensor_scalar(out=r2, in0=lg, scalar1=r1[:, 0:1], scalar2=None, op0=ALU.is_ge),
             reads=[lg, (r1, 'c0')], writes=[r2])
        p.op('vector', lambda e: e.tensor_scalar(out=r2, in0=r2, scalar1=r1[:, 3:4], scalar2=None, op0=ALU.mult),
             reads=[r2, (r1, 'c3')], writes=[r2])
        p.op('vector', lambda e: e.tensor_copy(cwbf, r2), reads=[r2], writes=[cwbf])
        pb = cx.psum_bf[0]
        p.op('tensor', lambda e, pb=pb: e.transpose(pb[0:8, :], cwbf, cx.consts['ident_bf']),
             reads=[cwbf], writes=[pb])
        p.op('vector', lambda e, ti=ti, pb=pb: e.tensor_copy(cw1_8[:, ti * 128:(ti + 1) * 128], pb[0:8, :]),
             reads=[pb], writes=[(cw1_8, f't{ti}')])

    for ti in range(16):
        emit_ln(cx, p, pool, x_src, xn_hi, xn_lo, lb, ti, route_fn=route1)
    p.barrier()
    for k in list(lb):
        pool.free('d_' + k)
    pool.free_listed('gbf', 8)
    pool.free_listed('s0f', 8)
    pool.free_listed('xn_lo1', 8)
    for nm in ('wred32', 'wredh', 'wredl', 'wgt', 'wsc', 'wsq', 'wgh', 'wgl',
               'wghT', 'wglT', 'd_gt', 'd_smt', 'd_redt', 'd_redh', 'd_redl', 'd_rht', 'd_rlt',
               'd_lg', 'd_r1', 'd_r2', 'd_cwbf'):
        pool.free(nm)


def emit_l1attn(cx, xn_hi, s1):
    p, pool = cx.p, cx.pool
    ones = cx.consts['ones_bf']
    wk = load_weight_fm(cx, cx.dram['WkT'], 8, D, 'wk')
    wv = load_weight_fm(cx, cx.dram['WvT'], 8, D, 'wv')
    bq = pool.alloc((1, D), BF16, 'bq'); bk = pool.alloc((1, D), BF16, 'bk'); bv = pool.alloc((1, D), BF16, 'bv')
    for t, src in ((bq, 'bq'), (bk, 'bk'), (bv, 'bv')):
        p.dma('gpsimd', lambda e, t=t, src=src: e.dma_start(out=t, in_=cx.dram[src]), writes=[t])
    K_f = pool.alloc_list(8, (128, T), BF16, 'K_f')
    V_tm = pool.alloc_list(16, (128, D), BF16, 'V_tm')
    for m in range(8):
        for n in range(T // NB):
            ps = cx.psum[m % 4]
            pairs = [(wk[kc][:, m * 128:(m + 1) * 128], xn_hi[kc][:, n * NB:(n + 1) * NB],
                      [wk[kc], (xn_hi[kc], None)]) for kc in range(8)]
            extra = [(bk[:, m * 128:(m + 1) * 128], ones[:, :NB], [bk, ones])]
            mm_group(cx, ps[:, :], pairs, extra=extra, key=ps)
            p.op('scalar', lambda e, m=m, n=n, ps=ps: e.activation(
                out=K_f[m][:, n * NB:(n + 1) * NB], in_=ps[:, :], func=AF.Copy),
                reads=[ps], writes=[K_f[m]])
    for ti in range(16):
        for nb2 in range(2):
            ps = cx.psum[4 + (nb2 % 2)]
            pairs = [(xn_hi[kc][:, ti * 128:(ti + 1) * 128], wv[kc][:, nb2 * NB:(nb2 + 1) * NB],
                      [(xn_hi[kc], None), wv[kc]]) for kc in range(8)]
            extra = [(ones[:, 0:128], bv[:, nb2 * NB:(nb2 + 1) * NB], [bv, ones])]
            mm_group(cx, ps[:, :], pairs, extra=extra, key=ps)
            p.op('scalar', lambda e, ti=ti, nb2=nb2, ps=ps: e.activation(
                out=V_tm[ti][:, nb2 * NB:(nb2 + 1) * NB], in_=ps[:, :], func=AF.Copy),
                reads=[ps], writes=[V_tm[ti]])
    p.barrier()
    pool.free_listed('wk', 8)
    pool.free_listed('wv', 8)
    pool.new_epoch()
    import os as _os3
    _qlim = int(_os3.environ.get('E_QLIM', '16'))
    wq = load_weight_fm(cx, cx.dram['WqT'], 8, D, 'wq')
    QT = pool.alloc_list(8, (128, 128), BF16, 'QT')
    Srow = pool.alloc((128, T), F32, 'Srow')
    Prow = pool.alloc((128, T), BF16, 'Prow')
    PT = pool.alloc_list(16, (128, 128), BF16, 'PT')
    mrow = pool.alloc((128, 4), F32, 'mrow')
    for qi in range(_qlim):
        kend = (qi + 1) * 128
        for m in range(8):
            ps = cx.psum[m % 2]
            pairs = [(wq[kc][:, m * 128:(m + 1) * 128], xn_hi[kc][:, qi * 128:(qi + 1) * 128],
                      [wq[kc], (xn_hi[kc], None)]) for kc in range(8)]
            extra = [(bq[:, m * 128:(m + 1) * 128], ones[:, 0:128], [bq, ones])]
            mm_group(cx, ps[:, 0:128], pairs, extra=extra, key=ps)
            p.op('scalar', lambda e, m=m, ps=ps: e.activation(out=QT[m], in_=ps[:, 0:128], func=AF.Copy),
                 reads=[ps], writes=[QT[m]])
        nblocks = (kend + NB - 1) // NB
        for nb in range(nblocks):
            nw = min(NB, kend - nb * NB)
            ps = cx.psum[2 + nb % 2]
            pairs = [(QT[kc], K_f[kc][:, nb * NB: nb * NB + nw], [QT[kc], K_f[kc]])
                     for kc in range(8)]
            mm_group(cx, ps[:, :nw], pairs, key=ps)
            p.op('vector', lambda e, nb=nb, nw=nw, ps=ps: e.tensor_copy(Srow[:, nb * NB: nb * NB + nw], ps[:, :nw]),
                 reads=[ps], writes=[Srow])
        p.op('vector', lambda e, qi=qi, kend=kend: e.tensor_tensor(
            Srow[:, qi * 128:kend], Srow[:, qi * 128:kend], cx.consts['trineg'], ALU.add),
            reads=[Srow, cx.consts['trineg']], writes=[Srow])
        p.op('vector', lambda e, kend=kend: e.tensor_reduce(out=mrow[:, 0:1], in_=Srow[:, :kend], axis=AX.X, op=ALU.max),
             reads=[Srow], writes=[(mrow, 'c0')])
        p.op('vector', lambda e: e.tensor_scalar_mul(mrow[:, 1:2], mrow[:, 0:1], -1.0 / 32.0),
             reads=[(mrow, 'c0')], writes=[(mrow, 'c1')])
        p.op('scalar', lambda e, kend=kend: e.activation(out=Srow[:, :kend], in_=Srow[:, :kend], func=AF.Exp,
                                                         scale=1.0 / 32.0, bias=mrow[:, 1:2], accum_out=mrow[:, 2:3]),
             reads=[Srow, (mrow, 'c1')], writes=[Srow, (mrow, 'c2')])
        p.op('vector', lambda e: e.reciprocal(mrow[:, 3:4], mrow[:, 2:3]), reads=[(mrow, 'c2')], writes=[(mrow, 'c3')])
        p.op('vector', lambda e, kend=kend: e.tensor_copy(Prow[:, :kend], Srow[:, :kend]),
             reads=[Srow], writes=[Prow])
        for kc in range(qi + 1):
            pb = cx.psum_bf[kc % 2]
            p.op('tensor', lambda e, kc=kc, pb=pb: e.transpose(pb[:, :], Prow[:, kc * 128:(kc + 1) * 128],
                                                               cx.consts['ident_bf']),
                 reads=[Prow], writes=[pb])
            p.op('scalar', lambda e, kc=kc, pb=pb: e.activation(out=PT[kc], in_=pb[:, :], func=AF.Copy),
                 reads=[pb], writes=[PT[kc]])
        for nb2 in range(2):
            ps = cx.psum[4 + nb2]
            pairs = [(PT[kc], V_tm[kc][:, nb2 * NB:(nb2 + 1) * NB], [PT[kc], V_tm[kc]])
                     for kc in range(qi + 1)]
            mm_group(cx, ps[:, :], pairs, key=ps)
            p.op('scalar', lambda e, qi=qi, nb2=nb2, ps=ps: e.activation(
                out=s1[qi][:, nb2 * NB:(nb2 + 1) * NB], in_=ps[:, :], func=AF.Copy, scale=mrow[:, 3:4]),
                reads=[ps, (mrow, 'c3')], writes=[s1[qi]])
    pool.free_listed('wq', 8)
    pool.free_listed('K_f', 8)
    pool.free_listed('V_tm', 16)
    pool.free_listed('QT', 8)
    pool.free_listed('PT', 16)
    for nm in ('bq', 'bk', 'bv', 'Srow', 'Prow', 'mrow'):
        pool.free(nm)


def emit_l1ffn(cx, xn_hi, cw1_8, y1):
    p, pool = cx.p, cx.pool
    d = cx.dram
    for md in range(8):
        p.op('vector', lambda e, md=md: e.memset(y1[md], 0.0), writes=[y1[md]])
    b2t = pool.alloc((8, D), BF16, 'l1b2t')
    p.dma('gpsimd', lambda e: e.dma_start(out=b2t, in_=d['l1b2']), writes=[b2t])
    for md in range(8):
        for n in range(T // NB):
            ps = cx.psum[5]
            mm_group(cx, ps[:, :], [(b2t[:, md * 128:(md + 1) * 128], cw1_8[:, n * NB:(n + 1) * NB],
                                     [b2t, (cw1_8, None)])], key=ps)
            p.op('vector', lambda e, md=md, n=n, ps=ps: e.tensor_tensor(
                y1[md][:, n * NB:(n + 1) * NB], y1[md][:, n * NB:(n + 1) * NB], ps[:, :], ALU.add),
                reads=[ps, y1[md]], writes=[y1[md]])
    cwb = pool.alloc((128, T), BF16, 'l1cwb')
    ab8 = pool.alloc_list(8, (128, NB), BF16, 'l1ab8')
    b1c = pool.alloc((128, 32), F32, 'l1b1c')
    wpar = []
    for par in range(2):
        wpar.append({'w1': pool.alloc_list(16, (128, 512), BF16, f'l1w1_{par}'),
                     'w2': pool.alloc_list(8, (128, 1024), BF16, f'l1w2_{par}')})

    def load_pair(par, ei, qq):
        q0 = 2 * qq
        for sub in range(2):
            for kc in range(8):
                t = wpar[par]['w1'][sub * 8 + kc]
                p.dma('gpsimd', lambda e, t=t, ei=ei, q=q0 + sub, kc=kc: e.dma_start(
                    out=t, in_=d['l1w1'][ei][kc * 128:(kc + 1) * 128, q * 512:(q + 1) * 512]), writes=[t])
            for kc in range(4):
                t = wpar[par]['w2'][sub * 4 + kc]
                p.dma('gpsimd', lambda e, t=t, ei=ei, q=q0 + sub, kc=kc: e.dma_start(
                    out=t, in_=d['l1w2'][ei][q * 512 + kc * 128: q * 512 + (kc + 1) * 128, :]), writes=[t])

    steps = [(ei, qq) for ei in range(E) for qq in range(4)]
    load_pair(0, 0, 0)
    for idx, (ei, qq) in enumerate(steps):
        par = idx % 2
        if idx + 1 < len(steps):
            ei2, qq2 = steps[idx + 1]
            load_pair(1 - par, ei2, qq2)
        if qq == 0:
            for n in range(T // NB):
                ps = cx.psum[5]
                mm_group(cx, ps[:, :], [(cx.consts['sel8_bf'][:, ei * 128:(ei + 1) * 128],
                                         cw1_8[:, n * NB:(n + 1) * NB],
                                         [cx.consts['sel8_bf'], (cw1_8, None)])], key=ps)
                p.op('scalar', lambda e, n=n, ps=ps: e.activation(out=cwb[:, n * NB:(n + 1) * NB],
                                                                  in_=ps[:, :], func=AF.Copy),
                     reads=[ps], writes=[(cwb, f'n{n}')])
            p.dma('sync', lambda e, ei=ei: e.dma_start(out=b1c, in_=d['l1b1_pp'][ei]), writes=[b1c])
        W = wpar[par]
        for n in range(T // NB):
            for mfa in range(8):
                sub, mf = mfa // 4, mfa % 4
                fa = (2 * qq + sub) * 4 + mf
                ps = cx.psum[mfa % 4]
                pairs = [(W['w1'][sub * 8 + kc][:, mf * 128:(mf + 1) * 128], xn_hi[kc][:, n * NB:(n + 1) * NB],
                          [W['w1'][sub * 8 + kc], (xn_hi[kc], None)]) for kc in range(8)]
                mm_group(cx, ps[:, :], pairs, key=ps)
                p.op('scalar', lambda e, mfa=mfa, n=n, fa=fa, ps=ps: e.activation(
                    out=ab8[mfa], in_=ps[:, :], func=AF.Gelu, bias=b1c[:, fa:fa + 1]),
                    reads=[ps, b1c], writes=[ab8[mfa]])
                p.op('vector', lambda e, mfa=mfa, n=n: e.tensor_tensor(
                    ab8[mfa], ab8[mfa], cwb[:, n * NB:(n + 1) * NB], ALU.mult),
                    reads=[ab8[mfa], (cwb, f'n{n}')], writes=[ab8[mfa]])
            for md in range(8):
                ps = cx.psum[4 + md % 2]
                pairs = [(W['w2'][kc][:, md * 128:(md + 1) * 128], ab8[kc],
                          [W['w2'][kc], ab8[kc]]) for kc in range(8)]
                mm_group(cx, ps[:, :], pairs, key=ps)
                p.op('vector', lambda e, md=md, n=n, ps=ps: e.tensor_tensor(
                    y1[md][:, n * NB:(n + 1) * NB], y1[md][:, n * NB:(n + 1) * NB], ps[:, :], ALU.add),
                    reads=[ps, y1[md]], writes=[y1[md]])
    for nm in ('l1b2t', 'l1cwb', 'l1b1c'):
        pool.free(nm)
    pool.free_listed('l1ab8', 8)
    for par in range(2):
        pool.free_listed(f'l1w1_{par}', 16)
        pool.free_listed(f'l1w2_{par}', 8)


def emit_final(cx, y1, s1_scr, x1_scr, out):
    p, pool = cx.p, cx.pool
    g1 = pool.alloc_list(8, (128, T), BF16, 'g1')
    for md in range(8):
        p.op('scalar', lambda e, md=md: e.activation(out=g1[md], in_=y1[md], func=AF.Silu),
             reads=[y1[md]], writes=[g1[md]])
    xt = pool.alloc((128, D), F32, 'f_xt')
    gt = pool.alloc((128, D), BF16, 'f_gt')
    ot = pool.alloc((128, D), F32, 'f_ot')
    s1t = pool.alloc((128, D), BF16, 'f_s1t')
    for ti in range(16):
        p.dma('sync', lambda e, ti=ti: e.dma_start(out=s1t, in_=s1_scr[ti * 128:(ti + 1) * 128, :]),
              reads=[(s1_scr, f't{ti}')], writes=[s1t])
        for md in range(8):
            pb = cx.psum_bf[md % 2]
            p.op('tensor', lambda e, md=md, ti=ti, pb=pb: e.transpose(
                pb[:, :], g1[md][:, ti * 128:(ti + 1) * 128], cx.consts['ident_bf']),
                reads=[g1[md]], writes=[pb])
            p.op('scalar', lambda e, md=md, pb=pb: e.activation(out=gt[:, md * 128:(md + 1) * 128], in_=pb[:, :], func=AF.Copy),
                 reads=[pb], writes=[(gt, f'm{md}')])
        p.dma('sync', lambda e, ti=ti: e.dma_start(out=xt, in_=x1_scr[ti * 128:(ti + 1) * 128, :]),
              reads=[(x1_scr, f't{ti}')], writes=[xt])
        greads = [(gt, f'm{m}') for m in range(8)]
        p.op('vector', lambda e: e.tensor_tensor(ot, gt, s1t, ALU.mult),
             reads=greads + [s1t], writes=[(ot, 'o')])
        p.op('vector', lambda e: e.tensor_tensor(ot, ot, xt, ALU.add), reads=[(ot, 'o'), xt], writes=[(ot, 'o')])
        p.dma('sync', lambda e, ti=ti: e.dma_start(out=out[ti * 128:(ti + 1) * 128, :], in_=ot),
              reads=[(ot, 'o')], writes=[(out, f't{ti}')])
    pool.free_listed('g1', 8)
    pool.free('f_xt'); pool.free('f_gt'); pool.free('f_ot'); pool.free('f_s1t')


def kernel(**inputs):
    g = host_prep(inputs)
    x = np.ascontiguousarray(np.asarray(inputs['L0_attn_x'], dtype=np.float32))
    cfg = {'g_shapes': {k: v.shape for k, v in g.items()}, 'stop_after': None, 'dbg': []}
    nc, _dbg = build(cfg)
    common = {'g_' + k: v for k, v in g.items()}
    in_maps = []
    for b in range(8):
        m = dict(common)
        m['x'] = np.ascontiguousarray(x[b])
        in_maps.append(m)
    res = run_bass_kernel_spmd(nc, in_maps, core_ids=list(range(8)))
    out = np.stack([res.results[b]['out'] for b in range(8)], axis=0)
    return out.astype(np.float32)


# version 04
# speedup vs baseline: 1.0055x; 1.0055x over previous
import numpy as np
import ml_dtypes
import concourse.bass as bass
import concourse.mybir as mybir
from concourse.bass_utils import run_bass_kernel_spmd
from contextlib import ExitStack

F32 = mybir.dt.float32
BF16 = mybir.dt.bfloat16
AF = mybir.ActivationFunctionType
ALU = mybir.AluOpType
AX = mybir.AxisListType

T = 2048
D = 1024
DI = 2048
DS = 16
DTR = 64
E = 8
NB = 512
LSC = 256
POOL_BYTES = 212736

ENGINES = ('tensor', 'vector', 'scalar', 'gpsimd', 'sync')
DMA_QUEUES = ('sync', 'gpsimd', 'scalar')


class Rec:
    __slots__ = ('eng', 'fn', 'deps', 'pre_drain', 'inc', 'dma', 'idx', 'val')

    def __init__(self, eng, fn, dma, idx):
        self.eng = eng
        self.fn = fn
        self.deps = []
        self.pre_drain = False
        self.inc = dma
        self.dma = dma
        self.idx = idx
        self.val = None


class P:

    def __init__(self, nc):
        self.nc = nc
        self.stack = ExitStack()
        self.recs = {e: [] for e in ENGINES}
        self.sems = {}
        for name in list(ENGINES) + [q + '_q' for q in DMA_QUEUES]:
            self.sems[name] = self.stack.enter_context(nc.semaphore('sem_' + name))
        self.buf = {}
        self.last_drain = {e: -1 for e in ENGINES}
        self.force_drain = set()
        self.seen = {}

    @staticmethod
    def _keys(items):
        out = []
        for it in items or ():
            if isinstance(it, tuple):
                h, tag = it
            else:
                h, tag = it, None
            out.append((id(h), tag))
        return out

    def _state(self, tid, tag):
        return self.buf.setdefault((tid, tag), {'w': None, 'r': []})

    def _conflicting(self, tid, tag):
        states = []
        if tag is None:
            found = False
            for (t2, _g2), st in list(self.buf.items()):
                if t2 == tid:
                    states.append(st)
                    found = True
            if not found:
                states.append(self._state(tid, None))
        else:
            states.append(self._state(tid, tag))
            if (tid, None) in self.buf:
                states.append(self.buf[(tid, None)])
        return states

    def _edge(self, rec, producer):
        if producer is None or producer is rec:
            return
        if producer.eng == rec.eng and not producer.dma and not rec.dma:
            if rec.eng == 'tensor':
                return
        producer.inc = True
        rec.deps.append(producer)

    def _wire(self, rec, reads, writes):
        if rec.eng in self.force_drain and not rec.dma:
            if rec.eng not in ('tensor',):
                rec.pre_drain = True
                self.last_drain[rec.eng] = rec.idx
            self.force_drain.discard(rec.eng)
        for tid, tag in self._keys(reads):
            for st in self._conflicting(tid, tag):
                self._edge(rec, st['w'])
        for tid, tag in self._keys(writes):
            for st in self._conflicting(tid, tag):
                self._edge(rec, st['w'])
                for r in st['r']:
                    self._edge(rec, r)
        for tid, tag in self._keys(reads):
            self._state(tid, tag)['r'].append(rec)
        for tid, tag in self._keys(writes):
            st = self._state(tid, tag)
            st['w'] = rec
            st['r'] = []

    def op(self, eng, fn, reads=(), writes=()):
        rec = Rec(eng, fn, False, len(self.recs[eng]))
        self._wire(rec, reads, writes)
        self.recs[eng].append(rec)
        return rec

    def dma(self, queue, fn, reads=(), writes=()):
        rec = Rec(queue, fn, True, len(self.recs[queue]))
        self._wire(rec, reads, writes)
        self.recs[queue].append(rec)
        return rec

    def barrier(self):
        lasts = []
        for e in ENGINES:
            for r in reversed(self.recs[e]):
                if r.fn is not None:
                    r.inc = True
                    lasts.append(r)
                    break
        for e in ENGINES:
            rec = Rec(e, None, False, len(self.recs[e]))
            rec.deps = [l for l in lasts if l.eng != e or l.dma]
            self.recs[e].append(rec)
        self.force_drain = set(('vector', 'scalar'))

    def _sem_name(self, rec):
        return rec.eng + '_q' if rec.dma else rec.eng

    def finish_and_flush(self, block):
        lasts = []
        for q in DMA_QUEUES:
            dr = [r for r in self.recs[q] if r.dma]
            if dr:
                dr[-1].inc = True
                lasts.append(dr[-1])
        rec = Rec('sync', None, False, len(self.recs['sync']))
        rec.deps = lasts
        self.recs['sync'].append(rec)

        cnt = {}
        for e in ENGINES:
            for r in self.recs[e]:
                if r.inc:
                    sn = self._sem_name(r)
                    cnt[sn] = cnt.get(sn, 0) + (16 if r.dma else 1)
                    r.val = cnt[sn]

        def emit_for(eng):
            def body(eh):
                for r in self.recs[eng]:
                    for dep in r.deps:
                        sn = self._sem_name(dep)
                        key = (eng, sn)
                        if self.seen.get(key, 0) < dep.val:
                            self.seen[key] = dep.val
                            eh.wait_ge(self.sems[sn], dep.val)
                    if r.pre_drain:
                        eh.drain()
                    if r.fn is None:
                        assert not r.inc, 'barrier rec marked inc'
                        continue
                    inst = r.fn(eh)
                    if r.inc:
                        inst.then_inc(self.sems[self._sem_name(r)], 16 if r.dma else 1)
            return body

        for eng in ENGINES:
            if self.recs[eng]:
                getattr(block, eng)(emit_for(eng))


class Pool:

    def __init__(self, nc, stack, nbytes):
        self.nw = nbytes // 4
        self.base = stack.enter_context(nc.sbuf_tensor('pool', [128, self.nw], F32))
        self.cur = 0
        self.epoch = 0
        self.free_list = []
        self.live = {}

    def alloc(self, shape, dtype, name):
        assert name not in self.live, name
        esz = {F32: 1, BF16: 2}[dtype]
        n_el = int(np.prod(shape[1:]))
        nw = (n_el + esz - 1) // esz
        nw = (nw + 15) & ~15
        off = None
        for i, (o, size, ep) in enumerate(self.free_list):
            if ep < self.epoch and size >= nw:
                off = o
                if size > nw:
                    self.free_list[i] = (o + nw, size - nw, ep)
                else:
                    self.free_list.pop(i)
                break
        if off is None:
            off = self.cur
            assert off + nw <= self.nw, f'SBUF pool overflow at {name}: {(off + nw) * 4} bytes'
            self.cur = off + nw
        self.live[name] = (off, nw)
        v = self.base[:shape[0], off:off + nw]
        if dtype == BF16:
            v = v.bitcast(BF16)
        v = v[:, :n_el]
        if len(shape) > 2:
            dims = ' '.join(f'd{i}' for i in range(1, len(shape)))
            v = v.rearrange(f'p ({dims}) -> p {dims}',
                            **{f'd{i}': shape[i] for i in range(1, len(shape))})
        return v

    def alloc_list(self, n, shape, dtype, name):
        return [self.alloc(shape, dtype, f'{name}@{i}') for i in range(n)]

    def free(self, name):
        off, nw = self.live.pop(name)
        self.free_list.append((off, nw, self.epoch))
        self.free_list.sort()
        merged = []
        for o, s, ep in self.free_list:
            if merged and merged[-1][0] + merged[-1][1] == o and merged[-1][2] == ep:
                mo, ms, mep = merged[-1]
                merged[-1] = (mo, ms + s, mep)
            else:
                merged.append((o, s, ep))
        self.free_list = merged

    def free_listed(self, name, n):
        for i in range(n):
            self.free(f'{name}@{i}')

    def new_epoch(self):
        self.epoch += 1
        old_ep = self.epoch - 1
        self.free_list = [(o, s, min(ep, old_ep)) for o, s, ep in self.free_list]
        self.free_list.sort()
        merged = []
        for o, s, ep in self.free_list:
            if merged and merged[-1][0] + merged[-1][1] == o:
                merged[-1] = (merged[-1][0], merged[-1][1] + s, old_ep)
            else:
                merged.append((o, s, old_ep))
        self.free_list = merged
        if self.free_list and self.free_list[-1][0] + self.free_list[-1][1] == self.cur:
            self.cur = self.free_list[-1][0]
            self.free_list.pop()


def host_prep(w):
    g = {}
    f32 = lambda a: np.ascontiguousarray(np.asarray(a), dtype=np.float32)
    g['w_inT'] = f32(w['L0_attn_in_proj_w'].T)
    g['conv_w_pp'] = f32(np.transpose(np.reshape(np.asarray(w['L0_attn_conv_w'])[:, 0, :], (16, 128, 4)), (1, 0, 2)).reshape(128, 64))
    g['conv_b_pp'] = f32(np.reshape(w['L0_attn_conv_b'], (16, 128)).T)
    g['xpw_dtT'] = f32(np.asarray(w['L0_attn_x_proj_w'])[:DTR].T)
    g['xpw_BT'] = f32(np.asarray(w['L0_attn_x_proj_w'])[DTR:DTR + DS].T)
    g['xpw_CT'] = f32(np.asarray(w['L0_attn_x_proj_w'])[DTR + DS:].T)
    g['dtw_T'] = f32(w['L0_attn_dt_proj_w'].T)
    g['dtb_pp'] = f32(np.reshape(w['L0_attn_dt_proj_b'], (16, 128)).T)
    g['Alog_pp'] = f32(np.transpose(np.reshape(w['L0_attn_A_log'], (16, 128, DS)), (1, 0, 2)).reshape(128, 16 * DS))
    g['D_pp'] = f32(np.reshape(w['L0_attn_D'], (16, 128)).T)
    g['w_outT'] = f32(w['L0_attn_out_proj_w'].T)
    g['gate_wT'] = f32(w['L0_ffn_gate_w'].T)
    g['gate_b_bc'] = f32(np.broadcast_to(np.asarray(w['L0_ffn_gate_b'])[None, :], (128, 8)))
    w1 = f32(w['L0_ffn_w1']); w3 = f32(w['L0_ffn_w3'])
    g['w1gT'] = np.ascontiguousarray(np.transpose(w1[:, 0::2, :], (0, 2, 1)))
    g['w1lT'] = np.ascontiguousarray(np.transpose(w1[:, 1::2, :], (0, 2, 1)))
    g['w3gT'] = np.ascontiguousarray(np.transpose(w3[:, 0::2, :], (0, 2, 1)))
    g['w3lT'] = np.ascontiguousarray(np.transpose(w3[:, 1::2, :], (0, 2, 1)))
    b1 = f32(w['L0_ffn_b1']); b3 = f32(w['L0_ffn_b3'])
    g['b1g'] = np.ascontiguousarray(b1[:, 0::2]); g['b1l'] = np.ascontiguousarray(b1[:, 1::2])
    g['b3g'] = np.ascontiguousarray(b3[:, 0::2]); g['b3l'] = np.ascontiguousarray(b3[:, 1::2])
    g['w2T'] = np.ascontiguousarray(np.transpose(f32(w['L0_ffn_w2']), (0, 2, 1))) * np.float32(1.0 / 1.702)
    g['b2'] = f32(w['L0_ffn_b2'])
    sw1 = f32(w['L0_ffn_sw1']); sw3 = f32(w['L0_ffn_sw3'])
    g['sw1gT'] = np.ascontiguousarray(sw1[0::2, :].T); g['sw1lT'] = np.ascontiguousarray(sw1[1::2, :].T)
    g['sw3gT'] = np.ascontiguousarray(sw3[0::2, :].T); g['sw3lT'] = np.ascontiguousarray(sw3[1::2, :].T)
    sb1 = f32(w['L0_ffn_sb1']); sb3 = f32(w['L0_ffn_sb3'])
    g['sb1g'] = np.ascontiguousarray(sb1[0::2])[None, :]; g['sb1l'] = np.ascontiguousarray(sb1[1::2])[None, :]
    g['sb3g'] = np.ascontiguousarray(sb3[0::2])[None, :]; g['sb3l'] = np.ascontiguousarray(sb3[1::2])[None, :]
    g['sw2T'] = f32(w['L0_ffn_sw2'].T) * np.float32(1.0 / 1.702)
    g['sb2'] = f32(w['L0_ffn_sb2'])[None, :]
    g['WqT'] = f32(w['L1_attn_Wq'].T); g['WkT'] = f32(w['L1_attn_Wk'].T); g['WvT'] = f32(w['L1_attn_Wv'].T)
    g['bq'] = f32(w['L1_attn_bq'])[None, :]; g['bk'] = f32(w['L1_attn_bk'])[None, :]; g['bv'] = f32(w['L1_attn_bv'])[None, :]
    g['w_red'] = f32(w['L1_ffn_w_red'])
    g['wg'] = f32(w['L1_ffn_wg'])
    g['l1w1'] = f32(w['L1_ffn_w1'])
    g['l1b1_pp'] = np.ascontiguousarray(np.transpose(np.reshape(f32(w['L1_ffn_b1']), (8, 32, 128)), (0, 2, 1)))
    g['l1w2'] = f32(w['L1_ffn_w2'])
    g['l1b2'] = f32(w['L1_ffn_b2'])
    g['ident'] = np.eye(128, dtype=np.float32)
    g['ones_row'] = np.ones((1, NB), np.float32)
    tri = np.tril(np.ones((128, 128), np.float32))
    g['trineg'] = (tri - 1.0) * 1e30
    sel8 = np.zeros((8, 8 * 128), np.float32)
    for e in range(8):
        sel8[e, e * 128:(e + 1) * 128] = 1.0
    g['sel8'] = sel8
    return g


class Ctx:
    def __init__(self, nc, p, pool, dram):
        self.nc = nc
        self.p = p
        self.pool = pool
        self.dram = dram
        self.psum = []
        self.psum_bf = []
        self.consts = {}


def load_const_tiles(cx):
    p, pool = cx.p, cx.pool
    for name, shape, src in [('trineg', (128, 128), 'trineg'),
                             ('gate_b_bc', (128, 8), 'gate_b_bc')]:
        t = pool.alloc(shape, F32, 'c_' + name)
        cx.consts[name] = t
        p.dma('sync', lambda e, t=t, src=src: e.dma_start(out=t, in_=cx.dram[src]), writes=[t])
    for val, nm in ((1e-5, 'eps'), (1.0, 'one')):
        t = pool.alloc((128, 1), F32, 'c_' + nm)
        cx.consts[nm] = t
        p.op('vector', lambda e, t=t, val=val: e.memset(t, val), writes=[t])
    for name, shape, src in [('ident_bf', (128, 128), 'ident'),
                             ('ones_bf', (1, NB), 'ones_row'),
                             ('sel8_bf', (8, 8 * 128), 'sel8')]:
        t = pool.alloc(shape, BF16, 'c_' + name)
        cx.consts[name] = t
        p.dma('gpsimd', lambda e, t=t, src=src: e.dma_start(out=t, in_=cx.dram[src]), writes=[t])


def load_weight_fm(cx, src_ap, kchunks, mcols, name, queue='gpsimd'):
    ts = cx.pool.alloc_list(kchunks, (128, mcols), BF16, name)
    for kc in range(kchunks):
        cx.p.dma(queue, lambda e, t=ts[kc], kc=kc: e.dma_start(
            out=t, in_=src_ap[kc * 128:(kc + 1) * 128, :]), writes=[ts[kc]])
    return ts


def mm_group(cx, psum, pairs, extra=(), key=None):
    p = cx.p
    if key is None:
        key = psum
    allr = []
    for pr in pairs:
        allr += pr[2]
    for pr in extra:
        allr += pr[2]
    n = len(pairs)
    ne = len(extra)
    for i, pr in enumerate(pairs):
        lt, rt = pr[0], pr[1]
        start = (i == 0)
        stop = (i == n - 1) and ne == 0
        p.op('tensor', lambda e, lt=lt, rt=rt, s=start, st=stop: e.matmul(psum, lt, rt, start=s, stop=st),
             reads=allr if start else (), writes=[key] if (start or stop) else ())
    for j, pr in enumerate(extra):
        lt, rt = pr[0], pr[1]
        stop = (j == ne - 1)
        p.op('tensor', lambda e, lt=lt, rt=rt, st=stop: e.matmul(psum, lt, rt, start=False, stop=st),
             reads=(), writes=[key] if stop else ())


def emit_ln(cx, p, pool, x_src_fn, xn_hi, xn_lo, bufs, ti, route_fn=None):
    xt, xsq, xnf, xh, xl, st = (bufs[k] for k in ('xt', 'xsq', 'xnf', 'xh', 'xl', 'st'))
    x_src_fn(ti)
    p.op('vector', lambda e: e.tensor_reduce(out=st[:, 0:1], in_=xt, axis=AX.X, op=ALU.add),
         reads=[(xt, 'x1')], writes=[(st, 's')])
    p.op('scalar', lambda e: e.activation(out=xsq, in_=xt, func=AF.Square, accum_out=st[:, 1:2]),
         reads=[(xt, 'x1')], writes=[xsq, (st, 'q')])
    p.op('vector', lambda e: e.tensor_scalar_mul(st[:, 2:3], st[:, 0:1], 1.0 / D), reads=[(st, 's')], writes=[(st, 'm')])
    p.op('vector', lambda e: e.tensor_scalar_mul(st[:, 3:4], st[:, 0:1], -1.0 / D), reads=[(st, 's')], writes=[(st, 'nm')])
    p.op('vector', lambda e: e.tensor_tensor(st[:, 4:5], st[:, 2:3], st[:, 2:3], ALU.mult), reads=[(st, 'm')], writes=[(st, 'm2')])
    p.op('vector', lambda e: e.scalar_tensor_tensor(out=st[:, 5:6], in0=st[:, 1:2], scalar=1.0 / D, in1=st[:, 4:5],
                                                    op0=ALU.mult, op1=ALU.subtract),
         reads=[(st, 'q'), (st, 'm2')], writes=[(st, 'v')])
    p.op('scalar', lambda e: e.activation(out=st[:, 6:7], in_=st[:, 5:6], func=AF.Ln, bias=cx.consts['eps'][:, 0:1]),
         reads=[(st, 'v'), cx.consts['eps']], writes=[(st, 'ln')])
    p.op('scalar', lambda e: e.activation(out=st[:, 7:8], in_=st[:, 6:7], func=AF.Exp, scale=-0.5),
         reads=[(st, 'ln')], writes=[(st, 'r')])
    p.op('vector', lambda e: e.scalar_tensor_tensor(out=xnf, in0=xt, scalar=st[:, 3:4],
                                                    in1=st[:, 7:8].broadcast_to([128, D]),
                                                    op0=ALU.add, op1=ALU.mult),
         reads=[(xt, 'x1'), (st, 'nm'), (st, 'r')], writes=[xnf])
    p.op('scalar', lambda e: e.activation(out=xh, in_=xnf, func=AF.Copy), reads=[xnf], writes=[xh])
    p.op('vector', lambda e: e.tensor_tensor(xl, xnf, xh, ALU.subtract), reads=[xnf, xh], writes=[xl])
    for kc in range(8):
        for (src, dstl) in ((xh, xn_hi), (xl, xn_lo)):
            pb = cx.psum_bf[kc % 2]
            p.op('tensor', lambda e, src=src, kc=kc, pb=pb: e.transpose(
                pb[:, :], src[:, kc * 128:(kc + 1) * 128], cx.consts['ident_bf']),
                reads=[src, cx.consts['ident_bf']], writes=[pb])
            p.op('scalar', lambda e, dt_=dstl[kc], ti=ti, pb=pb: e.activation(
                out=dt_[:, ti * 128:(ti + 1) * 128], in_=pb[:, :], func=AF.Copy),
                reads=[pb], writes=[(dstl[kc], f't{ti}')])
    if route_fn is not None:
        route_fn(ti, xh, xl)


def emit_l0_route(cx, ti, xn_hi, xn_lo, gwhi, gwlo, cw8, rb):
    p = cx.p
    gb = cx.consts['gate_b_bc']
    cwt, scr1, scr2 = rb['cwt'], rb['scr1'], rb['scr2']
    ps = cx.psum[4]
    pairs = []
    for kc in range(8):
        for (xa, ga) in ((xn_hi, gwhi), (xn_hi, gwlo), (xn_lo, gwhi), (xn_lo, gwlo)):
            pairs.append((xa[kc][:, ti * 128:(ti + 1) * 128], ga[:, kc, :],
                          [(xa[kc], f't{ti}'), ga]))
    mm_group(cx, ps[:, 0:8], pairs, key=ps)
    p.op('vector', lambda e: e.tensor_copy(scr1, ps[:, 0:8]), reads=[ps], writes=[scr1])
    p.op('vector', lambda e: e.tensor_reduce(out=scr2[:, 8:9], in_=scr1, axis=AX.X, op=ALU.max),
         reads=[scr1], writes=[(scr2, 'c8')])
    p.op('vector', lambda e: e.tensor_scalar_mul(scr2[:, 9:10], scr2[:, 8:9], -1.0),
         reads=[(scr2, 'c8')], writes=[(scr2, 'c9')])
    p.op('scalar', lambda e: e.activation(out=cwt, in_=scr1, func=AF.Exp, bias=scr2[:, 9:10], accum_out=scr2[:, 10:11]),
         reads=[scr1, (scr2, 'c9')], writes=[cwt, (scr2, 'c10')])
    p.op('vector', lambda e: e.reciprocal(scr2[:, 11:12], scr2[:, 10:11]), reads=[(scr2, 'c10')], writes=[(scr2, 'c11')])
    p.op('vector', lambda e: e.tensor_scalar(out=cwt, in0=cwt, scalar1=scr2[:, 11:12], scalar2=None, op0=ALU.mult),
         reads=[cwt, (scr2, 'c11')], writes=[cwt])
    p.op('vector', lambda e: e.tensor_tensor(scr1, cwt, gb, ALU.add), reads=[cwt, gb], writes=[scr1])
    p.op('vector', lambda e: e.tensor_reduce(out=scr2[:, 12:13], in_=scr1, axis=AX.X, op=ALU.max),
         reads=[scr1], writes=[(scr2, 'c12')])
    p.op('vector', lambda e: e.tensor_scalar(out=scr2[:, 0:8], in0=scr1, scalar1=scr2[:, 12:13], scalar2=None, op0=ALU.is_ge),
         reads=[scr1, (scr2, 'c12')], writes=[(scr2, 'v8')])
    p.op('vector', lambda e: e.scalar_tensor_tensor(out=scr2[:, 0:8], in0=scr2[:, 0:8], scalar=-1e30, in1=scr1,
                                                    op0=ALU.mult, op1=ALU.add),
         reads=[(scr2, 'v8'), scr1], writes=[(scr2, 'v8')])
    p.op('vector', lambda e: e.tensor_reduce(out=scr2[:, 13:14], in_=scr2[:, 0:8], axis=AX.X, op=ALU.max),
         reads=[(scr2, 'v8')], writes=[(scr2, 'c13')])
    p.op('vector', lambda e: e.tensor_scalar(out=scr2[:, 0:8], in0=scr1, scalar1=scr2[:, 13:14], scalar2=None, op0=ALU.is_ge),
         reads=[scr1, (scr2, 'c13')], writes=[(scr2, 'v8')])
    p.op('vector', lambda e: e.tensor_tensor(cwt, cwt, scr2[:, 0:8], ALU.mult),
         reads=[cwt, (scr2, 'v8')], writes=[cwt])
    p.op('vector', lambda e: e.tensor_copy(rb['cwb'], cwt), reads=[cwt], writes=[rb['cwb']])
    pb = cx.psum_bf[0]
    p.op('tensor', lambda e, pb=pb: e.transpose(pb[0:8, :], rb['cwb'], cx.consts['ident_bf']),
         reads=[rb['cwb'], cx.consts['ident_bf']], writes=[pb])
    p.op('vector', lambda e, ti=ti, pb=pb: e.tensor_copy(cw8[:, ti * 128:(ti + 1) * 128], pb[0:8, :]),
         reads=[pb], writes=[(cw8, f't{ti}')])


def build(cfg):
    nc = bass.Bass('TRN2', target_bir_lowering=False)
    stop_after = cfg.get('stop_after')
    dbg_names = set(cfg.get('dbg', ()))

    dram = {}
    for k, shape in cfg['g_shapes'].items():
        dram[k] = nc.dram_tensor('g_' + k, list(shape), F32, kind='ExternalInput').ap()
    dram['x'] = nc.dram_tensor('x', [T, D], F32, kind='ExternalInput').ap()
    out = nc.dram_tensor('out', [T, D], F32, kind='ExternalOutput').ap()

    scrb = {}
    scrb['sz'] = nc.dram_tensor('scrb_sz', [DI, T], BF16, kind='Internal').ap()
    scrb['xc'] = nc.dram_tensor('scrb_xc', [DI, T], BF16, kind='Internal').ap()
    scrb['s0'] = nc.dram_tensor('scrb_s0', [D, T], BF16, kind='Internal').ap()
    x1_scr = nc.dram_tensor('scr_x1', [T, D], F32, kind='Internal').ap()
    s1_scr = nc.dram_tensor('scr_s1', [T, D], BF16, kind='Internal').ap()

    dbg = {}

    def mkdbg(name, shape, dt=F32):
        if name in dbg_names:
            dbg[name] = nc.dram_tensor('dbg_' + name, list(shape), dt, kind='ExternalOutput').ap()
            return dbg[name]
        return None

    stk = ExitStack()
    p = P(nc)
    pool = Pool(nc, stk, POOL_BYTES)
    cx = Ctx(nc, p, pool, dram)
    for i in range(6):
        cx.psum.append(stk.enter_context(nc.psum_tensor(f'ps{i}', [128, NB], F32)))
    cx.psum_bf.append(stk.enter_context(nc.psum_tensor('psbf0', [128, 128], BF16)))
    cx.psum_bf.append(stk.enter_context(nc.psum_tensor('psbf1', [128, 128], BF16)))

    load_const_tiles(cx)

    def done():
        blk = stk.enter_context(nc.Block())
        p.finish_and_flush(blk)
        stk.close()
        p.stack.close()
        return nc, dbg

    def dump_list(lst, name, width, dt=BF16):
        dd = mkdbg(name, (128, len(lst) * width), dt)
        if dd is not None:
            for i, t in enumerate(lst):
                p.dma('sync', lambda e, t=t, i=i: e.dma_start(out=dd[:, i * width:(i + 1) * width], in_=t),
                      reads=[(t, None)])

    cw8 = pool.alloc((8, T), BF16, 'cw8')
    xn_hi = pool.alloc_list(8, (128, T), BF16, 'xn_hi')
    xn_lo = pool.alloc_list(8, (128, T), BF16, 'xn_lo')
    lb = {k: pool.alloc(s, dt, 'a1_' + k) for k, s, dt in (
        ('xt', (128, D), F32), ('xsq', (128, D), F32), ('xnf', (128, D), F32),
        ('xh', (128, D), BF16), ('xl', (128, D), BF16), ('st', (128, 16), F32))}
    rb = {'cwt': pool.alloc((128, 8), F32, 'a1_cwt'),
          'scr1': pool.alloc((128, 8), F32, 'a1_scr1'),
          'scr2': pool.alloc((128, 16), F32, 'a1_scr2'),
          'cwb': pool.alloc((128, 8), BF16, 'a1_cwbf')}
    gw32 = pool.alloc((128, 8, 8), F32, 'a1_gw32')
    p.dma('sync', lambda e: e.dma_start(out=gw32, in_=dram['gate_wT'].rearrange('(kc p) m -> p kc m', p=128)),
          writes=[gw32])
    gwhi = pool.alloc((128, 8, 8), BF16, 'a1_gwhi')
    gwlo = pool.alloc((128, 8, 8), BF16, 'a1_gwlo')
    gw32f = gw32.rearrange('p a b -> p (a b)')
    gwhif = gwhi.rearrange('p a b -> p (a b)')
    gwlof = gwlo.rearrange('p a b -> p (a b)')
    p.op('vector', lambda e: e.tensor_copy(gwhif, gw32f), reads=[gw32], writes=[gwhi])
    p.op('vector', lambda e: e.tensor_tensor(gwlof, gw32f, gwhif, ALU.subtract), reads=[gw32, gwhi], writes=[gwlo])

    def x_src(ti):
        p.dma('sync', lambda e, ti=ti: e.dma_start(out=lb['xt'], in_=dram['x'][ti * 128:(ti + 1) * 128, :]),
              writes=[(lb['xt'], 'x1')])

    for ti in range(16):
        emit_ln(cx, p, pool, x_src, xn_hi, xn_lo, lb, ti,
                route_fn=lambda ti, xh, xl: emit_l0_route(cx, ti, xn_hi, xn_lo, gwhi, gwlo, cw8, rb))
    p.barrier()
    if stop_after == 'A1':
        dump_list(xn_hi, 'xn_hi', T)
        dump_list(xn_lo, 'xn_lo', T)
        d2 = mkdbg('cw8', (8, T), BF16)
        if d2 is not None:
            p.dma('sync', lambda e: e.dma_start(out=d2, in_=cw8), reads=[(cw8, None)])
        return done()
    for k in list(lb):
        pool.free('a1_' + k)
    for k in ('cwt', 'scr1', 'scr2', 'cwbf', 'gw32', 'gwhi', 'gwlo'):
        pool.free('a1_' + k)
    pool.free_listed('xn_lo', 8)
    pool.new_epoch()

    import os as _os2
    if _os2.environ.get('A2_SKIPW', '0') == '1':
        w_in = cx.pool.alloc_list(8, (128, 4096), BF16, 'w_in')
    else:
        w_in = load_weight_fm(cx, dram['w_inT'], 8, 4096, 'w_in')
    xi = pool.alloc_list(16, (128, T + 4), BF16, 'xi')
    if _os2.environ.get('A2_SKIPMS', '0') != '1':
        for gi in range(16):
            p.op('vector', lambda e, gi=gi: e.memset(xi[gi][:, 0:4], 0.0), writes=[xi[gi]])
    sz_t = pool.alloc((128, NB), BF16, 'sz_t')
    import os as _os
    _mlim = int(_os.environ.get('A2_MLIM', '32'))
    _skipsz = _os.environ.get('A2_SKIPSZ', '0') == '1'
    for m in range(_mlim):
        for n in range(T // NB):
            ps = cx.psum[m % 2]
            pairs = [(w_in[kc][:, m * 128:(m + 1) * 128],
                      xn_hi[kc][:, n * NB:(n + 1) * NB], [w_in[kc], (xn_hi[kc], None)]) for kc in range(8)]
            mm_group(cx, ps[:, :], pairs, key=ps)
            if m < 16:
                p.op('vector', lambda e, m=m, n=n, ps=ps: e.tensor_copy(
                    xi[m][:, 3 + n * NB: 3 + n * NB + NB], ps[:, :]),
                    reads=[ps], writes=[xi[m]])
            else:
                gi = m - 16
                p.op('scalar', lambda e, ps=ps: e.activation(out=sz_t, in_=ps[:, :], func=AF.Silu),
                     reads=[ps], writes=[sz_t])
                if not _skipsz:
                    p.dma('gpsimd', lambda e, gi=gi, n=n: e.dma_start(
                        out=scrb['sz'][gi * 128:(gi + 1) * 128, n * NB:(n + 1) * NB], in_=sz_t),
                        reads=[sz_t], writes=[(scrb['sz'], f'g{gi}')])
    p.barrier()
    if stop_after == 'A2':
        dump_list(xi, 'xi', T + 4)
        return done()
    pool.free_listed('w_in', 8)
    pool.free('sz_t')
    pool.new_epoch()

    conv_w = pool.alloc((128, 16, 4), F32, 'conv_w')
    conv_b = pool.alloc((128, 16), F32, 'conv_b')
    p.dma('sync', lambda e: e.dma_start(out=conv_w.rearrange('p a b -> p (a b)'), in_=dram['conv_w_pp']), writes=[conv_w])
    p.dma('sync', lambda e: e.dma_start(out=conv_b, in_=dram['conv_b_pp']), writes=[conv_b])
    xcv = pool.alloc_list(16, (128, T), BF16, 'xcv')
    cscr = pool.alloc((128, T), F32, 'cscr')
    for gi in range(16):
        p.op('vector', lambda e, gi=gi: e.tensor_scalar(
            out=cscr, in0=xi[gi][:, 3:3 + T], scalar1=conv_w[:, gi, 3:4], scalar2=None, op0=ALU.mult),
            reads=[xi[gi], conv_w], writes=[cscr])
        for j in range(3):
            p.op('vector', lambda e, gi=gi, j=j: e.scalar_tensor_tensor(
                out=cscr, in0=xi[gi][:, j:j + T], scalar=conv_w[:, gi, j:j + 1], in1=cscr,
                op0=ALU.mult, op1=ALU.add),
                reads=[xi[gi], conv_w, cscr], writes=[cscr])
        p.op('scalar', lambda e, gi=gi: e.activation(
            out=xcv[gi], in_=cscr, func=AF.Silu, bias=conv_b[:, gi:gi + 1]),
            reads=[cscr, conv_b], writes=[xcv[gi]])
        p.dma('gpsimd', lambda e, gi=gi: e.dma_start(
            out=scrb['xc'][gi * 128:(gi + 1) * 128, :], in_=xcv[gi]),
            reads=[xcv[gi]], writes=[(scrb['xc'], f'g{gi}')])
    p.barrier()
    pool.free_listed('xi', 16)
    pool.free('cscr')
    pool.new_epoch()
    xpw_dt = load_weight_fm(cx, dram['xpw_dtT'], 16, 64, 'xpw_dt')
    xpw_B = load_weight_fm(cx, dram['xpw_BT'], 16, 16, 'xpw_B')
    xpw_C = load_weight_fm(cx, dram['xpw_CT'], 16, 16, 'xpw_C')
    dtr_f = pool.alloc((64, T), BF16, 'dtr_f')
    B_f = pool.alloc((16, T), BF16, 'B_f')
    C_f = pool.alloc((16, T), BF16, 'C_f')
    for n in range(T // NB):
        for (wt, rows, dst) in ((xpw_dt, 64, dtr_f), (xpw_B, 16, B_f), (xpw_C, 16, C_f)):
            ps = cx.psum[3]
            pairs = [(wt[kc][:, :rows], xcv[kc][:, n * NB:(n + 1) * NB], [wt[kc], xcv[kc]]) for kc in range(16)]
            mm_group(cx, ps[:rows, :], pairs, key=ps)
            p.op('vector', lambda e, ps=ps, dst=dst, rows=rows, n=n: e.tensor_copy(
                dst[:rows, n * NB:(n + 1) * NB], ps[:rows, :]),
                reads=[ps], writes=[(dst, f'n{n}')])
    p.barrier()
    pool.free_listed('xcv', 16)
    pool.free_listed('xpw_dt', 16); pool.free_listed('xpw_B', 16); pool.free_listed('xpw_C', 16)
    pool.free('conv_w'); pool.free('conv_b')
    pool.new_epoch()
    if stop_after == 'A3':
        for nm, tt, rows in (('dtr_f', dtr_f, 64), ('B_f', B_f, 16), ('C_f', C_f, 16)):
            dd = mkdbg(nm, (rows, T), BF16)
            if dd is not None:
                p.dma('sync', lambda e, dd=dd, tt=tt, rows=rows: e.dma_start(out=dd, in_=tt[:rows, :]), reads=[(tt, None)])
        return done()

    emit_scan(cx, dtr_f, B_f, C_f, scrb)
    p.barrier()
    for nm in ('dtr_f', 'B_f', 'C_f'):
        pool.free(nm)
    pool.new_epoch()
    if stop_after == 'B':
        dd = mkdbg('s0', (D, T), BF16)
        if dd is not None:
            sload = pool.alloc((128, T), BF16, 'dbg_s0')
            for md in range(8):
                p.dma('sync', lambda e, md=md: e.dma_start(out=sload, in_=scrb['s0'][md * 128:(md + 1) * 128, :]),
                      reads=[(scrb['s0'], None)], writes=[sload])
                p.dma('sync', lambda e, md=md: e.dma_start(out=dd[md * 128:(md + 1) * 128, :], in_=sload),
                      reads=[sload], writes=[(dd, f'm{md}')])
        return done()

    y_acc = pool.alloc_list(8, (128, T), F32, 'y_acc')
    emit_l0ffn(cx, xn_hi, cw8, y_acc)
    p.barrier()
    pool.new_epoch()
    if stop_after == 'C':
        dump_list(y_acc, 'y_acc', T, F32)
        return done()

    cw1_8 = pool.alloc((8, T), BF16, 'cw1_8')
    emit_combine_ln(cx, y_acc, scrb['s0'], xn_hi, cw1_8, x1_scr)
    p.barrier()
    pool.new_epoch()
    if stop_after == 'D':
        dump_list(xn_hi, 'xn1_hi', T)
        d2 = mkdbg('cw1_8', (8, T), BF16)
        if d2 is not None:
            p.dma('sync', lambda e: e.dma_start(out=d2, in_=cw1_8), reads=[(cw1_8, None)])
        return done()

    s1 = pool.alloc_list(16, (128, D), BF16, 's1')
    emit_l1attn(cx, xn_hi, s1)
    for ti in range(16):
        p.dma('gpsimd', lambda e, ti=ti: e.dma_start(out=s1_scr[ti * 128:(ti + 1) * 128, :], in_=s1[ti]),
              reads=[s1[ti]], writes=[(s1_scr, f't{ti}')])
    p.barrier()
    if stop_after == 'E':
        dump_list(s1, 's1', D)
        return done()
    pool.free_listed('s1', 16)
    pool.new_epoch()

    y1 = pool.alloc_list(8, (128, T), F32, 'y1')
    emit_l1ffn(cx, xn_hi, cw1_8, y1)
    p.barrier()
    pool.new_epoch()

    emit_final(cx, y1, s1_scr, x1_scr, out)
    return done()


def emit_scan(cx, dtr_f, B_f, C_f, scrb):
    p, pool = cx.p, cx.pool
    L = LSC
    ones = cx.consts['ones_bf']
    dtw_t = pool.alloc((64, DI), BF16, 'dtw_t')
    p.dma('gpsimd', lambda e: e.dma_start(out=dtw_t, in_=cx.dram['dtw_T']), writes=[dtw_t])
    dtb = pool.alloc((128, 16), F32, 'dtb')
    p.dma('sync', lambda e: e.dma_start(out=dtb, in_=cx.dram['dtb_pp']), writes=[dtb])
    alog = pool.alloc((128, 16 * DS), F32, 'alog')
    p.dma('sync', lambda e: e.dma_start(out=alog, in_=cx.dram['Alog_pp']), writes=[alog])
    A_pp = pool.alloc((128, 16 * DS), F32, 'A_pp')
    p.op('scalar', lambda e: e.activation(out=A_pp, in_=alog, func=AF.Exp), reads=[alog], writes=[(A_pp, 'e')])
    p.op('vector', lambda e: e.tensor_scalar_mul(A_pp, A_pp, -1.0), reads=[(A_pp, 'e')], writes=[A_pp])
    D_pp = pool.alloc((128, 16), F32, 'D_pp')
    p.dma('sync', lambda e: e.dma_start(out=D_pp, in_=cx.dram['D_pp']), writes=[D_pp])
    w_out = load_weight_fm(cx, cx.dram['w_outT'], 16, D, 'w_out')

    dA = pool.alloc((128, DS * L), F32, 'dA')
    u = pool.alloc((128, DS * L), F32, 'u')
    h = pool.alloc((128, DS * L), F32, 'h')
    dtg = pool.alloc((128, L), F32, 'dtg')
    dtx = pool.alloc((128, L), F32, 'dtx')
    spx = pool.alloc((128, L), F32, 'spx')
    yg = pool.alloc((128, L), F32, 'yg')
    hc = pool.alloc((128, 16, DS), F32, 'hc')
    xcg = pool.alloc_list(16, (128, L), BF16, 'xcg')
    szg = pool.alloc_list(16, (128, L), BF16, 'szg')
    ygate = pool.alloc_list(16, (128, L), BF16, 'ygate')
    bflat = pool.alloc((1, DS * L), BF16, 'bflat')
    cflat = pool.alloc((1, DS * L), BF16, 'cflat')
    B_rep = pool.alloc((128, DS * L), BF16, 'B_rep')
    C_rep = pool.alloc((128, DS * L), BF16, 'C_rep')
    sout = pool.alloc((128, L), BF16, 'sout')
    for gi in range(16):
        p.op('vector', lambda e, gi=gi: e.memset(hc[:, gi, :], 0.0), writes=[(hc, f'g{gi}')])

    NCH = T // L
    for tc in range(NCH):
        t0 = tc * L
        p.dma('sync', lambda e, t0=t0: e.dma_start(out=bflat, in_=B_f[:, t0:t0 + L]),
              reads=[(B_f, None)], writes=[bflat])
        p.dma('sync', lambda e, t0=t0: e.dma_start(out=cflat, in_=C_f[:, t0:t0 + L]),
              reads=[(C_f, None)], writes=[cflat])
        for (flat, rep) in ((bflat, B_rep), (cflat, C_rep)):
            for nb in range(DS * L // NB):
                ps = cx.psum[5]
                mm_group(cx, ps[:, :], [(ones[:, 0:128], flat[:, nb * NB:(nb + 1) * NB], [flat, ones])], key=ps)
                p.op('scalar', lambda e, ps=ps, rep=rep, nb=nb: e.activation(
                    out=rep[:, nb * NB:(nb + 1) * NB], in_=ps[:, :], func=AF.Copy),
                    reads=[ps], writes=[(rep, f'n{nb}')])
        rep_reads_B = [(B_rep, f'n{nb}') for nb in range(DS * L // NB)]
        rep_reads_C = [(C_rep, f'n{nb}') for nb in range(DS * L // NB)]
        for gi in range(16):
            p.dma('sync', lambda e, gi=gi, t0=t0: e.dma_start(
                out=xcg[gi], in_=scrb['xc'][gi * 128:(gi + 1) * 128, t0:t0 + L]),
                reads=[(scrb['xc'], f'g{gi}')], writes=[xcg[gi]])
            p.dma('sync', lambda e, gi=gi, t0=t0: e.dma_start(
                out=szg[gi], in_=scrb['sz'][gi * 128:(gi + 1) * 128, t0:t0 + L]),
                reads=[(scrb['sz'], f'g{gi}')], writes=[szg[gi]])
        for gi in range(16):
            ps = cx.psum[4]
            mm_group(cx, ps[:, :L], [(dtw_t[:, gi * 128:(gi + 1) * 128], dtr_f[:, t0:t0 + L],
                                      [dtw_t, (dtr_f, None)])], key=ps)
            p.op('scalar', lambda e, gi=gi, ps=ps: e.activation(out=spx, in_=ps[:, :L], func=AF.Abs, bias=dtb[:, gi:gi + 1]),
                 reads=[ps, dtb], writes=[spx])
            p.op('scalar', lambda e: e.activation(out=spx, in_=spx, func=AF.Exp, scale=-1.0),
                 reads=[spx], writes=[spx])
            p.op('scalar', lambda e: e.activation(out=spx, in_=spx, func=AF.Ln, bias=cx.consts['one'][:, 0:1]),
                 reads=[spx, cx.consts['one']], writes=[spx])
            p.op('scalar', lambda e, gi=gi, ps=ps: e.activation(out=dtg, in_=ps[:, :L], func=AF.Relu, bias=dtb[:, gi:gi + 1]),
                 reads=[ps, dtb], writes=[dtg])
            p.op('vector', lambda e: e.tensor_tensor(dtg, dtg, spx, ALU.add),
                 reads=[dtg, spx], writes=[dtg])
            p.op('vector', lambda e, gi=gi: e.tensor_tensor(dtx, dtg, xcg[gi], ALU.mult),
                 reads=[dtg, xcg[gi]], writes=[dtx])
            for s in range(DS):
                p.op('scalar', lambda e, gi=gi, s=s: e.activation(
                    out=dA[:, s * L:(s + 1) * L], in_=dtg, func=AF.Exp, scale=A_pp[:, gi * DS + s: gi * DS + s + 1]),
                    reads=[dtg, A_pp], writes=[(dA, f's{s}')])
            dA_reads = [(dA, f's{s}') for s in range(DS)]
            H = DS // 4
            for hs in range(4):
                sl = slice(hs * H * L, (hs + 1) * H * L)
                p.op('vector', lambda e, sl=sl: e.tensor_tensor(
                    u[:, sl].rearrange('p (s l) -> p s l', s=H),
                    dtx[:, None, :].broadcast_to([128, H, L]),
                    B_rep[:, sl].rearrange('p (s l) -> p s l', s=H),
                    ALU.mult),
                    reads=[dtx] + rep_reads_B, writes=[(u, f'h{hs}')])
            if tc > 0:
                p.op('vector', lambda e, gi=gi: e.tensor_tensor(dtx[:, 0:DS], dA[:, 0::L], hc[:, gi, :], ALU.mult),
                     reads=dA_reads + [(hc, f'g{gi}')], writes=[dtx])
                p.op('vector', lambda e: e.tensor_tensor(u[:, 0::L], dtx[:, 0:DS], u[:, 0::L], ALU.add),
                     reads=[dtx, (u, None)], writes=[(u, None)])
            p.op('vector', lambda e: e.memset(dA[:, 0::L], 0.0), writes=[(dA, None)])
            for hs in range(4):
                sl = slice(hs * H * L, (hs + 1) * H * L)
                dtags = [(dA, f's{s}') for s in range(hs * H, (hs + 1) * H)]
                p.op('vector', lambda e, sl=sl: e.tensor_tensor_scan(
                    h[:, sl], dA[:, sl], u[:, sl], 0.0, ALU.mult, ALU.add),
                    reads=dtags + [(u, f'h{hs}')], writes=[(h, f'h{hs}')])
            p.op('vector', lambda e, gi=gi: e.tensor_copy(hc[:, gi, :], h[:, L - 1::L]),
                 reads=[(h, f'h{q}') for q in range(4)], writes=[(hc, f'g{gi}')])
            for hs in range(4):
                sl = slice(hs * H * L, (hs + 1) * H * L)
                wtags = [(dA, f's{s}') for s in range(hs * H, (hs + 1) * H)]
                p.op('vector', lambda e, sl=sl: e.tensor_tensor(
                    dA[:, sl].rearrange('p (s l) -> p s l', s=H),
                    h[:, sl].rearrange('p (s l) -> p s l', s=H),
                    C_rep[:, sl].rearrange('p (s l) -> p s l', s=H),
                    ALU.mult),
                    reads=[(h, f'h{hs}')] + rep_reads_C, writes=wtags)
            for w in (8, 4, 2, 1):
                rtags = [(dA, f's{s}') for s in range(2 * w)]
                wt = [(dA, f's{s}') for s in range(w)]
                p.op('vector', lambda e, w=w: e.tensor_tensor(
                    dA[:, :w * L], dA[:, :w * L], dA[:, w * L:2 * w * L], ALU.add),
                    reads=rtags, writes=wt)
            p.op('vector', lambda e, gi=gi: e.scalar_tensor_tensor(
                out=yg, in0=xcg[gi], scalar=D_pp[:, gi:gi + 1], in1=dA[:, :L], op0=ALU.mult, op1=ALU.add),
                reads=[xcg[gi], D_pp, (dA, 's0')], writes=[yg])
            p.op('vector', lambda e, gi=gi: e.tensor_tensor(ygate[gi], yg, szg[gi], ALU.mult),
                 reads=[yg, szg[gi]], writes=[ygate[gi]])
        for md in range(8):
            ps = cx.psum[5]
            pairs = [(w_out[kc][:, md * 128:(md + 1) * 128], ygate[kc], [w_out[kc], ygate[kc]])
                     for kc in range(16)]
            mm_group(cx, ps[:, :L], pairs, key=ps)
            p.op('scalar', lambda e, ps=ps: e.activation(out=sout, in_=ps[:, :L], func=AF.Copy),
                 reads=[ps], writes=[sout])
            p.dma('gpsimd', lambda e, md=md, t0=t0: e.dma_start(
                out=scrb['s0'][md * 128:(md + 1) * 128, t0:t0 + L], in_=sout),
                reads=[sout], writes=[(scrb['s0'], f'm{md}c{t0}')])
    for nm in ('dtw_t', 'dtb', 'alog', 'A_pp', 'D_pp', 'dA', 'u', 'h', 'dtg', 'dtx',
               'spx', 'yg', 'hc', 'bflat', 'cflat', 'B_rep', 'C_rep', 'sout'):
        pool.free(nm)
    pool.free_listed('w_out', 16)
    for nm in ('xcg', 'szg', 'ygate'):
        pool.free_listed(nm, 16)


def emit_l0ffn(cx, xn_hi, cw8, y_acc):
    p, pool = cx.p, cx.pool
    ones = cx.consts['ones_bf']
    d = cx.dram
    for md in range(8):
        p.op('vector', lambda e, md=md: e.memset(y_acc[md], 0.0), writes=[y_acc[md]])
    b2t = pool.alloc((8, D), BF16, 'b2t')
    p.dma('gpsimd', lambda e: e.dma_start(out=b2t, in_=d['b2']), writes=[b2t])
    for md in range(8):
        for n in range(T // NB):
            ps = cx.psum[5]
            mm_group(cx, ps[:, :], [(b2t[:, md * 128:(md + 1) * 128], cw8[:, n * NB:(n + 1) * NB],
                                     [b2t, (cw8, None)])], key=ps)
            p.op('vector', lambda e, md=md, n=n, ps=ps: e.tensor_tensor(
                y_acc[md][:, n * NB:(n + 1) * NB], y_acc[md][:, n * NB:(n + 1) * NB], ps[:, :], ALU.add),
                reads=[ps, y_acc[md]], writes=[y_acc[md]])
    cwb = pool.alloc((128, NB), BF16, 'cwb')
    ev = {k: pool.alloc((128, NB), F32, 'l0_' + k) for k in ('hg', 'h3g', 'hl', 'h3l')}
    ev['sg'] = ev['h3g']
    ab4p = [pool.alloc_list(4, (128, NB), BF16, f'ab4_{pr}') for pr in range(2)]
    wpar = []
    for par in range(2):
        wpar.append({m: pool.alloc_list(8, (128, 512), BF16, f'l0w_{par}_{m}')
                     for m in ('w1g', 'w1l', 'w3g', 'w3l')})
    w2bp = [pool.alloc_list(4, (128, 1024), BF16, f'l0w2_{pr}') for pr in range(2)]
    bgs4 = pool.alloc((1, 4 * 512), BF16, 'l0bgs')
    brow = {'bg1': 0, 'bl1': 1, 'bg3': 2, 'bl3': 3}
    bgs = {m: bgs4[0:1, r * 512:(r + 1) * 512] for m, r in brow.items()}

    def load_half(par, wsrcs, hh):
        for mi, m in enumerate(('w1g', 'w1l', 'w3g', 'w3l')):
            for kc in range(8):
                t = wpar[par][m][kc]
                p.dma('gpsimd', lambda e, t=t, src=wsrcs[mi], hh=hh, kc=kc: e.dma_start(
                    out=t, in_=src[kc * 128:(kc + 1) * 128, hh * 512:(hh + 1) * 512]), writes=[t])

    def load_w2(par, w2src, hh):
        for kc in range(4):
            t = w2bp[par][kc]
            p.dma('gpsimd', lambda e, t=t, src=w2src, hh=hh, kc=kc: e.dma_start(
                out=t, in_=src[hh * 512 + kc * 128: hh * 512 + (kc + 1) * 128, :]), writes=[t])

    passes = []
    for ei in range(E):
        passes.append((ei, (d['w1gT'][ei], d['w1lT'][ei], d['w3gT'][ei], d['w3lT'][ei]),
                       (d['b1g'][ei:ei + 1], d['b1l'][ei:ei + 1], d['b3g'][ei:ei + 1], d['b3l'][ei:ei + 1]),
                       d['w2T'][ei], True))
    passes.append((0, (d['sw1gT'], d['sw1lT'], d['sw3gT'], d['sw3lT']),
                   (d['sb1g'], d['sb1l'], d['sb3g'], d['sb3l']), d['sw2T'], False))

    halves = [(pi, hh) for pi in range(len(passes)) for hh in range(2)]
    load_half(0, passes[0][1], 0)
    abi = 0
    pending = None

    def emit_w2(w2_par, abp, n):
        for md in range(8):
            ps = cx.psum[4 + md % 2]
            pairs = [(w2bp[w2_par][kc][:, md * 128:(md + 1) * 128], ab4p[abp][kc],
                      [w2bp[w2_par][kc], ab4p[abp][kc]]) for kc in range(4)]
            mm_group(cx, ps[:, :], pairs, key=ps)
            p.op('vector', lambda e, md=md, n=n, ps=ps: e.tensor_tensor(
                y_acc[md][:, n * NB:(n + 1) * NB], y_acc[md][:, n * NB:(n + 1) * NB], ps[:, :], ALU.add),
                reads=[ps, y_acc[md]], writes=[y_acc[md]])

    for idx, (pi, hh) in enumerate(halves):
        ei, wsrcs, bsrcs, w2src, scaled = passes[pi]
        par = idx % 2
        if idx + 1 < len(halves):
            pi2, hh2 = halves[idx + 1]
            load_half(1 - par, passes[pi2][1], hh2)
        load_w2(par, w2src, hh)
        for m, srcb in (('bg1', bsrcs[0]), ('bl1', bsrcs[1]), ('bg3', bsrcs[2]), ('bl3', bsrcs[3])):
            p.dma('gpsimd', lambda e, t=bgs[m], src=srcb, hh=hh: e.dma_start(
                out=t, in_=src[:, hh * 512:(hh + 1) * 512]), writes=[bgs[m]])
        W = wpar[par]
        for n in range(T // NB):
            if scaled:
                ps = cx.psum[5]
                mm_group(cx, ps[:, :], [(cx.consts['sel8_bf'][:, ei * 128:(ei + 1) * 128],
                                         cw8[:, n * NB:(n + 1) * NB], [cx.consts['sel8_bf'], (cw8, None)])], key=ps)
                p.op('scalar', lambda e, ps=ps: e.activation(out=cwb, in_=ps[:, :], func=AF.Copy),
                     reads=[ps], writes=[cwb])
            for mf in range(4):
                fa = hh * 512 + mf * 128
                for ci, (m, bm) in enumerate((('w1g', 'bg1'), ('w3g', 'bg3'), ('w1l', 'bl1'), ('w3l', 'bl3'))):
                    psb = cx.psum[ci]
                    pairs = [(W[m][kc][:, mf * 128:(mf + 1) * 128], xn_hi[kc][:, n * NB:(n + 1) * NB],
                              [W[m][kc], (xn_hi[kc], None)]) for kc in range(8)]
                    extra = [(bgs[bm][:, mf * 128:(mf + 1) * 128], ones[:, :NB], [bgs[bm], ones])]
                    mm_group(cx, psb[:, :], pairs, extra=extra, key=psb)
                p.op('scalar', lambda e: e.activation(out=ev['hg'], in_=cx.psum[0][:, :], func=AF.Copy),
                     reads=[cx.psum[0]], writes=[ev['hg']])
                p.op('scalar', lambda e: e.activation(out=ev['h3g'], in_=cx.psum[1][:, :], func=AF.Copy),
                     reads=[cx.psum[1]], writes=[ev['h3g']])
                p.op('scalar', lambda e: e.activation(out=ev['hl'], in_=cx.psum[2][:, :], func=AF.Copy),
                     reads=[cx.psum[2]], writes=[ev['hl']])
                p.op('vector', lambda e: e.tensor_copy(ev['h3l'], cx.psum[3][:, :]),
                     reads=[cx.psum[3]], writes=[ev['h3l']])
                p.op('vector', lambda e: e.tensor_tensor(ev['hg'], ev['hg'], ev['h3g'], ALU.mult),
                     reads=[ev['hg'], ev['h3g']], writes=[ev['hg']])
                p.op('vector', lambda e: e.tensor_tensor(ev['hl'], ev['hl'], ev['h3l'], ALU.mult),
                     reads=[ev['hl'], ev['h3l']], writes=[ev['hl']])
                p.op('vector', lambda e: e.tensor_scalar(out=ev['hg'], in0=ev['hg'], scalar1=7.0, scalar2=None, op0=ALU.min),
                     reads=[ev['hg']], writes=[ev['hg']])
                p.op('scalar', lambda e: e.activation(out=ev['sg'], in_=ev['hg'], func=AF.Silu, scale=1.702),
                     reads=[ev['hg']], writes=[ev['sg']])
                p.op('vector', lambda e: e.tensor_scalar(out=ev['hl'], in0=ev['hl'], scalar1=7.0, scalar2=-7.0,
                                                         op0=ALU.min, op1=ALU.max),
                     reads=[ev['hl']], writes=[ev['hl']])
                p.op('vector', lambda e: e.scalar_tensor_tensor(out=ev['hl'], in0=ev['hl'], scalar=1.0, in1=ev['sg'],
                                                                op0=ALU.add, op1=ALU.mult),
                     reads=[ev['hl'], ev['sg']], writes=[ev['hl']])
                if scaled:
                    p.op('vector', lambda e, mf=mf, ab=ab4p[abi % 2]: e.tensor_tensor(
                        ab[mf], ev['hl'], cwb, ALU.mult),
                        reads=[ev['hl'], cwb], writes=[ab4p[abi % 2][mf]])
                else:
                    p.op('vector', lambda e, mf=mf, ab=ab4p[abi % 2]: e.tensor_copy(ab[mf], ev['hl']),
                         reads=[ev['hl']], writes=[ab4p[abi % 2][mf]])
            if pending is not None:
                emit_w2(*pending)
            pending = (par, abi % 2, n)
            abi += 1
    if pending is not None:
        emit_w2(*pending)
    sb2t = b2t[0:1, :]
    p.dma('gpsimd', lambda e: e.dma_start(out=sb2t, in_=d['sb2']),
          reads=[(b2t, None)], writes=[(b2t, 'sb2')])
    for md in range(8):
        for n in range(T // NB):
            ps = cx.psum[5]
            mm_group(cx, ps[:, :], [(sb2t[:, md * 128:(md + 1) * 128], ones[:, :NB], [(b2t, 'sb2'), ones])], key=ps)
            p.op('vector', lambda e, md=md, n=n, ps=ps: e.tensor_tensor(
                y_acc[md][:, n * NB:(n + 1) * NB], y_acc[md][:, n * NB:(n + 1) * NB], ps[:, :], ALU.add),
                reads=[ps, y_acc[md]], writes=[y_acc[md]])
    for nm in ('b2t', 'cwb'):
        pool.free(nm)
    for k in ('hg', 'h3g', 'hl', 'h3l'):
        pool.free('l0_' + k)
    for pr in range(2):
        pool.free_listed(f'ab4_{pr}', 4)
        pool.free_listed(f'l0w2_{pr}', 4)
    for par in range(2):
        for m in ('w1g', 'w1l', 'w3g', 'w3l'):
            pool.free_listed(f'l0w_{par}_{m}', 8)
    pool.free('l0bgs')


def emit_combine_ln(cx, y_acc, s0_scr, xn_hi, cw1_8, x1_scr):
    p, pool = cx.p, cx.pool
    gbf = pool.alloc_list(8, (128, T), BF16, 'gbf')
    for md in range(8):
        p.op('scalar', lambda e, md=md: e.activation(out=gbf[md], in_=y_acc[md], func=AF.Silu),
             reads=[y_acc[md]], writes=[gbf[md]])
    p.barrier()
    pool.free_listed('y_acc', 8)
    pool.new_epoch()
    s0f = pool.alloc_list(8, (128, T), BF16, 's0f')
    for md in range(8):
        p.dma('sync', lambda e, md=md: e.dma_start(out=s0f[md], in_=s0_scr[md * 128:(md + 1) * 128, :]),
              reads=[(s0_scr, None)], writes=[s0f[md]])
    xn_lo = pool.alloc_list(8, (128, T), BF16, 'xn_lo1')
    wred32 = pool.alloc((128, 8, 16), F32, 'wred32')
    p.dma('sync', lambda e: e.dma_start(out=wred32, in_=cx.dram['w_red'].rearrange('(kc p) m -> p kc m', p=128)),
          writes=[wred32])
    wredh = pool.alloc((128, 8, 16), BF16, 'wredh')
    wredl = pool.alloc((128, 8, 16), BF16, 'wredl')
    wr32f = wred32.rearrange('p a b -> p (a b)')
    wrhf = wredh.rearrange('p a b -> p (a b)')
    wrlf = wredl.rearrange('p a b -> p (a b)')
    p.op('vector', lambda e: e.tensor_copy(wrhf, wr32f), reads=[wred32], writes=[wredh])
    p.op('vector', lambda e: e.tensor_tensor(wrlf, wr32f, wrhf, ALU.subtract), reads=[wred32, wredh], writes=[wredl])
    wgt = pool.alloc((8, 16), F32, 'wgt')
    p.dma('sync', lambda e: e.dma_start(out=wgt, in_=cx.dram['wg']), writes=[wgt])
    wsc = pool.alloc((8, 4), F32, 'wsc')
    wsq = pool.alloc((8, 16), F32, 'wsq')
    p.op('scalar', lambda e: e.activation(out=wsq, in_=wgt, func=AF.Square, accum_out=wsc[:, 0:1]),
         reads=[wgt], writes=[wsq, (wsc, 'c0')])
    p.op('scalar', lambda e: e.activation(out=wsc[:, 1:2], in_=wsc[:, 0:1], func=AF.Sqrt),
         reads=[(wsc, 'c0')], writes=[(wsc, 'c1')])
    p.op('vector', lambda e: e.tensor_scalar(out=wsc[:, 1:2], in0=wsc[:, 1:2], scalar1=1e-4, scalar2=None, op0=ALU.max),
         reads=[(wsc, 'c1')], writes=[(wsc, 'c1')])
    p.op('vector', lambda e: e.reciprocal(wsc[:, 2:3], wsc[:, 1:2]), reads=[(wsc, 'c1')], writes=[(wsc, 'c2')])
    p.op('vector', lambda e: e.tensor_scalar(out=wgt, in0=wgt, scalar1=wsc[:, 2:3], scalar2=None, op0=ALU.mult),
         reads=[wgt, (wsc, 'c2')], writes=[wgt])
    wgh = pool.alloc((8, 16), BF16, 'wgh')
    wgl = pool.alloc((8, 16), BF16, 'wgl')
    p.op('vector', lambda e: e.tensor_copy(wgh, wgt), reads=[wgt], writes=[wgh])
    p.op('vector', lambda e: e.tensor_tensor(wgl, wgt, wgh, ALU.subtract), reads=[wgt, wgh], writes=[wgl])
    wghT = pool.alloc((16, 8), BF16, 'wghT')
    wglT = pool.alloc((16, 8), BF16, 'wglT')
    for (src, dst) in ((wgh, wghT), (wgl, wglT)):
        pb = cx.psum_bf[1]
        p.op('tensor', lambda e, src=src, pb=pb: e.transpose(pb[0:16, 0:8], src, cx.consts['ident_bf'][0:8, 0:8]),
             reads=[src, cx.consts['ident_bf']], writes=[pb])
        p.op('vector', lambda e, dst=dst, pb=pb: e.tensor_copy(dst, pb[0:16, 0:8]), reads=[pb], writes=[dst])

    lb = {k: pool.alloc(s, dt, 'd_' + k) for k, s, dt in (
        ('xt', (128, D), F32), ('xsq', (128, D), F32), ('xnf', (128, D), F32),
        ('xh', (128, D), BF16), ('xl', (128, D), BF16), ('st', (128, 16), F32))}
    gt = pool.alloc((128, D), BF16, 'd_gt')
    smt = pool.alloc((128, D), BF16, 'd_smt')
    redt = pool.alloc((128, 16), F32, 'd_redt')
    redh = pool.alloc((128, 16), BF16, 'd_redh')
    redl = pool.alloc((128, 16), BF16, 'd_redl')
    rht = pool.alloc((16, 128), BF16, 'd_rht')
    rlt = pool.alloc((16, 128), BF16, 'd_rlt')
    lg = pool.alloc((128, 8), F32, 'd_lg')
    r1 = pool.alloc((128, 16), F32, 'd_r1')
    r2 = pool.alloc((128, 8), F32, 'd_r2')
    cwbf = pool.alloc((128, 8), BF16, 'd_cwbf')

    def x_src(ti):
        for md in range(8):
            pb = cx.psum_bf[md % 2]
            p.op('tensor', lambda e, md=md, ti=ti, pb=pb: e.transpose(
                pb[:, :], gbf[md][:, ti * 128:(ti + 1) * 128], cx.consts['ident_bf']),
                reads=[gbf[md]], writes=[pb])
            p.op('scalar', lambda e, md=md, pb=pb: e.activation(out=gt[:, md * 128:(md + 1) * 128], in_=pb[:, :], func=AF.Copy),
                 reads=[pb], writes=[(gt, f'm{md}')])
            pb2 = cx.psum_bf[(md + 1) % 2]
            p.op('tensor', lambda e, md=md, ti=ti, pb2=pb2: e.transpose(
                pb2[:, :], s0f[md][:, ti * 128:(ti + 1) * 128], cx.consts['ident_bf']),
                reads=[s0f[md]], writes=[pb2])
            p.op('scalar', lambda e, md=md, pb2=pb2: e.activation(out=smt[:, md * 128:(md + 1) * 128], in_=pb2[:, :], func=AF.Copy),
                 reads=[pb2], writes=[(smt, f'm{md}')])
        p.dma('sync', lambda e, ti=ti: e.dma_start(out=lb['xnf'], in_=cx.dram['x'][ti * 128:(ti + 1) * 128, :]),
              writes=[(lb['xnf'], 'x0')])
        greads = [(gt, f'm{m}') for m in range(8)] + [(smt, f'm{m}') for m in range(8)]
        p.op('vector', lambda e: e.tensor_tensor(lb['xt'], gt, smt, ALU.mult), reads=greads, writes=[(lb['xt'], 'x1')])
        p.op('vector', lambda e: e.tensor_tensor(lb['xt'], lb['xt'], lb['xnf'], ALU.add),
             reads=[(lb['xt'], 'x1'), (lb['xnf'], 'x0')], writes=[(lb['xt'], 'x1')])
        p.dma('sync', lambda e, ti=ti: e.dma_start(out=x1_scr[ti * 128:(ti + 1) * 128, :], in_=lb['xt']),
              reads=[(lb['xt'], 'x1')], writes=[(x1_scr, f't{ti}')])

    def route1(ti, xh, xl):
        ps = cx.psum[4]
        pairs = []
        for kc in range(8):
            for (xa, ga) in ((xn_hi, wredh), (xn_hi, wredl), (xn_lo, wredh), (xn_lo, wredl)):
                pairs.append((xa[kc][:, ti * 128:(ti + 1) * 128], ga[:, kc, :],
                              [(xa[kc], f't{ti}'), ga]))
        mm_group(cx, ps[:, 0:16], pairs, key=ps)
        p.op('vector', lambda e, ps=ps: e.tensor_copy(redt, ps[:, 0:16]), reads=[ps], writes=[redt])
        p.op('vector', lambda e: e.tensor_copy(redh, redt), reads=[redt], writes=[redh])
        p.op('vector', lambda e: e.tensor_tensor(redl, redt, redh, ALU.subtract), reads=[redt, redh], writes=[redl])
        for (src, dst) in ((redh, rht), (redl, rlt)):
            pb = cx.psum_bf[1]
            p.op('tensor', lambda e, src=src, pb=pb: e.transpose(pb[0:16, :], src, cx.consts['ident_bf']),
                 reads=[src], writes=[pb])
            p.op('vector', lambda e, dst=dst, pb=pb: e.tensor_copy(dst, pb[0:16, :]), reads=[pb], writes=[dst])
        ps2 = cx.psum[5]
        pairs2 = [(rht, wghT, [rht, wghT]), (rht, wglT, [rht, wglT]),
                  (rlt, wghT, [rlt, wghT]), (rlt, wglT, [rlt, wglT])]
        mm_group(cx, ps2[:, 0:8], pairs2, key=ps2)
        p.op('vector', lambda e, ps2=ps2: e.tensor_copy(lg, ps2[:, 0:8]), reads=[ps2], writes=[lg])
        p.op('vector', lambda e: e.tensor_reduce(out=r1[:, 0:1], in_=lg, axis=AX.X, op=ALU.max),
             reads=[lg], writes=[(r1, 'c0')])
        p.op('vector', lambda e: e.tensor_scalar_mul(r1[:, 1:2], r1[:, 0:1], -1.0), reads=[(r1, 'c0')], writes=[(r1, 'c1')])
        p.op('scalar', lambda e: e.activation(out=r2, in_=lg, func=AF.Exp, bias=r1[:, 1:2], accum_out=r1[:, 2:3]),
             reads=[lg, (r1, 'c1')], writes=[r2, (r1, 'c2')])
        p.op('vector', lambda e: e.reciprocal(r1[:, 3:4], r1[:, 2:3]), reads=[(r1, 'c2')], writes=[(r1, 'c3')])
        p.op('vector', lambda e: e.tensor_scalar(out=r2, in0=lg, scalar1=r1[:, 0:1], scalar2=None, op0=ALU.is_ge),
             reads=[lg, (r1, 'c0')], writes=[r2])
        p.op('vector', lambda e: e.tensor_scalar(out=r2, in0=r2, scalar1=r1[:, 3:4], scalar2=None, op0=ALU.mult),
             reads=[r2, (r1, 'c3')], writes=[r2])
        p.op('vector', lambda e: e.tensor_copy(cwbf, r2), reads=[r2], writes=[cwbf])
        pb = cx.psum_bf[0]
        p.op('tensor', lambda e, pb=pb: e.transpose(pb[0:8, :], cwbf, cx.consts['ident_bf']),
             reads=[cwbf], writes=[pb])
        p.op('vector', lambda e, ti=ti, pb=pb: e.tensor_copy(cw1_8[:, ti * 128:(ti + 1) * 128], pb[0:8, :]),
             reads=[pb], writes=[(cw1_8, f't{ti}')])

    for ti in range(16):
        emit_ln(cx, p, pool, x_src, xn_hi, xn_lo, lb, ti, route_fn=route1)
    p.barrier()
    for k in list(lb):
        pool.free('d_' + k)
    pool.free_listed('gbf', 8)
    pool.free_listed('s0f', 8)
    pool.free_listed('xn_lo1', 8)
    for nm in ('wred32', 'wredh', 'wredl', 'wgt', 'wsc', 'wsq', 'wgh', 'wgl',
               'wghT', 'wglT', 'd_gt', 'd_smt', 'd_redt', 'd_redh', 'd_redl', 'd_rht', 'd_rlt',
               'd_lg', 'd_r1', 'd_r2', 'd_cwbf'):
        pool.free(nm)


def emit_l1attn(cx, xn_hi, s1):
    p, pool = cx.p, cx.pool
    ones = cx.consts['ones_bf']
    wk = load_weight_fm(cx, cx.dram['WkT'], 8, D, 'wk')
    wv = load_weight_fm(cx, cx.dram['WvT'], 8, D, 'wv')
    bq = pool.alloc((1, D), BF16, 'bq'); bk = pool.alloc((1, D), BF16, 'bk'); bv = pool.alloc((1, D), BF16, 'bv')
    for t, src in ((bq, 'bq'), (bk, 'bk'), (bv, 'bv')):
        p.dma('gpsimd', lambda e, t=t, src=src: e.dma_start(out=t, in_=cx.dram[src]), writes=[t])
    K_f = pool.alloc_list(8, (128, T), BF16, 'K_f')
    V_tm = pool.alloc_list(16, (128, D), BF16, 'V_tm')
    for m in range(8):
        for n in range(T // NB):
            ps = cx.psum[m % 4]
            pairs = [(wk[kc][:, m * 128:(m + 1) * 128], xn_hi[kc][:, n * NB:(n + 1) * NB],
                      [wk[kc], (xn_hi[kc], None)]) for kc in range(8)]
            extra = [(bk[:, m * 128:(m + 1) * 128], ones[:, :NB], [bk, ones])]
            mm_group(cx, ps[:, :], pairs, extra=extra, key=ps)
            p.op('scalar', lambda e, m=m, n=n, ps=ps: e.activation(
                out=K_f[m][:, n * NB:(n + 1) * NB], in_=ps[:, :], func=AF.Copy),
                reads=[ps], writes=[K_f[m]])
    for ti in range(16):
        for nb2 in range(2):
            ps = cx.psum[4 + (nb2 % 2)]
            pairs = [(xn_hi[kc][:, ti * 128:(ti + 1) * 128], wv[kc][:, nb2 * NB:(nb2 + 1) * NB],
                      [(xn_hi[kc], None), wv[kc]]) for kc in range(8)]
            extra = [(ones[:, 0:128], bv[:, nb2 * NB:(nb2 + 1) * NB], [bv, ones])]
            mm_group(cx, ps[:, :], pairs, extra=extra, key=ps)
            p.op('scalar', lambda e, ti=ti, nb2=nb2, ps=ps: e.activation(
                out=V_tm[ti][:, nb2 * NB:(nb2 + 1) * NB], in_=ps[:, :], func=AF.Copy),
                reads=[ps], writes=[V_tm[ti]])
    p.barrier()
    pool.free_listed('wk', 8)
    pool.free_listed('wv', 8)
    pool.new_epoch()
    import os as _os3
    _qlim = int(_os3.environ.get('E_QLIM', '16'))
    wq = load_weight_fm(cx, cx.dram['WqT'], 8, D, 'wq')
    QT = pool.alloc_list(8, (128, 128), BF16, 'QT')
    Srow = pool.alloc((128, T), F32, 'Srow')
    Prow = pool.alloc((128, T), BF16, 'Prow')
    PT = pool.alloc_list(16, (128, 128), BF16, 'PT')
    mrow = pool.alloc((128, 4), F32, 'mrow')
    for qi in range(_qlim):
        kend = (qi + 1) * 128
        for m in range(8):
            ps = cx.psum[m % 2]
            pairs = [(wq[kc][:, m * 128:(m + 1) * 128], xn_hi[kc][:, qi * 128:(qi + 1) * 128],
                      [wq[kc], (xn_hi[kc], None)]) for kc in range(8)]
            extra = [(bq[:, m * 128:(m + 1) * 128], ones[:, 0:128], [bq, ones])]
            mm_group(cx, ps[:, 0:128], pairs, extra=extra, key=ps)
            p.op('scalar', lambda e, m=m, ps=ps: e.activation(out=QT[m], in_=ps[:, 0:128], func=AF.Copy),
                 reads=[ps], writes=[QT[m]])
        nblocks = (kend + NB - 1) // NB
        for nb in range(nblocks):
            nw = min(NB, kend - nb * NB)
            ps = cx.psum[2 + nb % 2]
            pairs = [(QT[kc], K_f[kc][:, nb * NB: nb * NB + nw], [QT[kc], K_f[kc]])
                     for kc in range(8)]
            mm_group(cx, ps[:, :nw], pairs, key=ps)
            p.op('vector', lambda e, nb=nb, nw=nw, ps=ps: e.tensor_copy(Srow[:, nb * NB: nb * NB + nw], ps[:, :nw]),
                 reads=[ps], writes=[Srow])
        p.op('vector', lambda e, qi=qi, kend=kend: e.tensor_tensor(
            Srow[:, qi * 128:kend], Srow[:, qi * 128:kend], cx.consts['trineg'], ALU.add),
            reads=[Srow, cx.consts['trineg']], writes=[Srow])
        p.op('vector', lambda e, kend=kend: e.tensor_reduce(out=mrow[:, 0:1], in_=Srow[:, :kend], axis=AX.X, op=ALU.max),
             reads=[Srow], writes=[(mrow, 'c0')])
        p.op('vector', lambda e: e.tensor_scalar_mul(mrow[:, 1:2], mrow[:, 0:1], -1.0 / 32.0),
             reads=[(mrow, 'c0')], writes=[(mrow, 'c1')])
        p.op('scalar', lambda e, kend=kend: e.activation(out=Srow[:, :kend], in_=Srow[:, :kend], func=AF.Exp,
                                                         scale=1.0 / 32.0, bias=mrow[:, 1:2], accum_out=mrow[:, 2:3]),
             reads=[Srow, (mrow, 'c1')], writes=[Srow, (mrow, 'c2')])
        p.op('vector', lambda e: e.reciprocal(mrow[:, 3:4], mrow[:, 2:3]), reads=[(mrow, 'c2')], writes=[(mrow, 'c3')])
        p.op('vector', lambda e, kend=kend: e.tensor_copy(Prow[:, :kend], Srow[:, :kend]),
             reads=[Srow], writes=[Prow])
        for kc in range(qi + 1):
            pb = cx.psum_bf[kc % 2]
            p.op('tensor', lambda e, kc=kc, pb=pb: e.transpose(pb[:, :], Prow[:, kc * 128:(kc + 1) * 128],
                                                               cx.consts['ident_bf']),
                 reads=[Prow], writes=[pb])
            p.op('scalar', lambda e, kc=kc, pb=pb: e.activation(out=PT[kc], in_=pb[:, :], func=AF.Copy),
                 reads=[pb], writes=[PT[kc]])
        for nb2 in range(2):
            ps = cx.psum[4 + nb2]
            pairs = [(PT[kc], V_tm[kc][:, nb2 * NB:(nb2 + 1) * NB], [PT[kc], V_tm[kc]])
                     for kc in range(qi + 1)]
            mm_group(cx, ps[:, :], pairs, key=ps)
            p.op('scalar', lambda e, qi=qi, nb2=nb2, ps=ps: e.activation(
                out=s1[qi][:, nb2 * NB:(nb2 + 1) * NB], in_=ps[:, :], func=AF.Copy, scale=mrow[:, 3:4]),
                reads=[ps, (mrow, 'c3')], writes=[s1[qi]])
    pool.free_listed('wq', 8)
    pool.free_listed('K_f', 8)
    pool.free_listed('V_tm', 16)
    pool.free_listed('QT', 8)
    pool.free_listed('PT', 16)
    for nm in ('bq', 'bk', 'bv', 'Srow', 'Prow', 'mrow'):
        pool.free(nm)


def emit_l1ffn(cx, xn_hi, cw1_8, y1):
    p, pool = cx.p, cx.pool
    d = cx.dram
    for md in range(8):
        p.op('vector', lambda e, md=md: e.memset(y1[md], 0.0), writes=[y1[md]])
    b2t = pool.alloc((8, D), BF16, 'l1b2t')
    p.dma('gpsimd', lambda e: e.dma_start(out=b2t, in_=d['l1b2']), writes=[b2t])
    for md in range(8):
        for n in range(T // NB):
            ps = cx.psum[5]
            mm_group(cx, ps[:, :], [(b2t[:, md * 128:(md + 1) * 128], cw1_8[:, n * NB:(n + 1) * NB],
                                     [b2t, (cw1_8, None)])], key=ps)
            p.op('vector', lambda e, md=md, n=n, ps=ps: e.tensor_tensor(
                y1[md][:, n * NB:(n + 1) * NB], y1[md][:, n * NB:(n + 1) * NB], ps[:, :], ALU.add),
                reads=[ps, y1[md]], writes=[y1[md]])
    cwb = pool.alloc((128, T), BF16, 'l1cwb')
    ab8p = [pool.alloc_list(8, (128, NB), BF16, f'l1ab8_{pr}') for pr in range(2)]
    b1c = pool.alloc((128, 32), F32, 'l1b1c')
    wpar = []
    for par in range(2):
        wpar.append({'w1': pool.alloc_list(16, (128, 512), BF16, f'l1w1_{par}'),
                     'w2': pool.alloc_list(8, (128, 1024), BF16, f'l1w2_{par}')})

    def load_pair(par, ei, qq):
        q0 = 2 * qq
        for sub in range(2):
            for kc in range(8):
                t = wpar[par]['w1'][sub * 8 + kc]
                p.dma('gpsimd', lambda e, t=t, ei=ei, q=q0 + sub, kc=kc: e.dma_start(
                    out=t, in_=d['l1w1'][ei][kc * 128:(kc + 1) * 128, q * 512:(q + 1) * 512]), writes=[t])
            for kc in range(4):
                t = wpar[par]['w2'][sub * 4 + kc]
                p.dma('gpsimd', lambda e, t=t, ei=ei, q=q0 + sub, kc=kc: e.dma_start(
                    out=t, in_=d['l1w2'][ei][q * 512 + kc * 128: q * 512 + (kc + 1) * 128, :]), writes=[t])

    steps = [(ei, qq) for ei in range(E) for qq in range(4)]
    load_pair(0, 0, 0)
    abi = 0
    pending = None

    def emit_w2(w2bufs, abp, n):
        for md in range(8):
            ps = cx.psum[4 + md % 2]
            pairs = [(w2bufs[kc][:, md * 128:(md + 1) * 128], ab8p[abp][kc],
                      [w2bufs[kc], ab8p[abp][kc]]) for kc in range(8)]
            mm_group(cx, ps[:, :], pairs, key=ps)
            p.op('vector', lambda e, md=md, n=n, ps=ps: e.tensor_tensor(
                y1[md][:, n * NB:(n + 1) * NB], y1[md][:, n * NB:(n + 1) * NB], ps[:, :], ALU.add),
                reads=[ps, y1[md]], writes=[y1[md]])

    for idx, (ei, qq) in enumerate(steps):
        par = idx % 2
        if idx + 1 < len(steps):
            ei2, qq2 = steps[idx + 1]
            load_pair(1 - par, ei2, qq2)
        if qq == 0:
            for n in range(T // NB):
                ps = cx.psum[5]
                mm_group(cx, ps[:, :], [(cx.consts['sel8_bf'][:, ei * 128:(ei + 1) * 128],
                                         cw1_8[:, n * NB:(n + 1) * NB],
                                         [cx.consts['sel8_bf'], (cw1_8, None)])], key=ps)
                p.op('scalar', lambda e, n=n, ps=ps: e.activation(out=cwb[:, n * NB:(n + 1) * NB],
                                                                  in_=ps[:, :], func=AF.Copy),
                     reads=[ps], writes=[(cwb, f'n{n}')])
            p.dma('sync', lambda e, ei=ei: e.dma_start(out=b1c, in_=d['l1b1_pp'][ei]), writes=[b1c])
        W = wpar[par]
        for n in range(T // NB):
            for mfa in range(8):
                sub, mf = mfa // 4, mfa % 4
                fa = (2 * qq + sub) * 4 + mf
                ps = cx.psum[mfa % 4]
                pairs = [(W['w1'][sub * 8 + kc][:, mf * 128:(mf + 1) * 128], xn_hi[kc][:, n * NB:(n + 1) * NB],
                          [W['w1'][sub * 8 + kc], (xn_hi[kc], None)]) for kc in range(8)]
                mm_group(cx, ps[:, :], pairs, key=ps)
                p.op('scalar', lambda e, mfa=mfa, n=n, fa=fa, ps=ps, ab=ab8p[abi % 2]: e.activation(
                    out=ab[mfa], in_=ps[:, :], func=AF.Gelu, bias=b1c[:, fa:fa + 1]),
                    reads=[ps, b1c], writes=[ab8p[abi % 2][mfa]])
                p.op('vector', lambda e, mfa=mfa, n=n, ab=ab8p[abi % 2]: e.tensor_tensor(
                    ab[mfa], ab[mfa], cwb[:, n * NB:(n + 1) * NB], ALU.mult),
                    reads=[ab8p[abi % 2][mfa], (cwb, f'n{n}')], writes=[ab8p[abi % 2][mfa]])
            if pending is not None:
                emit_w2(*pending)
            pending = (W['w2'], abi % 2, n)
            abi += 1
    if pending is not None:
        emit_w2(*pending)
    for nm in ('l1b2t', 'l1cwb', 'l1b1c'):
        pool.free(nm)
    for pr in range(2):
        pool.free_listed(f'l1ab8_{pr}', 8)
    for par in range(2):
        pool.free_listed(f'l1w1_{par}', 16)
        pool.free_listed(f'l1w2_{par}', 8)


def emit_final(cx, y1, s1_scr, x1_scr, out):
    p, pool = cx.p, cx.pool
    g1 = pool.alloc_list(8, (128, T), BF16, 'g1')
    for md in range(8):
        p.op('scalar', lambda e, md=md: e.activation(out=g1[md], in_=y1[md], func=AF.Silu),
             reads=[y1[md]], writes=[g1[md]])
    xt = pool.alloc((128, D), F32, 'f_xt')
    gt = pool.alloc((128, D), BF16, 'f_gt')
    ot = pool.alloc((128, D), F32, 'f_ot')
    s1t = pool.alloc((128, D), BF16, 'f_s1t')
    for ti in range(16):
        p.dma('sync', lambda e, ti=ti: e.dma_start(out=s1t, in_=s1_scr[ti * 128:(ti + 1) * 128, :]),
              reads=[(s1_scr, f't{ti}')], writes=[s1t])
        for md in range(8):
            pb = cx.psum_bf[md % 2]
            p.op('tensor', lambda e, md=md, ti=ti, pb=pb: e.transpose(
                pb[:, :], g1[md][:, ti * 128:(ti + 1) * 128], cx.consts['ident_bf']),
                reads=[g1[md]], writes=[pb])
            p.op('scalar', lambda e, md=md, pb=pb: e.activation(out=gt[:, md * 128:(md + 1) * 128], in_=pb[:, :], func=AF.Copy),
                 reads=[pb], writes=[(gt, f'm{md}')])
        p.dma('sync', lambda e, ti=ti: e.dma_start(out=xt, in_=x1_scr[ti * 128:(ti + 1) * 128, :]),
              reads=[(x1_scr, f't{ti}')], writes=[xt])
        greads = [(gt, f'm{m}') for m in range(8)]
        p.op('vector', lambda e: e.tensor_tensor(ot, gt, s1t, ALU.mult),
             reads=greads + [s1t], writes=[(ot, 'o')])
        p.op('vector', lambda e: e.tensor_tensor(ot, ot, xt, ALU.add), reads=[(ot, 'o'), xt], writes=[(ot, 'o')])
        p.dma('sync', lambda e, ti=ti: e.dma_start(out=out[ti * 128:(ti + 1) * 128, :], in_=ot),
              reads=[(ot, 'o')], writes=[(out, f't{ti}')])
    pool.free_listed('g1', 8)
    pool.free('f_xt'); pool.free('f_gt'); pool.free('f_ot'); pool.free('f_s1t')


def kernel(**inputs):
    g = host_prep(inputs)
    x = np.ascontiguousarray(np.asarray(inputs['L0_attn_x'], dtype=np.float32))
    cfg = {'g_shapes': {k: v.shape for k, v in g.items()}, 'stop_after': None, 'dbg': []}
    nc, _dbg = build(cfg)
    common = {'g_' + k: v for k, v in g.items()}
    in_maps = []
    for b in range(8):
        m = dict(common)
        m['x'] = np.ascontiguousarray(x[b])
        in_maps.append(m)
    res = run_bass_kernel_spmd(nc, in_maps, core_ids=list(range(8)))
    out = np.stack([res.results[b]['out'] for b in range(8)], axis=0)
    return out.astype(np.float32)


# version 05 (final)
# speedup vs baseline: 1.0332x; 1.0275x over previous
import numpy as np
import ml_dtypes
import concourse.bass as bass
import concourse.mybir as mybir
from concourse.bass_utils import run_bass_kernel_spmd
from contextlib import ExitStack

F32 = mybir.dt.float32
BF16 = mybir.dt.bfloat16
AF = mybir.ActivationFunctionType
ALU = mybir.AluOpType
AX = mybir.AxisListType

T = 2048
D = 1024
DI = 2048
DS = 16
DTR = 64
E = 8
NB = 512
LSC = 256
POOL_BYTES = 212736

ENGINES = ('tensor', 'vector', 'scalar', 'gpsimd', 'sync')
DMA_QUEUES = ('sync', 'gpsimd', 'scalar')


class Rec:
    __slots__ = ('eng', 'fn', 'deps', 'pre_drain', 'inc', 'dma', 'idx', 'val')

    def __init__(self, eng, fn, dma, idx):
        self.eng = eng
        self.fn = fn
        self.deps = []
        self.pre_drain = False
        self.inc = dma
        self.dma = dma
        self.idx = idx
        self.val = None


class P:

    def __init__(self, nc):
        self.nc = nc
        self.stack = ExitStack()
        self.recs = {e: [] for e in ENGINES}
        self.sems = {}
        for name in list(ENGINES) + [q + '_q' for q in DMA_QUEUES]:
            self.sems[name] = self.stack.enter_context(nc.semaphore('sem_' + name))
        self.buf = {}
        self.last_drain = {e: -1 for e in ENGINES}
        self.force_drain = set()
        self.seen = {}

    @staticmethod
    def _keys(items):
        out = []
        for it in items or ():
            if isinstance(it, tuple):
                h, tag = it
            else:
                h, tag = it, None
            out.append((id(h), tag))
        return out

    def _state(self, tid, tag):
        return self.buf.setdefault((tid, tag), {'w': None, 'r': []})

    def _conflicting(self, tid, tag):
        states = []
        if tag is None:
            found = False
            for (t2, _g2), st in list(self.buf.items()):
                if t2 == tid:
                    states.append(st)
                    found = True
            if not found:
                states.append(self._state(tid, None))
        else:
            states.append(self._state(tid, tag))
            if (tid, None) in self.buf:
                states.append(self.buf[(tid, None)])
        return states

    def _edge(self, rec, producer):
        if producer is None or producer is rec:
            return
        if producer.eng == rec.eng and not producer.dma and not rec.dma:
            if rec.eng == 'tensor':
                return
        producer.inc = True
        rec.deps.append(producer)

    def _wire(self, rec, reads, writes):
        if rec.eng in self.force_drain and not rec.dma:
            if rec.eng not in ('tensor',):
                rec.pre_drain = True
                self.last_drain[rec.eng] = rec.idx
            self.force_drain.discard(rec.eng)
        for tid, tag in self._keys(reads):
            for st in self._conflicting(tid, tag):
                self._edge(rec, st['w'])
        for tid, tag in self._keys(writes):
            for st in self._conflicting(tid, tag):
                self._edge(rec, st['w'])
                for r in st['r']:
                    self._edge(rec, r)
        for tid, tag in self._keys(reads):
            self._state(tid, tag)['r'].append(rec)
        for tid, tag in self._keys(writes):
            st = self._state(tid, tag)
            st['w'] = rec
            st['r'] = []

    def op(self, eng, fn, reads=(), writes=()):
        rec = Rec(eng, fn, False, len(self.recs[eng]))
        self._wire(rec, reads, writes)
        self.recs[eng].append(rec)
        return rec

    def dma(self, queue, fn, reads=(), writes=()):
        rec = Rec(queue, fn, True, len(self.recs[queue]))
        self._wire(rec, reads, writes)
        self.recs[queue].append(rec)
        return rec

    def barrier(self):
        lasts = []
        for e in ENGINES:
            for r in reversed(self.recs[e]):
                if r.fn is not None:
                    r.inc = True
                    lasts.append(r)
                    break
        for e in ENGINES:
            rec = Rec(e, None, False, len(self.recs[e]))
            rec.deps = [l for l in lasts if l.eng != e or l.dma]
            self.recs[e].append(rec)
        self.force_drain = set(('vector', 'scalar'))

    def _sem_name(self, rec):
        return rec.eng + '_q' if rec.dma else rec.eng

    def finish_and_flush(self, block):
        lasts = []
        for q in DMA_QUEUES:
            dr = [r for r in self.recs[q] if r.dma]
            if dr:
                dr[-1].inc = True
                lasts.append(dr[-1])
        rec = Rec('sync', None, False, len(self.recs['sync']))
        rec.deps = lasts
        self.recs['sync'].append(rec)

        cnt = {}
        for e in ENGINES:
            for r in self.recs[e]:
                if r.inc:
                    sn = self._sem_name(r)
                    cnt[sn] = cnt.get(sn, 0) + (16 if r.dma else 1)
                    r.val = cnt[sn]

        def emit_for(eng):
            def body(eh):
                for r in self.recs[eng]:
                    for dep in r.deps:
                        sn = self._sem_name(dep)
                        key = (eng, sn)
                        if self.seen.get(key, 0) < dep.val:
                            self.seen[key] = dep.val
                            eh.wait_ge(self.sems[sn], dep.val)
                    if r.pre_drain:
                        eh.drain()
                    if r.fn is None:
                        assert not r.inc, 'barrier rec marked inc'
                        continue
                    inst = r.fn(eh)
                    if r.inc:
                        inst.then_inc(self.sems[self._sem_name(r)], 16 if r.dma else 1)
            return body

        for eng in ENGINES:
            if self.recs[eng]:
                getattr(block, eng)(emit_for(eng))


class Pool:

    def __init__(self, nc, stack, nbytes):
        self.nw = nbytes // 4
        self.base = stack.enter_context(nc.sbuf_tensor('pool', [128, self.nw], F32))
        self.cur = 0
        self.epoch = 0
        self.free_list = []
        self.live = {}

    def alloc(self, shape, dtype, name):
        assert name not in self.live, name
        esz = {F32: 1, BF16: 2}[dtype]
        n_el = int(np.prod(shape[1:]))
        nw = (n_el + esz - 1) // esz
        nw = (nw + 15) & ~15
        off = None
        for i, (o, size, ep) in enumerate(self.free_list):
            if ep < self.epoch and size >= nw:
                off = o
                if size > nw:
                    self.free_list[i] = (o + nw, size - nw, ep)
                else:
                    self.free_list.pop(i)
                break
        if off is None:
            off = self.cur
            assert off + nw <= self.nw, f'SBUF pool overflow at {name}: {(off + nw) * 4} bytes'
            self.cur = off + nw
        self.live[name] = (off, nw)
        v = self.base[:shape[0], off:off + nw]
        if dtype == BF16:
            v = v.bitcast(BF16)
        v = v[:, :n_el]
        if len(shape) > 2:
            dims = ' '.join(f'd{i}' for i in range(1, len(shape)))
            v = v.rearrange(f'p ({dims}) -> p {dims}',
                            **{f'd{i}': shape[i] for i in range(1, len(shape))})
        return v

    def alloc_list(self, n, shape, dtype, name):
        return [self.alloc(shape, dtype, f'{name}@{i}') for i in range(n)]

    def free(self, name):
        off, nw = self.live.pop(name)
        self.free_list.append((off, nw, self.epoch))
        self.free_list.sort()
        merged = []
        for o, s, ep in self.free_list:
            if merged and merged[-1][0] + merged[-1][1] == o and merged[-1][2] == ep:
                mo, ms, mep = merged[-1]
                merged[-1] = (mo, ms + s, mep)
            else:
                merged.append((o, s, ep))
        self.free_list = merged

    def free_listed(self, name, n):
        for i in range(n):
            self.free(f'{name}@{i}')

    def new_epoch(self):
        self.epoch += 1
        old_ep = self.epoch - 1
        self.free_list = [(o, s, min(ep, old_ep)) for o, s, ep in self.free_list]
        self.free_list.sort()
        merged = []
        for o, s, ep in self.free_list:
            if merged and merged[-1][0] + merged[-1][1] == o:
                merged[-1] = (merged[-1][0], merged[-1][1] + s, old_ep)
            else:
                merged.append((o, s, old_ep))
        self.free_list = merged
        if self.free_list and self.free_list[-1][0] + self.free_list[-1][1] == self.cur:
            self.cur = self.free_list[-1][0]
            self.free_list.pop()


def host_prep(w):
    g = {}
    f32 = lambda a: np.ascontiguousarray(np.asarray(a), dtype=np.float32)
    g['w_inT'] = f32(w['L0_attn_in_proj_w'].T)
    g['conv_w_pp'] = f32(np.transpose(np.reshape(np.asarray(w['L0_attn_conv_w'])[:, 0, :], (16, 128, 4)), (1, 0, 2)).reshape(128, 64))
    g['conv_b_pp'] = f32(np.reshape(w['L0_attn_conv_b'], (16, 128)).T)
    g['xpw_dtT'] = f32(np.asarray(w['L0_attn_x_proj_w'])[:DTR].T)
    g['xpw_BT'] = f32(np.asarray(w['L0_attn_x_proj_w'])[DTR:DTR + DS].T)
    g['xpw_CT'] = f32(np.asarray(w['L0_attn_x_proj_w'])[DTR + DS:].T)
    g['dtw_T'] = f32(w['L0_attn_dt_proj_w'].T)
    g['dtb_pp'] = f32(np.reshape(w['L0_attn_dt_proj_b'], (16, 128)).T)
    g['Alog_pp'] = f32(np.transpose(np.reshape(w['L0_attn_A_log'], (16, 128, DS)), (1, 0, 2)).reshape(128, 16 * DS))
    g['D_pp'] = f32(np.reshape(w['L0_attn_D'], (16, 128)).T)
    g['w_outT'] = f32(w['L0_attn_out_proj_w'].T)
    g['gate_wT'] = f32(w['L0_ffn_gate_w'].T)
    g['gate_b_bc'] = f32(np.broadcast_to(np.asarray(w['L0_ffn_gate_b'])[None, :], (128, 8)))
    w1 = f32(w['L0_ffn_w1']); w3 = f32(w['L0_ffn_w3'])
    g['w1gT'] = np.ascontiguousarray(np.transpose(w1[:, 0::2, :], (0, 2, 1)))
    g['w1lT'] = np.ascontiguousarray(np.transpose(w1[:, 1::2, :], (0, 2, 1)))
    g['w3gT'] = np.ascontiguousarray(np.transpose(w3[:, 0::2, :], (0, 2, 1)))
    g['w3lT'] = np.ascontiguousarray(np.transpose(w3[:, 1::2, :], (0, 2, 1)))
    b1 = f32(w['L0_ffn_b1']); b3 = f32(w['L0_ffn_b3'])
    g['b1g'] = np.ascontiguousarray(b1[:, 0::2]); g['b1l'] = np.ascontiguousarray(b1[:, 1::2])
    g['b3g'] = np.ascontiguousarray(b3[:, 0::2]); g['b3l'] = np.ascontiguousarray(b3[:, 1::2])
    g['w2T'] = np.ascontiguousarray(np.transpose(f32(w['L0_ffn_w2']), (0, 2, 1))) * np.float32(1.0 / 1.702)
    g['b2'] = f32(w['L0_ffn_b2'])
    sw1 = f32(w['L0_ffn_sw1']); sw3 = f32(w['L0_ffn_sw3'])
    g['sw1gT'] = np.ascontiguousarray(sw1[0::2, :].T); g['sw1lT'] = np.ascontiguousarray(sw1[1::2, :].T)
    g['sw3gT'] = np.ascontiguousarray(sw3[0::2, :].T); g['sw3lT'] = np.ascontiguousarray(sw3[1::2, :].T)
    sb1 = f32(w['L0_ffn_sb1']); sb3 = f32(w['L0_ffn_sb3'])
    g['sb1g'] = np.ascontiguousarray(sb1[0::2])[None, :]; g['sb1l'] = np.ascontiguousarray(sb1[1::2])[None, :]
    g['sb3g'] = np.ascontiguousarray(sb3[0::2])[None, :]; g['sb3l'] = np.ascontiguousarray(sb3[1::2])[None, :]
    g['sw2T'] = f32(w['L0_ffn_sw2'].T) * np.float32(1.0 / 1.702)
    g['sb2'] = f32(w['L0_ffn_sb2'])[None, :]
    g['WqT'] = f32(w['L1_attn_Wq'].T); g['WkT'] = f32(w['L1_attn_Wk'].T); g['WvT'] = f32(w['L1_attn_Wv'].T)
    g['bq'] = f32(w['L1_attn_bq'])[None, :]; g['bk'] = f32(w['L1_attn_bk'])[None, :]; g['bv'] = f32(w['L1_attn_bv'])[None, :]
    g['w_red'] = f32(w['L1_ffn_w_red'])
    g['wg'] = f32(w['L1_ffn_wg'])
    g['l1w1'] = f32(w['L1_ffn_w1'])
    g['l1b1_pp'] = np.ascontiguousarray(np.transpose(np.reshape(f32(w['L1_ffn_b1']), (8, 32, 128)), (0, 2, 1)))
    g['l1w2'] = f32(w['L1_ffn_w2'])
    g['l1b2'] = f32(w['L1_ffn_b2'])
    g['ident'] = np.eye(128, dtype=np.float32)
    g['ones_row'] = np.ones((1, NB), np.float32)
    tri = np.tril(np.ones((128, 128), np.float32))
    g['trineg'] = (tri - 1.0) * 1e30
    sel8 = np.zeros((8, 8 * 128), np.float32)
    for e in range(8):
        sel8[e, e * 128:(e + 1) * 128] = 1.0
    g['sel8'] = sel8
    return g


class Ctx:
    def __init__(self, nc, p, pool, dram):
        self.nc = nc
        self.p = p
        self.pool = pool
        self.dram = dram
        self.psum = []
        self.psum_bf = []
        self.consts = {}


def load_const_tiles(cx):
    p, pool = cx.p, cx.pool
    for name, shape, src in [('trineg', (128, 128), 'trineg'),
                             ('gate_b_bc', (128, 8), 'gate_b_bc')]:
        t = pool.alloc(shape, F32, 'c_' + name)
        cx.consts[name] = t
        p.dma('sync', lambda e, t=t, src=src: e.dma_start(out=t, in_=cx.dram[src]), writes=[t])
    for val, nm in ((1e-5, 'eps'), (1.0, 'one')):
        t = pool.alloc((128, 1), F32, 'c_' + nm)
        cx.consts[nm] = t
        p.op('vector', lambda e, t=t, val=val: e.memset(t, val), writes=[t])
    for name, shape, src in [('ident_bf', (128, 128), 'ident'),
                             ('ones_bf', (1, NB), 'ones_row'),
                             ('sel8_bf', (8, 8 * 128), 'sel8')]:
        t = pool.alloc(shape, BF16, 'c_' + name)
        cx.consts[name] = t
        p.dma('gpsimd', lambda e, t=t, src=src: e.dma_start(out=t, in_=cx.dram[src]), writes=[t])


def load_weight_fm(cx, src_ap, kchunks, mcols, name, queue='gpsimd'):
    ts = cx.pool.alloc_list(kchunks, (128, mcols), BF16, name)
    for kc in range(kchunks):
        cx.p.dma(queue, lambda e, t=ts[kc], kc=kc: e.dma_start(
            out=t, in_=src_ap[kc * 128:(kc + 1) * 128, :]), writes=[ts[kc]])
    return ts


def mm_group(cx, psum, pairs, extra=(), key=None):
    p = cx.p
    if key is None:
        key = psum
    allr = []
    for pr in pairs:
        allr += pr[2]
    for pr in extra:
        allr += pr[2]
    n = len(pairs)
    ne = len(extra)
    for i, pr in enumerate(pairs):
        lt, rt = pr[0], pr[1]
        start = (i == 0)
        stop = (i == n - 1) and ne == 0
        p.op('tensor', lambda e, lt=lt, rt=rt, s=start, st=stop: e.matmul(psum, lt, rt, start=s, stop=st),
             reads=allr if start else (), writes=[key] if (start or stop) else ())
    for j, pr in enumerate(extra):
        lt, rt = pr[0], pr[1]
        stop = (j == ne - 1)
        p.op('tensor', lambda e, lt=lt, rt=rt, st=stop: e.matmul(psum, lt, rt, start=False, stop=st),
             reads=(), writes=[key] if stop else ())


def emit_ln(cx, p, pool, x_src_fn, xn_hi, xn_lo, bufs, ti, route_fn=None):
    xt, xsq, xnf, xh, xl, st = (bufs[k] for k in ('xt', 'xsq', 'xnf', 'xh', 'xl', 'st'))
    x_src_fn(ti)
    p.op('vector', lambda e: e.tensor_reduce(out=st[:, 0:1], in_=xt, axis=AX.X, op=ALU.add),
         reads=[(xt, 'x1')], writes=[(st, 's')])
    p.op('scalar', lambda e: e.activation(out=xsq, in_=xt, func=AF.Square, accum_out=st[:, 1:2]),
         reads=[(xt, 'x1')], writes=[xsq, (st, 'q')])
    p.op('vector', lambda e: e.tensor_scalar_mul(st[:, 2:3], st[:, 0:1], 1.0 / D), reads=[(st, 's')], writes=[(st, 'm')])
    p.op('vector', lambda e: e.tensor_scalar_mul(st[:, 3:4], st[:, 0:1], -1.0 / D), reads=[(st, 's')], writes=[(st, 'nm')])
    p.op('vector', lambda e: e.tensor_tensor(st[:, 4:5], st[:, 2:3], st[:, 2:3], ALU.mult), reads=[(st, 'm')], writes=[(st, 'm2')])
    p.op('vector', lambda e: e.scalar_tensor_tensor(out=st[:, 5:6], in0=st[:, 1:2], scalar=1.0 / D, in1=st[:, 4:5],
                                                    op0=ALU.mult, op1=ALU.subtract),
         reads=[(st, 'q'), (st, 'm2')], writes=[(st, 'v')])
    p.op('scalar', lambda e: e.activation(out=st[:, 6:7], in_=st[:, 5:6], func=AF.Ln, bias=cx.consts['eps'][:, 0:1]),
         reads=[(st, 'v'), cx.consts['eps']], writes=[(st, 'ln')])
    p.op('scalar', lambda e: e.activation(out=st[:, 7:8], in_=st[:, 6:7], func=AF.Exp, scale=-0.5),
         reads=[(st, 'ln')], writes=[(st, 'r')])
    p.op('vector', lambda e: e.scalar_tensor_tensor(out=xnf, in0=xt, scalar=st[:, 3:4],
                                                    in1=st[:, 7:8].broadcast_to([128, D]),
                                                    op0=ALU.add, op1=ALU.mult),
         reads=[(xt, 'x1'), (st, 'nm'), (st, 'r')], writes=[xnf])
    p.op('scalar', lambda e: e.activation(out=xh, in_=xnf, func=AF.Copy), reads=[xnf], writes=[xh])
    p.op('vector', lambda e: e.tensor_tensor(xl, xnf, xh, ALU.subtract), reads=[xnf, xh], writes=[xl])
    for kc in range(8):
        for (src, dstl) in ((xh, xn_hi), (xl, xn_lo)):
            pb = cx.psum_bf[kc % 2]
            p.op('tensor', lambda e, src=src, kc=kc, pb=pb: e.transpose(
                pb[:, :], src[:, kc * 128:(kc + 1) * 128], cx.consts['ident_bf']),
                reads=[src, cx.consts['ident_bf']], writes=[pb])
            p.op('scalar', lambda e, dt_=dstl[kc], ti=ti, pb=pb: e.activation(
                out=dt_[:, ti * 128:(ti + 1) * 128], in_=pb[:, :], func=AF.Copy),
                reads=[pb], writes=[(dstl[kc], f't{ti}')])
    if route_fn is not None:
        route_fn(ti, xh, xl)


def emit_l0_route(cx, ti, xn_hi, xn_lo, gwhi, gwlo, cw8, rb):
    p = cx.p
    gb = cx.consts['gate_b_bc']
    cwt, scr1, scr2 = rb['cwt'], rb['scr1'], rb['scr2']
    ps = cx.psum[4]
    pairs = []
    for kc in range(8):
        for (xa, ga) in ((xn_hi, gwhi), (xn_hi, gwlo), (xn_lo, gwhi), (xn_lo, gwlo)):
            pairs.append((xa[kc][:, ti * 128:(ti + 1) * 128], ga[:, kc, :],
                          [(xa[kc], f't{ti}'), ga]))
    mm_group(cx, ps[:, 0:8], pairs, key=ps)
    p.op('vector', lambda e: e.tensor_copy(scr1, ps[:, 0:8]), reads=[ps], writes=[scr1])
    p.op('vector', lambda e: e.tensor_reduce(out=scr2[:, 8:9], in_=scr1, axis=AX.X, op=ALU.max),
         reads=[scr1], writes=[(scr2, 'c8')])
    p.op('vector', lambda e: e.tensor_scalar_mul(scr2[:, 9:10], scr2[:, 8:9], -1.0),
         reads=[(scr2, 'c8')], writes=[(scr2, 'c9')])
    p.op('scalar', lambda e: e.activation(out=cwt, in_=scr1, func=AF.Exp, bias=scr2[:, 9:10], accum_out=scr2[:, 10:11]),
         reads=[scr1, (scr2, 'c9')], writes=[cwt, (scr2, 'c10')])
    p.op('vector', lambda e: e.reciprocal(scr2[:, 11:12], scr2[:, 10:11]), reads=[(scr2, 'c10')], writes=[(scr2, 'c11')])
    p.op('vector', lambda e: e.tensor_scalar(out=cwt, in0=cwt, scalar1=scr2[:, 11:12], scalar2=None, op0=ALU.mult),
         reads=[cwt, (scr2, 'c11')], writes=[cwt])
    p.op('vector', lambda e: e.tensor_tensor(scr1, cwt, gb, ALU.add), reads=[cwt, gb], writes=[scr1])
    p.op('vector', lambda e: e.tensor_reduce(out=scr2[:, 12:13], in_=scr1, axis=AX.X, op=ALU.max),
         reads=[scr1], writes=[(scr2, 'c12')])
    p.op('vector', lambda e: e.tensor_scalar(out=scr2[:, 0:8], in0=scr1, scalar1=scr2[:, 12:13], scalar2=None, op0=ALU.is_ge),
         reads=[scr1, (scr2, 'c12')], writes=[(scr2, 'v8')])
    p.op('vector', lambda e: e.scalar_tensor_tensor(out=scr2[:, 0:8], in0=scr2[:, 0:8], scalar=-1e30, in1=scr1,
                                                    op0=ALU.mult, op1=ALU.add),
         reads=[(scr2, 'v8'), scr1], writes=[(scr2, 'v8')])
    p.op('vector', lambda e: e.tensor_reduce(out=scr2[:, 13:14], in_=scr2[:, 0:8], axis=AX.X, op=ALU.max),
         reads=[(scr2, 'v8')], writes=[(scr2, 'c13')])
    p.op('vector', lambda e: e.tensor_scalar(out=scr2[:, 0:8], in0=scr1, scalar1=scr2[:, 13:14], scalar2=None, op0=ALU.is_ge),
         reads=[scr1, (scr2, 'c13')], writes=[(scr2, 'v8')])
    p.op('vector', lambda e: e.tensor_tensor(cwt, cwt, scr2[:, 0:8], ALU.mult),
         reads=[cwt, (scr2, 'v8')], writes=[cwt])
    p.op('vector', lambda e: e.tensor_copy(rb['cwb'], cwt), reads=[cwt], writes=[rb['cwb']])
    pb = cx.psum_bf[0]
    p.op('tensor', lambda e, pb=pb: e.transpose(pb[0:8, :], rb['cwb'], cx.consts['ident_bf']),
         reads=[rb['cwb'], cx.consts['ident_bf']], writes=[pb])
    p.op('vector', lambda e, ti=ti, pb=pb: e.tensor_copy(cw8[:, ti * 128:(ti + 1) * 128], pb[0:8, :]),
         reads=[pb], writes=[(cw8, f't{ti}')])


def build(cfg):
    nc = bass.Bass('TRN2', target_bir_lowering=False)
    stop_after = cfg.get('stop_after')
    dbg_names = set(cfg.get('dbg', ()))

    dram = {}
    for k, shape in cfg['g_shapes'].items():
        dram[k] = nc.dram_tensor('g_' + k, list(shape), F32, kind='ExternalInput').ap()
    dram['x'] = nc.dram_tensor('x', [T, D], F32, kind='ExternalInput').ap()
    out = nc.dram_tensor('out', [T, D], F32, kind='ExternalOutput').ap()

    scrb = {}
    scrb['sz'] = nc.dram_tensor('scrb_sz', [DI, T], BF16, kind='Internal').ap()
    scrb['xc'] = nc.dram_tensor('scrb_xc', [DI, T], BF16, kind='Internal').ap()
    scrb['s0'] = nc.dram_tensor('scrb_s0', [D, T], BF16, kind='Internal').ap()
    x1_scr = nc.dram_tensor('scr_x1', [T, D], F32, kind='Internal').ap()
    s1_scr = nc.dram_tensor('scr_s1', [T, D], BF16, kind='Internal').ap()

    dbg = {}

    def mkdbg(name, shape, dt=F32):
        if name in dbg_names:
            dbg[name] = nc.dram_tensor('dbg_' + name, list(shape), dt, kind='ExternalOutput').ap()
            return dbg[name]
        return None

    stk = ExitStack()
    p = P(nc)
    pool = Pool(nc, stk, POOL_BYTES)
    cx = Ctx(nc, p, pool, dram)
    for i in range(6):
        cx.psum.append(stk.enter_context(nc.psum_tensor(f'ps{i}', [128, NB], F32)))
    cx.psum_bf.append(stk.enter_context(nc.psum_tensor('psbf0', [128, 128], BF16)))
    cx.psum_bf.append(stk.enter_context(nc.psum_tensor('psbf1', [128, 128], BF16)))

    load_const_tiles(cx)

    def done():
        blk = stk.enter_context(nc.Block())
        p.finish_and_flush(blk)
        stk.close()
        p.stack.close()
        return nc, dbg

    def dump_list(lst, name, width, dt=BF16):
        dd = mkdbg(name, (128, len(lst) * width), dt)
        if dd is not None:
            for i, t in enumerate(lst):
                p.dma('sync', lambda e, t=t, i=i: e.dma_start(out=dd[:, i * width:(i + 1) * width], in_=t),
                      reads=[(t, None)])

    cw8 = pool.alloc((8, T), BF16, 'cw8')
    xn_hi = pool.alloc_list(8, (128, T), BF16, 'xn_hi')
    xn_lo = pool.alloc_list(8, (128, T), BF16, 'xn_lo')
    lb = {k: pool.alloc(s, dt, 'a1_' + k) for k, s, dt in (
        ('xt', (128, D), F32), ('xsq', (128, D), F32), ('xnf', (128, D), F32),
        ('xh', (128, D), BF16), ('xl', (128, D), BF16), ('st', (128, 16), F32))}
    rb = {'cwt': pool.alloc((128, 8), F32, 'a1_cwt'),
          'scr1': pool.alloc((128, 8), F32, 'a1_scr1'),
          'scr2': pool.alloc((128, 16), F32, 'a1_scr2'),
          'cwb': pool.alloc((128, 8), BF16, 'a1_cwbf')}
    gw32 = pool.alloc((128, 8, 8), F32, 'a1_gw32')
    p.dma('sync', lambda e: e.dma_start(out=gw32, in_=dram['gate_wT'].rearrange('(kc p) m -> p kc m', p=128)),
          writes=[gw32])
    gwhi = pool.alloc((128, 8, 8), BF16, 'a1_gwhi')
    gwlo = pool.alloc((128, 8, 8), BF16, 'a1_gwlo')
    gw32f = gw32.rearrange('p a b -> p (a b)')
    gwhif = gwhi.rearrange('p a b -> p (a b)')
    gwlof = gwlo.rearrange('p a b -> p (a b)')
    p.op('vector', lambda e: e.tensor_copy(gwhif, gw32f), reads=[gw32], writes=[gwhi])
    p.op('vector', lambda e: e.tensor_tensor(gwlof, gw32f, gwhif, ALU.subtract), reads=[gw32, gwhi], writes=[gwlo])

    def x_src(ti):
        p.dma('sync', lambda e, ti=ti: e.dma_start(out=lb['xt'], in_=dram['x'][ti * 128:(ti + 1) * 128, :]),
              writes=[(lb['xt'], 'x1')])

    for ti in range(16):
        emit_ln(cx, p, pool, x_src, xn_hi, xn_lo, lb, ti,
                route_fn=lambda ti, xh, xl: emit_l0_route(cx, ti, xn_hi, xn_lo, gwhi, gwlo, cw8, rb))
    p.barrier()
    if stop_after == 'A1':
        dump_list(xn_hi, 'xn_hi', T)
        dump_list(xn_lo, 'xn_lo', T)
        d2 = mkdbg('cw8', (8, T), BF16)
        if d2 is not None:
            p.dma('sync', lambda e: e.dma_start(out=d2, in_=cw8), reads=[(cw8, None)])
        return done()
    for k in list(lb):
        pool.free('a1_' + k)
    for k in ('cwt', 'scr1', 'scr2', 'cwbf', 'gw32', 'gwhi', 'gwlo'):
        pool.free('a1_' + k)
    pool.free_listed('xn_lo', 8)
    pool.new_epoch()

    import os as _os2
    if _os2.environ.get('A2_SKIPW', '0') == '1':
        w_in = cx.pool.alloc_list(8, (128, 4096), BF16, 'w_in')
    else:
        w_in = load_weight_fm(cx, dram['w_inT'], 8, 4096, 'w_in')
    xi = pool.alloc_list(16, (128, T + 4), BF16, 'xi')
    if _os2.environ.get('A2_SKIPMS', '0') != '1':
        for gi in range(16):
            p.op('vector', lambda e, gi=gi: e.memset(xi[gi][:, 0:4], 0.0), writes=[xi[gi]])
    sz_t = pool.alloc((128, NB), BF16, 'sz_t')
    import os as _os
    _mlim = int(_os.environ.get('A2_MLIM', '32'))
    _skipsz = _os.environ.get('A2_SKIPSZ', '0') == '1'
    for m in range(_mlim):
        for n in range(T // NB):
            ps = cx.psum[m % 2]
            pairs = [(w_in[kc][:, m * 128:(m + 1) * 128],
                      xn_hi[kc][:, n * NB:(n + 1) * NB], [w_in[kc], (xn_hi[kc], None)]) for kc in range(8)]
            mm_group(cx, ps[:, :], pairs, key=ps)
            if m < 16:
                p.op('vector', lambda e, m=m, n=n, ps=ps: e.tensor_copy(
                    xi[m][:, 3 + n * NB: 3 + n * NB + NB], ps[:, :]),
                    reads=[ps], writes=[xi[m]])
            else:
                gi = m - 16
                p.op('scalar', lambda e, ps=ps: e.activation(out=sz_t, in_=ps[:, :], func=AF.Silu),
                     reads=[ps], writes=[sz_t])
                if not _skipsz:
                    p.dma('gpsimd', lambda e, gi=gi, n=n: e.dma_start(
                        out=scrb['sz'][gi * 128:(gi + 1) * 128, n * NB:(n + 1) * NB], in_=sz_t),
                        reads=[sz_t], writes=[(scrb['sz'], f'g{gi}')])
    p.barrier()
    if stop_after == 'A2':
        dump_list(xi, 'xi', T + 4)
        return done()
    pool.free_listed('w_in', 8)
    pool.free('sz_t')
    pool.new_epoch()

    conv_w = pool.alloc((128, 16, 4), F32, 'conv_w')
    conv_b = pool.alloc((128, 16), F32, 'conv_b')
    p.dma('sync', lambda e: e.dma_start(out=conv_w.rearrange('p a b -> p (a b)'), in_=dram['conv_w_pp']), writes=[conv_w])
    p.dma('sync', lambda e: e.dma_start(out=conv_b, in_=dram['conv_b_pp']), writes=[conv_b])
    xcv = pool.alloc_list(16, (128, T), BF16, 'xcv')
    cscr = pool.alloc((128, T), F32, 'cscr')
    for gi in range(16):
        p.op('vector', lambda e, gi=gi: e.tensor_scalar(
            out=cscr, in0=xi[gi][:, 3:3 + T], scalar1=conv_w[:, gi, 3:4], scalar2=None, op0=ALU.mult),
            reads=[xi[gi], conv_w], writes=[cscr])
        for j in range(3):
            p.op('vector', lambda e, gi=gi, j=j: e.scalar_tensor_tensor(
                out=cscr, in0=xi[gi][:, j:j + T], scalar=conv_w[:, gi, j:j + 1], in1=cscr,
                op0=ALU.mult, op1=ALU.add),
                reads=[xi[gi], conv_w, cscr], writes=[cscr])
        p.op('scalar', lambda e, gi=gi: e.activation(
            out=xcv[gi], in_=cscr, func=AF.Silu, bias=conv_b[:, gi:gi + 1]),
            reads=[cscr, conv_b], writes=[xcv[gi]])
        p.dma('gpsimd', lambda e, gi=gi: e.dma_start(
            out=scrb['xc'][gi * 128:(gi + 1) * 128, :], in_=xcv[gi]),
            reads=[xcv[gi]], writes=[(scrb['xc'], f'g{gi}')])
    p.barrier()
    pool.free_listed('xi', 16)
    pool.free('cscr')
    pool.new_epoch()
    xpw_dt = load_weight_fm(cx, dram['xpw_dtT'], 16, 64, 'xpw_dt')
    xpw_B = load_weight_fm(cx, dram['xpw_BT'], 16, 16, 'xpw_B')
    xpw_C = load_weight_fm(cx, dram['xpw_CT'], 16, 16, 'xpw_C')
    dtr_f = pool.alloc((64, T), BF16, 'dtr_f')
    B_f = pool.alloc((16, T), BF16, 'B_f')
    C_f = pool.alloc((16, T), BF16, 'C_f')
    for n in range(T // NB):
        for (wt, rows, dst) in ((xpw_dt, 64, dtr_f), (xpw_B, 16, B_f), (xpw_C, 16, C_f)):
            ps = cx.psum[3]
            pairs = [(wt[kc][:, :rows], xcv[kc][:, n * NB:(n + 1) * NB], [wt[kc], xcv[kc]]) for kc in range(16)]
            mm_group(cx, ps[:rows, :], pairs, key=ps)
            p.op('vector', lambda e, ps=ps, dst=dst, rows=rows, n=n: e.tensor_copy(
                dst[:rows, n * NB:(n + 1) * NB], ps[:rows, :]),
                reads=[ps], writes=[(dst, f'n{n}')])
    p.barrier()
    pool.free_listed('xcv', 16)
    pool.free_listed('xpw_dt', 16); pool.free_listed('xpw_B', 16); pool.free_listed('xpw_C', 16)
    pool.free('conv_w'); pool.free('conv_b')
    pool.new_epoch()
    if stop_after == 'A3':
        for nm, tt, rows in (('dtr_f', dtr_f, 64), ('B_f', B_f, 16), ('C_f', C_f, 16)):
            dd = mkdbg(nm, (rows, T), BF16)
            if dd is not None:
                p.dma('sync', lambda e, dd=dd, tt=tt, rows=rows: e.dma_start(out=dd, in_=tt[:rows, :]), reads=[(tt, None)])
        return done()

    emit_scan(cx, dtr_f, B_f, C_f, scrb)
    p.barrier()
    for nm in ('dtr_f', 'B_f', 'C_f'):
        pool.free(nm)
    pool.new_epoch()
    if stop_after == 'B':
        dd = mkdbg('s0', (D, T), BF16)
        if dd is not None:
            sload = pool.alloc((128, T), BF16, 'dbg_s0')
            for md in range(8):
                p.dma('sync', lambda e, md=md: e.dma_start(out=sload, in_=scrb['s0'][md * 128:(md + 1) * 128, :]),
                      reads=[(scrb['s0'], None)], writes=[sload])
                p.dma('sync', lambda e, md=md: e.dma_start(out=dd[md * 128:(md + 1) * 128, :], in_=sload),
                      reads=[sload], writes=[(dd, f'm{md}')])
        return done()

    y_acc = pool.alloc_list(8, (128, T), F32, 'y_acc')
    emit_l0ffn(cx, xn_hi, cw8, y_acc)
    p.barrier()
    pool.new_epoch()
    if stop_after == 'C':
        dump_list(y_acc, 'y_acc', T, F32)
        return done()

    cw1_8 = pool.alloc((8, T), BF16, 'cw1_8')
    emit_combine_ln(cx, y_acc, scrb['s0'], xn_hi, cw1_8, x1_scr)
    p.barrier()
    pool.new_epoch()
    if stop_after == 'D':
        dump_list(xn_hi, 'xn1_hi', T)
        d2 = mkdbg('cw1_8', (8, T), BF16)
        if d2 is not None:
            p.dma('sync', lambda e: e.dma_start(out=d2, in_=cw1_8), reads=[(cw1_8, None)])
        return done()

    s1 = pool.alloc_list(16, (128, D), BF16, 's1')
    emit_l1attn(cx, xn_hi, s1)
    for ti in range(16):
        p.dma('gpsimd', lambda e, ti=ti: e.dma_start(out=s1_scr[ti * 128:(ti + 1) * 128, :], in_=s1[ti]),
              reads=[s1[ti]], writes=[(s1_scr, f't{ti}')])
    p.barrier()
    if stop_after == 'E':
        dump_list(s1, 's1', D)
        return done()
    pool.free_listed('s1', 16)
    pool.new_epoch()

    y1 = pool.alloc_list(8, (128, T), F32, 'y1')
    emit_l1ffn(cx, xn_hi, cw1_8, y1)
    p.barrier()
    pool.new_epoch()

    emit_final(cx, y1, s1_scr, x1_scr, out)
    return done()


def emit_scan(cx, dtr_f, B_f, C_f, scrb):
    p, pool = cx.p, cx.pool
    L = LSC
    ones = cx.consts['ones_bf']
    dtw_t = pool.alloc((64, DI), BF16, 'dtw_t')
    p.dma('gpsimd', lambda e: e.dma_start(out=dtw_t, in_=cx.dram['dtw_T']), writes=[dtw_t])
    dtb = pool.alloc((128, 16), F32, 'dtb')
    p.dma('sync', lambda e: e.dma_start(out=dtb, in_=cx.dram['dtb_pp']), writes=[dtb])
    alog = pool.alloc((128, 16 * DS), F32, 'alog')
    p.dma('sync', lambda e: e.dma_start(out=alog, in_=cx.dram['Alog_pp']), writes=[alog])
    A_pp = pool.alloc((128, 16 * DS), F32, 'A_pp')
    p.op('scalar', lambda e: e.activation(out=A_pp, in_=alog, func=AF.Exp), reads=[alog], writes=[(A_pp, 'e')])
    p.op('vector', lambda e: e.tensor_scalar_mul(A_pp, A_pp, -1.0), reads=[(A_pp, 'e')], writes=[A_pp])
    D_pp = pool.alloc((128, 16), F32, 'D_pp')
    p.dma('sync', lambda e: e.dma_start(out=D_pp, in_=cx.dram['D_pp']), writes=[D_pp])
    w_out = load_weight_fm(cx, cx.dram['w_outT'], 16, D, 'w_out')

    dA = pool.alloc((128, DS * L), F32, 'dA')
    u = pool.alloc((128, DS * L), F32, 'u')
    h = pool.alloc((128, DS * L), F32, 'h')
    dtg = pool.alloc((128, L), F32, 'dtg')
    dtx = pool.alloc((128, L), F32, 'dtx')
    spx = pool.alloc((128, L), F32, 'spx')
    yg = pool.alloc((128, L), F32, 'yg')
    hc = pool.alloc((128, 16, DS), F32, 'hc')
    xcg = pool.alloc_list(16, (128, L), BF16, 'xcg')
    szg = pool.alloc_list(16, (128, L), BF16, 'szg')
    ygate = pool.alloc_list(16, (128, L), BF16, 'ygate')
    bflat = pool.alloc((1, DS * L), BF16, 'bflat')
    cflat = pool.alloc((1, DS * L), BF16, 'cflat')
    B_rep = pool.alloc((128, DS * L), BF16, 'B_rep')
    C_rep = pool.alloc((128, DS * L), BF16, 'C_rep')
    sout = pool.alloc((128, L), BF16, 'sout')
    for gi in range(16):
        p.op('vector', lambda e, gi=gi: e.memset(hc[:, gi, :], 0.0), writes=[(hc, f'g{gi}')])

    NCH = T // L
    for tc in range(NCH):
        t0 = tc * L
        p.dma('sync', lambda e, t0=t0: e.dma_start(out=bflat, in_=B_f[:, t0:t0 + L]),
              reads=[(B_f, None)], writes=[bflat])
        p.dma('sync', lambda e, t0=t0: e.dma_start(out=cflat, in_=C_f[:, t0:t0 + L]),
              reads=[(C_f, None)], writes=[cflat])
        for (flat, rep) in ((bflat, B_rep), (cflat, C_rep)):
            for nb in range(DS * L // NB):
                ps = cx.psum[5]
                mm_group(cx, ps[:, :], [(ones[:, 0:128], flat[:, nb * NB:(nb + 1) * NB], [flat, ones])], key=ps)
                p.op('scalar', lambda e, ps=ps, rep=rep, nb=nb: e.activation(
                    out=rep[:, nb * NB:(nb + 1) * NB], in_=ps[:, :], func=AF.Copy),
                    reads=[ps], writes=[(rep, f'n{nb}')])
        rep_reads_B = [(B_rep, f'n{nb}') for nb in range(DS * L // NB)]
        rep_reads_C = [(C_rep, f'n{nb}') for nb in range(DS * L // NB)]
        for gi in range(16):
            p.dma('sync', lambda e, gi=gi, t0=t0: e.dma_start(
                out=xcg[gi], in_=scrb['xc'][gi * 128:(gi + 1) * 128, t0:t0 + L]),
                reads=[(scrb['xc'], f'g{gi}')], writes=[xcg[gi]])
            p.dma('sync', lambda e, gi=gi, t0=t0: e.dma_start(
                out=szg[gi], in_=scrb['sz'][gi * 128:(gi + 1) * 128, t0:t0 + L]),
                reads=[(scrb['sz'], f'g{gi}')], writes=[szg[gi]])
        for gi in range(16):
            ps = cx.psum[4]
            mm_group(cx, ps[:, :L], [(dtw_t[:, gi * 128:(gi + 1) * 128], dtr_f[:, t0:t0 + L],
                                      [dtw_t, (dtr_f, None)])], key=ps)
            p.op('scalar', lambda e, gi=gi, ps=ps: e.activation(out=spx, in_=ps[:, :L], func=AF.Abs, bias=dtb[:, gi:gi + 1]),
                 reads=[ps, dtb], writes=[spx])
            p.op('scalar', lambda e: e.activation(out=spx, in_=spx, func=AF.Exp, scale=-1.0),
                 reads=[spx], writes=[spx])
            p.op('scalar', lambda e: e.activation(out=spx, in_=spx, func=AF.Ln, bias=cx.consts['one'][:, 0:1]),
                 reads=[spx, cx.consts['one']], writes=[spx])
            p.op('scalar', lambda e, gi=gi, ps=ps: e.activation(out=dtg, in_=ps[:, :L], func=AF.Relu, bias=dtb[:, gi:gi + 1]),
                 reads=[ps, dtb], writes=[dtg])
            p.op('vector', lambda e: e.tensor_tensor(dtg, dtg, spx, ALU.add),
                 reads=[dtg, spx], writes=[dtg])
            p.op('vector', lambda e, gi=gi: e.tensor_tensor(dtx, dtg, xcg[gi], ALU.mult),
                 reads=[dtg, xcg[gi]], writes=[dtx])
            for s in range(DS):
                p.op('scalar', lambda e, gi=gi, s=s: e.activation(
                    out=dA[:, s * L:(s + 1) * L], in_=dtg, func=AF.Exp, scale=A_pp[:, gi * DS + s: gi * DS + s + 1]),
                    reads=[dtg, A_pp], writes=[(dA, f's{s}')])
            dA_reads = [(dA, f's{s}') for s in range(DS)]
            H = DS // 4
            for hs in range(4):
                sl = slice(hs * H * L, (hs + 1) * H * L)
                p.op('vector', lambda e, sl=sl: e.tensor_tensor(
                    u[:, sl].rearrange('p (s l) -> p s l', s=H),
                    dtx[:, None, :].broadcast_to([128, H, L]),
                    B_rep[:, sl].rearrange('p (s l) -> p s l', s=H),
                    ALU.mult),
                    reads=[dtx] + rep_reads_B, writes=[(u, f'h{hs}')])
            if tc > 0:
                p.op('vector', lambda e, gi=gi: e.tensor_tensor(dtx[:, 0:DS], dA[:, 0::L], hc[:, gi, :], ALU.mult),
                     reads=dA_reads + [(hc, f'g{gi}')], writes=[dtx])
                p.op('vector', lambda e: e.tensor_tensor(u[:, 0::L], dtx[:, 0:DS], u[:, 0::L], ALU.add),
                     reads=[dtx, (u, None)], writes=[(u, None)])
            p.op('vector', lambda e: e.memset(dA[:, 0::L], 0.0), writes=[(dA, None)])
            for hs in range(4):
                sl = slice(hs * H * L, (hs + 1) * H * L)
                dtags = [(dA, f's{s}') for s in range(hs * H, (hs + 1) * H)]
                p.op('vector', lambda e, sl=sl: e.tensor_tensor_scan(
                    h[:, sl], dA[:, sl], u[:, sl], 0.0, ALU.mult, ALU.add),
                    reads=dtags + [(u, f'h{hs}')], writes=[(h, f'h{hs}')])
            p.op('vector', lambda e, gi=gi: e.tensor_copy(hc[:, gi, :], h[:, L - 1::L]),
                 reads=[(h, f'h{q}') for q in range(4)], writes=[(hc, f'g{gi}')])
            for hs in range(4):
                sl = slice(hs * H * L, (hs + 1) * H * L)
                wtags = [(dA, f's{s}') for s in range(hs * H, (hs + 1) * H)]
                p.op('vector', lambda e, sl=sl: e.tensor_tensor(
                    dA[:, sl].rearrange('p (s l) -> p s l', s=H),
                    h[:, sl].rearrange('p (s l) -> p s l', s=H),
                    C_rep[:, sl].rearrange('p (s l) -> p s l', s=H),
                    ALU.mult),
                    reads=[(h, f'h{hs}')] + rep_reads_C, writes=wtags)
            for w in (8, 4, 2, 1):
                rtags = [(dA, f's{s}') for s in range(2 * w)]
                wt = [(dA, f's{s}') for s in range(w)]
                p.op('vector', lambda e, w=w: e.tensor_tensor(
                    dA[:, :w * L], dA[:, :w * L], dA[:, w * L:2 * w * L], ALU.add),
                    reads=rtags, writes=wt)
            p.op('vector', lambda e, gi=gi: e.scalar_tensor_tensor(
                out=yg, in0=xcg[gi], scalar=D_pp[:, gi:gi + 1], in1=dA[:, :L], op0=ALU.mult, op1=ALU.add),
                reads=[xcg[gi], D_pp, (dA, 's0')], writes=[yg])
            p.op('vector', lambda e, gi=gi: e.tensor_tensor(ygate[gi], yg, szg[gi], ALU.mult),
                 reads=[yg, szg[gi]], writes=[ygate[gi]])
        for md in range(8):
            ps = cx.psum[5]
            pairs = [(w_out[kc][:, md * 128:(md + 1) * 128], ygate[kc], [w_out[kc], ygate[kc]])
                     for kc in range(16)]
            mm_group(cx, ps[:, :L], pairs, key=ps)
            p.op('scalar', lambda e, ps=ps: e.activation(out=sout, in_=ps[:, :L], func=AF.Copy),
                 reads=[ps], writes=[sout])
            p.dma('gpsimd', lambda e, md=md, t0=t0: e.dma_start(
                out=scrb['s0'][md * 128:(md + 1) * 128, t0:t0 + L], in_=sout),
                reads=[sout], writes=[(scrb['s0'], f'm{md}c{t0}')])
    for nm in ('dtw_t', 'dtb', 'alog', 'A_pp', 'D_pp', 'dA', 'u', 'h', 'dtg', 'dtx',
               'spx', 'yg', 'hc', 'bflat', 'cflat', 'B_rep', 'C_rep', 'sout'):
        pool.free(nm)
    pool.free_listed('w_out', 16)
    for nm in ('xcg', 'szg', 'ygate'):
        pool.free_listed(nm, 16)


def emit_l0ffn(cx, xn_hi, cw8, y_acc):
    p, pool = cx.p, cx.pool
    ones = cx.consts['ones_bf']
    d = cx.dram
    for md in range(8):
        p.op('vector', lambda e, md=md: e.memset(y_acc[md], 0.0), writes=[y_acc[md]])
    b2t = pool.alloc((8, D), BF16, 'b2t')
    p.dma('gpsimd', lambda e: e.dma_start(out=b2t, in_=d['b2']), writes=[b2t])
    for md in range(8):
        for n in range(T // NB):
            ps = cx.psum[5]
            mm_group(cx, ps[:, :], [(b2t[:, md * 128:(md + 1) * 128], cw8[:, n * NB:(n + 1) * NB],
                                     [b2t, (cw8, None)])], key=ps)
            p.op('vector', lambda e, md=md, n=n, ps=ps: e.tensor_tensor(
                y_acc[md][:, n * NB:(n + 1) * NB], y_acc[md][:, n * NB:(n + 1) * NB], ps[:, :], ALU.add),
                reads=[ps, y_acc[md]], writes=[y_acc[md]])
    cwb = pool.alloc((128, NB), BF16, 'cwb')
    ev = {k: pool.alloc((128, NB), F32, 'l0_' + k) for k in ('hg', 'h3g', 'hl', 'h3l')}
    ev['sg'] = ev['h3g']
    ab4p = [pool.alloc_list(4, (128, NB), BF16, f'ab4_{pr}') for pr in range(2)]
    wpar = []
    for par in range(2):
        wpar.append({m: pool.alloc_list(8, (128, 512), BF16, f'l0w_{par}_{m}')
                     for m in ('w1g', 'w1l', 'w3g', 'w3l')})
    w2bp = [pool.alloc_list(4, (128, 1024), BF16, f'l0w2_{pr}') for pr in range(2)]
    bgs4 = pool.alloc((1, 4 * 512), BF16, 'l0bgs')
    brow = {'bg1': 0, 'bl1': 1, 'bg3': 2, 'bl3': 3}
    bgs = {m: bgs4[0:1, r * 512:(r + 1) * 512] for m, r in brow.items()}

    def load_half(par, wsrcs, hh):
        for mi, m in enumerate(('w1g', 'w1l', 'w3g', 'w3l')):
            for kc in range(8):
                t = wpar[par][m][kc]
                p.dma('gpsimd', lambda e, t=t, src=wsrcs[mi], hh=hh, kc=kc: e.dma_start(
                    out=t, in_=src[kc * 128:(kc + 1) * 128, hh * 512:(hh + 1) * 512]), writes=[t])

    def load_w2(par, w2src, hh):
        for kc in range(4):
            t = w2bp[par][kc]
            p.dma('gpsimd', lambda e, t=t, src=w2src, hh=hh, kc=kc: e.dma_start(
                out=t, in_=src[hh * 512 + kc * 128: hh * 512 + (kc + 1) * 128, :]), writes=[t])

    passes = []
    for ei in range(E):
        passes.append((ei, (d['w1gT'][ei], d['w1lT'][ei], d['w3gT'][ei], d['w3lT'][ei]),
                       (d['b1g'][ei:ei + 1], d['b1l'][ei:ei + 1], d['b3g'][ei:ei + 1], d['b3l'][ei:ei + 1]),
                       d['w2T'][ei], True))
    passes.append((0, (d['sw1gT'], d['sw1lT'], d['sw3gT'], d['sw3lT']),
                   (d['sb1g'], d['sb1l'], d['sb3g'], d['sb3l']), d['sw2T'], False))

    halves = [(pi, hh) for pi in range(len(passes)) for hh in range(2)]
    load_half(0, passes[0][1], 0)
    abi = 0
    pending = None

    def emit_w2(w2_par, abp, n):
        for md in range(8):
            ps = cx.psum[4 + md % 2]
            pairs = [(w2bp[w2_par][kc][:, md * 128:(md + 1) * 128], ab4p[abp][kc],
                      [w2bp[w2_par][kc], ab4p[abp][kc]]) for kc in range(4)]
            mm_group(cx, ps[:, :], pairs, key=ps)
            p.op('vector', lambda e, md=md, n=n, ps=ps: e.tensor_tensor(
                y_acc[md][:, n * NB:(n + 1) * NB], y_acc[md][:, n * NB:(n + 1) * NB], ps[:, :], ALU.add),
                reads=[ps, y_acc[md]], writes=[y_acc[md]])

    for idx, (pi, hh) in enumerate(halves):
        ei, wsrcs, bsrcs, w2src, scaled = passes[pi]
        par = idx % 2
        if idx + 1 < len(halves):
            pi2, hh2 = halves[idx + 1]
            load_half(1 - par, passes[pi2][1], hh2)
        load_w2(par, w2src, hh)
        for m, srcb in (('bg1', bsrcs[0]), ('bl1', bsrcs[1]), ('bg3', bsrcs[2]), ('bl3', bsrcs[3])):
            p.dma('gpsimd', lambda e, t=bgs[m], src=srcb, hh=hh: e.dma_start(
                out=t, in_=src[:, hh * 512:(hh + 1) * 512]), writes=[bgs[m]])
        W = wpar[par]
        for n in range(T // NB):
            if scaled:
                ps = cx.psum[5]
                mm_group(cx, ps[:, :], [(cx.consts['sel8_bf'][:, ei * 128:(ei + 1) * 128],
                                         cw8[:, n * NB:(n + 1) * NB], [cx.consts['sel8_bf'], (cw8, None)])], key=ps)
                p.op('scalar', lambda e, ps=ps: e.activation(out=cwb, in_=ps[:, :], func=AF.Copy),
                     reads=[ps], writes=[cwb])
            for mf in range(4):
                fa = hh * 512 + mf * 128
                for ci, (m, bm) in enumerate((('w1g', 'bg1'), ('w3g', 'bg3'), ('w1l', 'bl1'), ('w3l', 'bl3'))):
                    psb = cx.psum[ci]
                    pairs = [(W[m][kc][:, mf * 128:(mf + 1) * 128], xn_hi[kc][:, n * NB:(n + 1) * NB],
                              [W[m][kc], (xn_hi[kc], None)]) for kc in range(8)]
                    extra = [(bgs[bm][:, mf * 128:(mf + 1) * 128], ones[:, :NB], [bgs[bm], ones])]
                    mm_group(cx, psb[:, :], pairs, extra=extra, key=psb)
                p.op('scalar', lambda e: e.activation(out=ev['hg'], in_=cx.psum[0][:, :], func=AF.Copy),
                     reads=[cx.psum[0]], writes=[ev['hg']])
                p.op('scalar', lambda e: e.activation(out=ev['h3g'], in_=cx.psum[1][:, :], func=AF.Copy),
                     reads=[cx.psum[1]], writes=[ev['h3g']])
                p.op('scalar', lambda e: e.activation(out=ev['hl'], in_=cx.psum[2][:, :], func=AF.Copy),
                     reads=[cx.psum[2]], writes=[ev['hl']])
                p.op('vector', lambda e: e.tensor_copy(ev['h3l'], cx.psum[3][:, :]),
                     reads=[cx.psum[3]], writes=[ev['h3l']])
                p.op('vector', lambda e: e.tensor_tensor(ev['hg'], ev['hg'], ev['h3g'], ALU.mult),
                     reads=[ev['hg'], ev['h3g']], writes=[ev['hg']])
                p.op('vector', lambda e: e.tensor_tensor(ev['hl'], ev['hl'], ev['h3l'], ALU.mult),
                     reads=[ev['hl'], ev['h3l']], writes=[ev['hl']])
                p.op('vector', lambda e: e.tensor_scalar(out=ev['hg'], in0=ev['hg'], scalar1=7.0, scalar2=None, op0=ALU.min),
                     reads=[ev['hg']], writes=[ev['hg']])
                p.op('scalar', lambda e: e.activation(out=ev['sg'], in_=ev['hg'], func=AF.Silu, scale=1.702),
                     reads=[ev['hg']], writes=[ev['sg']])
                p.op('vector', lambda e: e.tensor_scalar(out=ev['hl'], in0=ev['hl'], scalar1=7.0, scalar2=-7.0,
                                                         op0=ALU.min, op1=ALU.max),
                     reads=[ev['hl']], writes=[ev['hl']])
                p.op('vector', lambda e: e.scalar_tensor_tensor(out=ev['hl'], in0=ev['hl'], scalar=1.0, in1=ev['sg'],
                                                                op0=ALU.add, op1=ALU.mult),
                     reads=[ev['hl'], ev['sg']], writes=[ev['hl']])
                if scaled:
                    p.op('vector', lambda e, mf=mf, ab=ab4p[abi % 2]: e.tensor_tensor(
                        ab[mf], ev['hl'], cwb, ALU.mult),
                        reads=[ev['hl'], cwb], writes=[ab4p[abi % 2][mf]])
                else:
                    p.op('vector', lambda e, mf=mf, ab=ab4p[abi % 2]: e.tensor_copy(ab[mf], ev['hl']),
                         reads=[ev['hl']], writes=[ab4p[abi % 2][mf]])
            if pending is not None:
                emit_w2(*pending)
            pending = (par, abi % 2, n)
            abi += 1
    if pending is not None:
        emit_w2(*pending)
    sb2t = b2t[0:1, :]
    p.dma('gpsimd', lambda e: e.dma_start(out=sb2t, in_=d['sb2']),
          reads=[(b2t, None)], writes=[(b2t, 'sb2')])
    for md in range(8):
        for n in range(T // NB):
            ps = cx.psum[5]
            mm_group(cx, ps[:, :], [(sb2t[:, md * 128:(md + 1) * 128], ones[:, :NB], [(b2t, 'sb2'), ones])], key=ps)
            p.op('vector', lambda e, md=md, n=n, ps=ps: e.tensor_tensor(
                y_acc[md][:, n * NB:(n + 1) * NB], y_acc[md][:, n * NB:(n + 1) * NB], ps[:, :], ALU.add),
                reads=[ps, y_acc[md]], writes=[y_acc[md]])
    for nm in ('b2t', 'cwb'):
        pool.free(nm)
    for k in ('hg', 'h3g', 'hl', 'h3l'):
        pool.free('l0_' + k)
    for pr in range(2):
        pool.free_listed(f'ab4_{pr}', 4)
        pool.free_listed(f'l0w2_{pr}', 4)
    for par in range(2):
        for m in ('w1g', 'w1l', 'w3g', 'w3l'):
            pool.free_listed(f'l0w_{par}_{m}', 8)
    pool.free('l0bgs')


def emit_combine_ln(cx, y_acc, s0_scr, xn_hi, cw1_8, x1_scr):
    p, pool = cx.p, cx.pool
    gbf = pool.alloc_list(8, (128, T), BF16, 'gbf')
    for md in range(8):
        p.op('scalar', lambda e, md=md: e.activation(out=gbf[md], in_=y_acc[md], func=AF.Silu),
             reads=[y_acc[md]], writes=[gbf[md]])
    p.barrier()
    pool.free_listed('y_acc', 8)
    pool.new_epoch()
    s0f = pool.alloc_list(8, (128, T), BF16, 's0f')
    for md in range(8):
        p.dma('sync', lambda e, md=md: e.dma_start(out=s0f[md], in_=s0_scr[md * 128:(md + 1) * 128, :]),
              reads=[(s0_scr, None)], writes=[s0f[md]])
    xn_lo = pool.alloc_list(8, (128, T), BF16, 'xn_lo1')
    wred32 = pool.alloc((128, 8, 16), F32, 'wred32')
    p.dma('sync', lambda e: e.dma_start(out=wred32, in_=cx.dram['w_red'].rearrange('(kc p) m -> p kc m', p=128)),
          writes=[wred32])
    wredh = pool.alloc((128, 8, 16), BF16, 'wredh')
    wredl = pool.alloc((128, 8, 16), BF16, 'wredl')
    wr32f = wred32.rearrange('p a b -> p (a b)')
    wrhf = wredh.rearrange('p a b -> p (a b)')
    wrlf = wredl.rearrange('p a b -> p (a b)')
    p.op('vector', lambda e: e.tensor_copy(wrhf, wr32f), reads=[wred32], writes=[wredh])
    p.op('vector', lambda e: e.tensor_tensor(wrlf, wr32f, wrhf, ALU.subtract), reads=[wred32, wredh], writes=[wredl])
    wgt = pool.alloc((8, 16), F32, 'wgt')
    p.dma('sync', lambda e: e.dma_start(out=wgt, in_=cx.dram['wg']), writes=[wgt])
    wsc = pool.alloc((8, 4), F32, 'wsc')
    wsq = pool.alloc((8, 16), F32, 'wsq')
    p.op('scalar', lambda e: e.activation(out=wsq, in_=wgt, func=AF.Square, accum_out=wsc[:, 0:1]),
         reads=[wgt], writes=[wsq, (wsc, 'c0')])
    p.op('scalar', lambda e: e.activation(out=wsc[:, 1:2], in_=wsc[:, 0:1], func=AF.Sqrt),
         reads=[(wsc, 'c0')], writes=[(wsc, 'c1')])
    p.op('vector', lambda e: e.tensor_scalar(out=wsc[:, 1:2], in0=wsc[:, 1:2], scalar1=1e-4, scalar2=None, op0=ALU.max),
         reads=[(wsc, 'c1')], writes=[(wsc, 'c1')])
    p.op('vector', lambda e: e.reciprocal(wsc[:, 2:3], wsc[:, 1:2]), reads=[(wsc, 'c1')], writes=[(wsc, 'c2')])
    p.op('vector', lambda e: e.tensor_scalar(out=wgt, in0=wgt, scalar1=wsc[:, 2:3], scalar2=None, op0=ALU.mult),
         reads=[wgt, (wsc, 'c2')], writes=[wgt])
    wgh = pool.alloc((8, 16), BF16, 'wgh')
    wgl = pool.alloc((8, 16), BF16, 'wgl')
    p.op('vector', lambda e: e.tensor_copy(wgh, wgt), reads=[wgt], writes=[wgh])
    p.op('vector', lambda e: e.tensor_tensor(wgl, wgt, wgh, ALU.subtract), reads=[wgt, wgh], writes=[wgl])
    wghT = pool.alloc((16, 8), BF16, 'wghT')
    wglT = pool.alloc((16, 8), BF16, 'wglT')
    for (src, dst) in ((wgh, wghT), (wgl, wglT)):
        pb = cx.psum_bf[1]
        p.op('tensor', lambda e, src=src, pb=pb: e.transpose(pb[0:16, 0:8], src, cx.consts['ident_bf'][0:8, 0:8]),
             reads=[src, cx.consts['ident_bf']], writes=[pb])
        p.op('vector', lambda e, dst=dst, pb=pb: e.tensor_copy(dst, pb[0:16, 0:8]), reads=[pb], writes=[dst])

    lb = {k: pool.alloc(s, dt, 'd_' + k) for k, s, dt in (
        ('xt', (128, D), F32), ('xsq', (128, D), F32), ('xnf', (128, D), F32),
        ('xh', (128, D), BF16), ('xl', (128, D), BF16), ('st', (128, 16), F32))}
    gt = pool.alloc((128, D), BF16, 'd_gt')
    smt = pool.alloc((128, D), BF16, 'd_smt')
    redt = pool.alloc((128, 16), F32, 'd_redt')
    redh = pool.alloc((128, 16), BF16, 'd_redh')
    redl = pool.alloc((128, 16), BF16, 'd_redl')
    rht = pool.alloc((16, 128), BF16, 'd_rht')
    rlt = pool.alloc((16, 128), BF16, 'd_rlt')
    lg = pool.alloc((128, 8), F32, 'd_lg')
    r1 = pool.alloc((128, 16), F32, 'd_r1')
    r2 = pool.alloc((128, 8), F32, 'd_r2')
    cwbf = pool.alloc((128, 8), BF16, 'd_cwbf')

    def x_src(ti):
        for md in range(8):
            pb = cx.psum_bf[md % 2]
            p.op('tensor', lambda e, md=md, ti=ti, pb=pb: e.transpose(
                pb[:, :], gbf[md][:, ti * 128:(ti + 1) * 128], cx.consts['ident_bf']),
                reads=[gbf[md]], writes=[pb])
            p.op('scalar', lambda e, md=md, pb=pb: e.activation(out=gt[:, md * 128:(md + 1) * 128], in_=pb[:, :], func=AF.Copy),
                 reads=[pb], writes=[(gt, f'm{md}')])
            pb2 = cx.psum_bf[(md + 1) % 2]
            p.op('tensor', lambda e, md=md, ti=ti, pb2=pb2: e.transpose(
                pb2[:, :], s0f[md][:, ti * 128:(ti + 1) * 128], cx.consts['ident_bf']),
                reads=[s0f[md]], writes=[pb2])
            p.op('scalar', lambda e, md=md, pb2=pb2: e.activation(out=smt[:, md * 128:(md + 1) * 128], in_=pb2[:, :], func=AF.Copy),
                 reads=[pb2], writes=[(smt, f'm{md}')])
        p.dma('sync', lambda e, ti=ti: e.dma_start(out=lb['xnf'], in_=cx.dram['x'][ti * 128:(ti + 1) * 128, :]),
              writes=[(lb['xnf'], 'x0')])
        greads = [(gt, f'm{m}') for m in range(8)] + [(smt, f'm{m}') for m in range(8)]
        p.op('vector', lambda e: e.tensor_tensor(lb['xt'], gt, smt, ALU.mult), reads=greads, writes=[(lb['xt'], 'x1')])
        p.op('vector', lambda e: e.tensor_tensor(lb['xt'], lb['xt'], lb['xnf'], ALU.add),
             reads=[(lb['xt'], 'x1'), (lb['xnf'], 'x0')], writes=[(lb['xt'], 'x1')])
        p.dma('sync', lambda e, ti=ti: e.dma_start(out=x1_scr[ti * 128:(ti + 1) * 128, :], in_=lb['xt']),
              reads=[(lb['xt'], 'x1')], writes=[(x1_scr, f't{ti}')])

    def route1(ti, xh, xl):
        ps = cx.psum[4]
        pairs = []
        for kc in range(8):
            for (xa, ga) in ((xn_hi, wredh), (xn_hi, wredl), (xn_lo, wredh), (xn_lo, wredl)):
                pairs.append((xa[kc][:, ti * 128:(ti + 1) * 128], ga[:, kc, :],
                              [(xa[kc], f't{ti}'), ga]))
        mm_group(cx, ps[:, 0:16], pairs, key=ps)
        p.op('vector', lambda e, ps=ps: e.tensor_copy(redt, ps[:, 0:16]), reads=[ps], writes=[redt])
        p.op('vector', lambda e: e.tensor_copy(redh, redt), reads=[redt], writes=[redh])
        p.op('vector', lambda e: e.tensor_tensor(redl, redt, redh, ALU.subtract), reads=[redt, redh], writes=[redl])
        for (src, dst) in ((redh, rht), (redl, rlt)):
            pb = cx.psum_bf[1]
            p.op('tensor', lambda e, src=src, pb=pb: e.transpose(pb[0:16, :], src, cx.consts['ident_bf']),
                 reads=[src], writes=[pb])
            p.op('vector', lambda e, dst=dst, pb=pb: e.tensor_copy(dst, pb[0:16, :]), reads=[pb], writes=[dst])
        ps2 = cx.psum[5]
        pairs2 = [(rht, wghT, [rht, wghT]), (rht, wglT, [rht, wglT]),
                  (rlt, wghT, [rlt, wghT]), (rlt, wglT, [rlt, wglT])]
        mm_group(cx, ps2[:, 0:8], pairs2, key=ps2)
        p.op('vector', lambda e, ps2=ps2: e.tensor_copy(lg, ps2[:, 0:8]), reads=[ps2], writes=[lg])
        p.op('vector', lambda e: e.tensor_reduce(out=r1[:, 0:1], in_=lg, axis=AX.X, op=ALU.max),
             reads=[lg], writes=[(r1, 'c0')])
        p.op('vector', lambda e: e.tensor_scalar_mul(r1[:, 1:2], r1[:, 0:1], -1.0), reads=[(r1, 'c0')], writes=[(r1, 'c1')])
        p.op('scalar', lambda e: e.activation(out=r2, in_=lg, func=AF.Exp, bias=r1[:, 1:2], accum_out=r1[:, 2:3]),
             reads=[lg, (r1, 'c1')], writes=[r2, (r1, 'c2')])
        p.op('vector', lambda e: e.reciprocal(r1[:, 3:4], r1[:, 2:3]), reads=[(r1, 'c2')], writes=[(r1, 'c3')])
        p.op('vector', lambda e: e.tensor_scalar(out=r2, in0=lg, scalar1=r1[:, 0:1], scalar2=None, op0=ALU.is_ge),
             reads=[lg, (r1, 'c0')], writes=[r2])
        p.op('vector', lambda e: e.tensor_scalar(out=r2, in0=r2, scalar1=r1[:, 3:4], scalar2=None, op0=ALU.mult),
             reads=[r2, (r1, 'c3')], writes=[r2])
        p.op('vector', lambda e: e.tensor_copy(cwbf, r2), reads=[r2], writes=[cwbf])
        pb = cx.psum_bf[0]
        p.op('tensor', lambda e, pb=pb: e.transpose(pb[0:8, :], cwbf, cx.consts['ident_bf']),
             reads=[cwbf], writes=[pb])
        p.op('vector', lambda e, ti=ti, pb=pb: e.tensor_copy(cw1_8[:, ti * 128:(ti + 1) * 128], pb[0:8, :]),
             reads=[pb], writes=[(cw1_8, f't{ti}')])

    for ti in range(16):
        emit_ln(cx, p, pool, x_src, xn_hi, xn_lo, lb, ti, route_fn=route1)
    p.barrier()
    for k in list(lb):
        pool.free('d_' + k)
    pool.free_listed('gbf', 8)
    pool.free_listed('s0f', 8)
    pool.free_listed('xn_lo1', 8)
    for nm in ('wred32', 'wredh', 'wredl', 'wgt', 'wsc', 'wsq', 'wgh', 'wgl',
               'wghT', 'wglT', 'd_gt', 'd_smt', 'd_redt', 'd_redh', 'd_redl', 'd_rht', 'd_rlt',
               'd_lg', 'd_r1', 'd_r2', 'd_cwbf'):
        pool.free(nm)


def emit_l1attn(cx, xn_hi, s1):
    p, pool = cx.p, cx.pool
    ones = cx.consts['ones_bf']
    wk = load_weight_fm(cx, cx.dram['WkT'], 8, D, 'wk')
    wv = load_weight_fm(cx, cx.dram['WvT'], 8, D, 'wv')
    bq = pool.alloc((1, D), BF16, 'bq'); bk = pool.alloc((1, D), BF16, 'bk'); bv = pool.alloc((1, D), BF16, 'bv')
    for t, src in ((bq, 'bq'), (bk, 'bk'), (bv, 'bv')):
        p.dma('gpsimd', lambda e, t=t, src=src: e.dma_start(out=t, in_=cx.dram[src]), writes=[t])
    K_f = pool.alloc_list(8, (128, T), BF16, 'K_f')
    V_tm = pool.alloc_list(16, (128, D), BF16, 'V_tm')
    for m in range(8):
        for n in range(T // NB):
            ps = cx.psum[m % 4]
            pairs = [(wk[kc][:, m * 128:(m + 1) * 128], xn_hi[kc][:, n * NB:(n + 1) * NB],
                      [wk[kc], (xn_hi[kc], None)]) for kc in range(8)]
            extra = [(bk[:, m * 128:(m + 1) * 128], ones[:, :NB], [bk, ones])]
            mm_group(cx, ps[:, :], pairs, extra=extra, key=ps)
            p.op('scalar', lambda e, m=m, n=n, ps=ps: e.activation(
                out=K_f[m][:, n * NB:(n + 1) * NB], in_=ps[:, :], func=AF.Copy),
                reads=[ps], writes=[K_f[m]])
    for ti in range(16):
        for nb2 in range(2):
            ps = cx.psum[4 + (nb2 % 2)]
            pairs = [(xn_hi[kc][:, ti * 128:(ti + 1) * 128], wv[kc][:, nb2 * NB:(nb2 + 1) * NB],
                      [(xn_hi[kc], None), wv[kc]]) for kc in range(8)]
            extra = [(ones[:, 0:128], bv[:, nb2 * NB:(nb2 + 1) * NB], [bv, ones])]
            mm_group(cx, ps[:, :], pairs, extra=extra, key=ps)
            p.op('scalar', lambda e, ti=ti, nb2=nb2, ps=ps: e.activation(
                out=V_tm[ti][:, nb2 * NB:(nb2 + 1) * NB], in_=ps[:, :], func=AF.Copy),
                reads=[ps], writes=[V_tm[ti]])
    p.barrier()
    pool.free_listed('wk', 8)
    pool.free_listed('wv', 8)
    pool.new_epoch()
    import os as _os3
    _qlim = int(_os3.environ.get('E_QLIM', '16'))
    wq = load_weight_fm(cx, cx.dram['WqT'], 8, D, 'wq')
    QT = pool.alloc_list(8, (128, 128), BF16, 'QT')
    Srow = pool.alloc((128, T), F32, 'Srow')
    Prow = pool.alloc((128, T), BF16, 'Prow')
    PT = pool.alloc_list(16, (128, 128), BF16, 'PT')
    mrow = pool.alloc((128, 4), F32, 'mrow')
    for qi in range(_qlim):
        kend = (qi + 1) * 128
        for m in range(8):
            ps = cx.psum[m % 2]
            pairs = [(wq[kc][:, m * 128:(m + 1) * 128], xn_hi[kc][:, qi * 128:(qi + 1) * 128],
                      [wq[kc], (xn_hi[kc], None)]) for kc in range(8)]
            extra = [(bq[:, m * 128:(m + 1) * 128], ones[:, 0:128], [bq, ones])]
            mm_group(cx, ps[:, 0:128], pairs, extra=extra, key=ps)
            p.op('scalar', lambda e, m=m, ps=ps: e.activation(out=QT[m], in_=ps[:, 0:128], func=AF.Copy),
                 reads=[ps], writes=[QT[m]])
        nblocks = (kend + NB - 1) // NB
        for nb in range(nblocks):
            nw = min(NB, kend - nb * NB)
            ps = cx.psum[2 + nb % 2]
            pairs = [(QT[kc], K_f[kc][:, nb * NB: nb * NB + nw], [QT[kc], K_f[kc]])
                     for kc in range(8)]
            mm_group(cx, ps[:, :nw], pairs, key=ps)
            p.op('vector', lambda e, nb=nb, nw=nw, ps=ps: e.tensor_copy(Srow[:, nb * NB: nb * NB + nw], ps[:, :nw]),
                 reads=[ps], writes=[Srow])
        p.op('vector', lambda e, qi=qi, kend=kend: e.tensor_tensor(
            Srow[:, qi * 128:kend], Srow[:, qi * 128:kend], cx.consts['trineg'], ALU.add),
            reads=[Srow, cx.consts['trineg']], writes=[Srow])
        p.op('vector', lambda e, kend=kend: e.tensor_reduce(out=mrow[:, 0:1], in_=Srow[:, :kend], axis=AX.X, op=ALU.max),
             reads=[Srow], writes=[(mrow, 'c0')])
        p.op('vector', lambda e: e.tensor_scalar_mul(mrow[:, 1:2], mrow[:, 0:1], -1.0 / 32.0),
             reads=[(mrow, 'c0')], writes=[(mrow, 'c1')])
        p.op('scalar', lambda e, kend=kend: e.activation(out=Srow[:, :kend], in_=Srow[:, :kend], func=AF.Exp,
                                                         scale=1.0 / 32.0, bias=mrow[:, 1:2], accum_out=mrow[:, 2:3]),
             reads=[Srow, (mrow, 'c1')], writes=[Srow, (mrow, 'c2')])
        p.op('vector', lambda e: e.reciprocal(mrow[:, 3:4], mrow[:, 2:3]), reads=[(mrow, 'c2')], writes=[(mrow, 'c3')])
        p.op('vector', lambda e, kend=kend: e.tensor_copy(Prow[:, :kend], Srow[:, :kend]),
             reads=[Srow], writes=[Prow])
        for kc in range(qi + 1):
            pb = cx.psum_bf[kc % 2]
            p.op('tensor', lambda e, kc=kc, pb=pb: e.transpose(pb[:, :], Prow[:, kc * 128:(kc + 1) * 128],
                                                               cx.consts['ident_bf']),
                 reads=[Prow], writes=[pb])
            p.op('scalar', lambda e, kc=kc, pb=pb: e.activation(out=PT[kc], in_=pb[:, :], func=AF.Copy),
                 reads=[pb], writes=[PT[kc]])
        for nb2 in range(2):
            ps = cx.psum[4 + nb2]
            pairs = [(PT[kc], V_tm[kc][:, nb2 * NB:(nb2 + 1) * NB], [PT[kc], V_tm[kc]])
                     for kc in range(qi + 1)]
            mm_group(cx, ps[:, :], pairs, key=ps)
            p.op('scalar', lambda e, qi=qi, nb2=nb2, ps=ps: e.activation(
                out=s1[qi][:, nb2 * NB:(nb2 + 1) * NB], in_=ps[:, :], func=AF.Copy, scale=mrow[:, 3:4]),
                reads=[ps, (mrow, 'c3')], writes=[s1[qi]])
    pool.free_listed('wq', 8)
    pool.free_listed('K_f', 8)
    pool.free_listed('V_tm', 16)
    pool.free_listed('QT', 8)
    pool.free_listed('PT', 16)
    for nm in ('bq', 'bk', 'bv', 'Srow', 'Prow', 'mrow'):
        pool.free(nm)


def emit_l1ffn(cx, xn_hi, cw1_8, y1):
    p, pool = cx.p, cx.pool
    d = cx.dram
    for md in range(8):
        p.op('vector', lambda e, md=md: e.memset(y1[md], 0.0), writes=[y1[md]])
    b2t = pool.alloc((8, D), BF16, 'l1b2t')
    p.dma('gpsimd', lambda e: e.dma_start(out=b2t, in_=d['l1b2']), writes=[b2t])
    for md in range(8):
        for n in range(T // NB):
            ps = cx.psum[5]
            mm_group(cx, ps[:, :], [(b2t[:, md * 128:(md + 1) * 128], cw1_8[:, n * NB:(n + 1) * NB],
                                     [b2t, (cw1_8, None)])], key=ps)
            p.op('vector', lambda e, md=md, n=n, ps=ps: e.tensor_tensor(
                y1[md][:, n * NB:(n + 1) * NB], y1[md][:, n * NB:(n + 1) * NB], ps[:, :], ALU.add),
                reads=[ps, y1[md]], writes=[y1[md]])
    cwb = pool.alloc((128, T), BF16, 'l1cwb')
    ab8p = [pool.alloc_list(8, (128, NB), BF16, f'l1ab8_{pr}') for pr in range(2)]
    b1c = pool.alloc((128, 32), F32, 'l1b1c')
    wpar = []
    for par in range(2):
        wpar.append({'w1': pool.alloc_list(16, (128, 512), BF16, f'l1w1_{par}'),
                     'w2': pool.alloc_list(8, (128, 1024), BF16, f'l1w2_{par}')})

    def load_pair_w1(par, ei, qq):
        q0 = 2 * qq
        for sub in range(2):
            for kc in range(8):
                t = wpar[par]['w1'][sub * 8 + kc]
                p.dma('gpsimd', lambda e, t=t, ei=ei, q=q0 + sub, kc=kc: e.dma_start(
                    out=t, in_=d['l1w1'][ei][kc * 128:(kc + 1) * 128, q * 512:(q + 1) * 512]), writes=[t])

    def load_pair_w2(par, ei, qq):
        q0 = 2 * qq
        for sub in range(2):
            for kc in range(4):
                t = wpar[par]['w2'][sub * 4 + kc]
                p.dma('gpsimd', lambda e, t=t, ei=ei, q=q0 + sub, kc=kc: e.dma_start(
                    out=t, in_=d['l1w2'][ei][q * 512 + kc * 128: q * 512 + (kc + 1) * 128, :]), writes=[t])

    def load_pair(par, ei, qq):
        load_pair_w1(par, ei, qq)
        load_pair_w2(par, ei, qq)

    steps = [(ei, qq) for ei in range(E) for qq in range(4)]
    load_pair(0, 0, 0)
    abi = 0
    pending = None

    def emit_w2(w2bufs, abp, n):
        for md in range(8):
            ps = cx.psum[4 + md % 2]
            pairs = [(w2bufs[kc][:, md * 128:(md + 1) * 128], ab8p[abp][kc],
                      [w2bufs[kc], ab8p[abp][kc]]) for kc in range(8)]
            mm_group(cx, ps[:, :], pairs, key=ps)
            p.op('vector', lambda e, md=md, n=n, ps=ps: e.tensor_tensor(
                y1[md][:, n * NB:(n + 1) * NB], y1[md][:, n * NB:(n + 1) * NB], ps[:, :], ALU.add),
                reads=[ps, y1[md]], writes=[y1[md]])

    for idx, (ei, qq) in enumerate(steps):
        par = idx % 2
        defer_load = None
        if idx + 1 < len(steps):
            ei2, qq2 = steps[idx + 1]
            load_pair_w1(1 - par, ei2, qq2)
            defer_load = (1 - par, ei2, qq2)
        if qq == 0:
            for n in range(T // NB):
                ps = cx.psum[5]
                mm_group(cx, ps[:, :], [(cx.consts['sel8_bf'][:, ei * 128:(ei + 1) * 128],
                                         cw1_8[:, n * NB:(n + 1) * NB],
                                         [cx.consts['sel8_bf'], (cw1_8, None)])], key=ps)
                p.op('scalar', lambda e, n=n, ps=ps: e.activation(out=cwb[:, n * NB:(n + 1) * NB],
                                                                  in_=ps[:, :], func=AF.Copy),
                     reads=[ps], writes=[(cwb, f'n{n}')])
            p.dma('sync', lambda e, ei=ei: e.dma_start(out=b1c, in_=d['l1b1_pp'][ei]), writes=[b1c])
        W = wpar[par]
        for n in range(T // NB):
            for mfa in range(8):
                sub, mf = mfa // 4, mfa % 4
                fa = (2 * qq + sub) * 4 + mf
                ps = cx.psum[mfa % 4]
                pairs = [(W['w1'][sub * 8 + kc][:, mf * 128:(mf + 1) * 128], xn_hi[kc][:, n * NB:(n + 1) * NB],
                          [W['w1'][sub * 8 + kc], (xn_hi[kc], None)]) for kc in range(8)]
                mm_group(cx, ps[:, :], pairs, key=ps)
                p.op('scalar', lambda e, mfa=mfa, n=n, fa=fa, ps=ps, ab=ab8p[abi % 2]: e.activation(
                    out=ab[mfa], in_=ps[:, :], func=AF.Gelu, bias=b1c[:, fa:fa + 1]),
                    reads=[ps, b1c], writes=[ab8p[abi % 2][mfa]])
                p.op('vector', lambda e, mfa=mfa, n=n, ab=ab8p[abi % 2]: e.tensor_tensor(
                    ab[mfa], ab[mfa], cwb[:, n * NB:(n + 1) * NB], ALU.mult),
                    reads=[ab8p[abi % 2][mfa], (cwb, f'n{n}')], writes=[ab8p[abi % 2][mfa]])
            if pending is not None:
                emit_w2(*pending)
            if defer_load is not None:
                load_pair_w2(*defer_load)
                defer_load = None
            pending = (W['w2'], abi % 2, n)
            abi += 1
    if pending is not None:
        emit_w2(*pending)
    for nm in ('l1b2t', 'l1cwb', 'l1b1c'):
        pool.free(nm)
    for pr in range(2):
        pool.free_listed(f'l1ab8_{pr}', 8)
    for par in range(2):
        pool.free_listed(f'l1w1_{par}', 16)
        pool.free_listed(f'l1w2_{par}', 8)


def emit_final(cx, y1, s1_scr, x1_scr, out):
    p, pool = cx.p, cx.pool
    g1 = pool.alloc_list(8, (128, T), BF16, 'g1')
    for md in range(8):
        p.op('scalar', lambda e, md=md: e.activation(out=g1[md], in_=y1[md], func=AF.Silu),
             reads=[y1[md]], writes=[g1[md]])
    xt = pool.alloc((128, D), F32, 'f_xt')
    gt = pool.alloc((128, D), BF16, 'f_gt')
    ot = pool.alloc((128, D), F32, 'f_ot')
    s1t = pool.alloc((128, D), BF16, 'f_s1t')
    for ti in range(16):
        p.dma('sync', lambda e, ti=ti: e.dma_start(out=s1t, in_=s1_scr[ti * 128:(ti + 1) * 128, :]),
              reads=[(s1_scr, f't{ti}')], writes=[s1t])
        for md in range(8):
            pb = cx.psum_bf[md % 2]
            p.op('tensor', lambda e, md=md, ti=ti, pb=pb: e.transpose(
                pb[:, :], g1[md][:, ti * 128:(ti + 1) * 128], cx.consts['ident_bf']),
                reads=[g1[md]], writes=[pb])
            p.op('scalar', lambda e, md=md, pb=pb: e.activation(out=gt[:, md * 128:(md + 1) * 128], in_=pb[:, :], func=AF.Copy),
                 reads=[pb], writes=[(gt, f'm{md}')])
        p.dma('sync', lambda e, ti=ti: e.dma_start(out=xt, in_=x1_scr[ti * 128:(ti + 1) * 128, :]),
              reads=[(x1_scr, f't{ti}')], writes=[xt])
        greads = [(gt, f'm{m}') for m in range(8)]
        p.op('vector', lambda e: e.tensor_tensor(ot, gt, s1t, ALU.mult),
             reads=greads + [s1t], writes=[(ot, 'o')])
        p.op('vector', lambda e: e.tensor_tensor(ot, ot, xt, ALU.add), reads=[(ot, 'o'), xt], writes=[(ot, 'o')])
        p.dma('sync', lambda e, ti=ti: e.dma_start(out=out[ti * 128:(ti + 1) * 128, :], in_=ot),
              reads=[(ot, 'o')], writes=[(out, f't{ti}')])
    pool.free_listed('g1', 8)
    pool.free('f_xt'); pool.free('f_gt'); pool.free('f_ot'); pool.free('f_s1t')


def kernel(**inputs):
    g = host_prep(inputs)
    x = np.ascontiguousarray(np.asarray(inputs['L0_attn_x'], dtype=np.float32))
    cfg = {'g_shapes': {k: v.shape for k, v in g.items()}, 'stop_after': None, 'dbg': []}
    nc, _dbg = build(cfg)
    common = {'g_' + k: v for k, v in g.items()}
    in_maps = []
    for b in range(8):
        m = dict(common)
        m['x'] = np.ascontiguousarray(x[b])
        in_maps.append(m)
    res = run_bass_kernel_spmd(nc, in_maps, core_ids=list(range(8)))
    out = np.stack([res.results[b]['out'] for b in range(8)], axis=0)
    return out.astype(np.float32)
